# version 23
# baseline (speedup 1.0000x reference)
import numpy as np
import ml_dtypes
from contextlib import ExitStack

import concourse.bass as bass
import concourse.mybir as mybir
from concourse.bass_utils import run_bass_kernel_spmd

F32 = mybir.dt.float32
BF16 = mybir.dt.bfloat16
FP8 = mybir.dt.float8e4
ALU = mybir.AluOpType
ACTF = mybir.ActivationFunctionType
AX = mybir.AxisListType.X

NCORES = 8
NB = 2
S = 4096
D = 1024
ST = 128
NSUB = S // ST
GT = 256
SPG = GT // ST
NGRP = S // GT
NCH = 8
EPS_LN = 1e-5
EPS_AT = 1e-6

ENGINES = ("sync", "vector", "scalar", "tensor", "gpsimd")
DR = mybir.MatmulPerfMode.DoubleRow


class P:

    AUTODRAIN = ("vector", "scalar")

    def __init__(self, val, sems):
        self.stream = {e: [] for e in ENGINES}
        self.val = val
        self.sems = sems
        self._last_drained = {e: True for e in ENGINES}

    def raw(self, eng, fn):
        self.stream[eng].append(fn)

    def wait(self, eng, tok):
        if tok and tok[1] > 0:
            h = self.sems[tok[0]]
            self.raw(eng, lambda e, s=h, v=tok[1]: e.wait_ge(s, v))

    def do(self, eng, fn, inc=None, n=1):
        if eng in self.AUTODRAIN:
            if not self._last_drained[eng]:
                self.raw(eng, lambda e: e.drain())
            self._last_drained[eng] = False
        if inc is None:
            self.raw(eng, fn)
            return None
        h = self.sems[inc]
        self.val[inc] = self.val.get(inc, 0) + n
        v = self.val[inc]
        self.raw(eng, lambda e, f=fn, hh=h, nn=n: f(e).then_inc(hh, nn))
        return (inc, v)

    def drain(self, eng):
        if not self._last_drained.get(eng, False):
            self.raw(eng, lambda e: e.drain())
            self._last_drained[eng] = True

    def cur(self, name):
        return (name, self.val.get(name, 0))


def emit_block(nc, p):
    with nc.Block() as blk:
        for eng in ENGINES:
            fns = p.stream[eng]
            if fns:
                getattr(blk, eng)(lambda e, fns=fns: [f(e) for f in fns])


def chunk3(ap_tile, sub, stride=ST):
    nsub = stride // ST
    v = ap_tile.rearrange("p (s c f) -> p s c f", s=nsub, c=NCH)
    return v[:, sub]


def chunk3cm(ap_tile, sub, nsub=SPG):
    v = ap_tile.rearrange("p (c s f) -> p s c f", c=NCH, s=nsub)
    return v[:, sub]


def ch_view(ap_tile, i, nsub=SPG):
    v = ap_tile.rearrange("p (s c f) -> p c s f", s=nsub, c=NCH)
    return v[:, i]


def build_core_program():
    nc = bass.Bass("TRN2", target_bir_lowering=False)

    d_x = nc.dram_tensor("x", [NB, S, D], F32, kind="ExternalInput").ap()
    d_wqT = nc.dram_tensor("wqT", [D, D], F32, kind="ExternalInput").ap()
    d_wkT = nc.dram_tensor("wkT", [D, D], F32, kind="ExternalInput").ap()
    d_wv = nc.dram_tensor("wv", [D, D], F32, kind="ExternalInput").ap()
    d_wo = nc.dram_tensor("wo", [D, D], F32, kind="ExternalInput").ap()
    d_rw = nc.dram_tensor("rwbd", [NCH, 128, 128], F32, kind="ExternalInput").ap()
    d_bq = nc.dram_tensor("bq_c", [128, NCH], F32, kind="ExternalInput").ap()
    d_bk = nc.dram_tensor("bk_c", [128, NCH], F32, kind="ExternalInput").ap()
    d_bvr = nc.dram_tensor("bv_row", [1, D], F32, kind="ExternalInput").ap()
    d_bo = nc.dram_tensor("bo_c", [128, NCH], F32, kind="ExternalInput").ap()
    d_fk = nc.dram_tensor("fk", [128, 64 * 128], BF16, kind="ExternalInput").ap()
    d_fb = nc.dram_tensor("fb_c", [128, NCH], BF16, kind="ExternalInput").ap()
    d_sin = nc.dram_tensor("s_in", [1, 1], F32, kind="ExternalInput").ap()
    d_sk = nc.dram_tensor("s_k", [1, 1], F32, kind="ExternalInput").ap()
    d_ones128 = nc.dram_tensor("c_ones128", [128, 1], BF16, kind="ExternalInput").ap()
    d_onesrow = nc.dram_tensor("c_onesrow", [1, 128], BF16, kind="ExternalInput").ap()
    d_masks = nc.dram_tensor("c_masks", [16, NCH * 128], BF16, kind="ExternalInput").ap()

    d_out = nc.dram_tensor("out", [NB, S, D], F32, kind="ExternalOutput").ap()
    d_swq_hi = nc.dram_tensor("swq_hi", [128, 64 * 128], BF16, kind="Internal").ap()
    d_swq_lo = nc.dram_tensor("swq_lo", [128, 64 * 128], BF16, kind="Internal").ap()
    d_swo_hi = nc.dram_tensor("swo_hi", [128, 64 * 128], BF16, kind="Internal").ap()
    d_swo_lo = nc.dram_tensor("swo_lo", [128, 64 * 128], BF16, kind="Internal").ap()
    d_sx1 = nc.dram_tensor("sx1", [NB, NSUB, 128, D], F32, kind="Internal").ap()
    d_sxlh = nc.dram_tensor("sxlh", [NB, NGRP, 128, NCH * GT], BF16, kind="Internal").ap()
    d_sxll = nc.dram_tensor("sxll", [NB, NGRP, 128, NCH * GT], BF16, kind="Internal").ap()

    es = ExitStack()
    sems = {n: es.enter_context(nc.semaphore(n))
            for n in ("WT", "WTS", "WTB", "WTW",
                      "LXA0", "LXA1", "LXB0", "LXB1", "LX10", "LX11",
                      "TPQ0", "TPQ1", "TPZ0", "TPZ1", "TPL0", "TPL1",
                      "TPO0", "TPO1", "TPOA0", "TPOA1", "TP30", "TP31", "TPZ20", "TPZ21",
                      "SVW", "SV10", "SV11", "SVX0", "SVX1", "SVO0", "SVO1",
                      "VE", "AE", "PE", "GE")}
    val = {}

    def sb(stack, name, shape, dtype, **kw):
        return stack.enter_context(nc.sbuf_tensor(name, shape, dtype, **kw))

    wa_hi = sb(es, "wa_hi", [128, 64 * 128], BF16)
    wa_lo = sb(es, "wa_lo", [128, 64 * 128], BF16)
    wb_hi = sb(es, "wb_hi", [128, 64 * 128], BF16)
    wb_lo = sb(es, "wb_lo", [128, 64 * 128], BF16)
    kq8 = sb(es, "kq8", [128, 64 * 128], FP8)
    bqp_c = sb(es, "bqp_c", [128, NCH], F32)
    bo_sb = sb(es, "bo_sb", [128, NCH], F32)
    fb32_c = sb(es, "fb32_c", [128, NCH], F32)
    bkb = sb(es, "bkb", [128, D], F32)
    bvb = sb(es, "bvb", [128, D], F32)
    epsln = sb(es, "epsln", [128, 1], F32)
    rsin = sb(es, "rsin", [128, 1], F32)
    rsk = sb(es, "rsk", [128, 1], F32)
    sprod = sb(es, "sprod", [128, 1], F32)
    ones128 = sb(es, "ones128", [128, 1], BF16)
    onesrow = sb(es, "onesrow", [1, 128], BF16)
    masks = sb(es, "masks", [16, NCH * 128], BF16)
    kv_hi = sb(es, "kv_hi", [128, NB * D], BF16)
    kv_lo = sb(es, "kv_lo", [128, NB * D], BF16)
    ksb_hi = sb(es, "ksb_hi", [128, NB * NCH * 16], BF16)
    ksb_lo = sb(es, "ksb_lo", [128, NB * NCH * 16], BF16)

    p = P(val, sems)
    pes = ExitStack()
    pst = sb(pes, "pst", [128, 64 * 128], F32)
    psh = sb(pes, "psh", [128, 64 * 128], BF16)
    psl = sb(pes, "psl", [128, 64 * 128], BF16)
    wout_hi = sb(pes, "wout_hi", [128, 64 * 128], BF16)
    wout_lo = sb(pes, "wout_lo", [128, 64 * 128], BF16)
    rwf = sb(pes, "rwf", [128, NCH * 128], F32)
    rw_hi = sb(pes, "rw_hi", [128, NCH * 128], BF16)
    rw_lo = sb(pes, "rw_lo", [128, NCH * 128], BF16)
    bstg = sb(pes, "bstg", [128, NCH], F32)
    bshi = sb(pes, "bshi", [128, NCH], BF16)
    bslo = sb(pes, "bslo", [128, NCH], BF16)
    rowf = sb(pes, "rowf", [1, D], F32)
    rowh = sb(pes, "rowh", [1, D], BF16)
    rowl = sb(pes, "rowl", [1, D], BF16)
    sc_f = sb(pes, "sc_f", [1, 1], F32)
    sc_h = sb(pes, "sc_h", [1, 1], BF16)
    sc_l = sb(pes, "sc_l", [1, 1], BF16)
    t128a = sb(pes, "t128a", [128, 1], F32)
    t128b = sb(pes, "t128b", [128, 1], F32)
    pp_w = pes.enter_context(nc.psum_tensor("pp_w", [128, 128], F32))
    pp_w2 = pes.enter_context(nc.psum_tensor("pp_w2", [128, 128], F32))
    pp_r = pes.enter_context(nc.psum_tensor("pp_r", [1, D], F32))
    pp_b = pes.enter_context(nc.psum_tensor("pp_b", [128, 512], F32))
    pp_c = pes.enter_context(nc.psum_tensor("pp_c", [128, NCH], F32))

    t_ld0 = None
    for dst, src in [
        (ones128[:], d_ones128), (onesrow[:], d_onesrow), (masks[:], d_masks),
        (rwf[:].rearrange("p (c f) -> p c f", c=NCH), d_rw.rearrange("c r f -> r c f")),
        (bo_sb[:], d_bo), (sc_f[:], d_sin),
    ]:
        t_ld0 = p.do("gpsimd", lambda e, d=dst, s=src: e.dma_start(out=d, in_=s), inc="WT", n=16)
    t_ldfb = p.do("gpsimd", lambda e: e.dma_start(out=bshi[:], in_=d_fb), inc="WT", n=16)

    p.wait("vector", t_ldfb)
    p.do("vector", lambda e: e.tensor_copy(rw_hi[:], rwf[:]))
    p.drain("vector")
    t_rw = p.do("vector", lambda e: e.tensor_sub(rw_lo[:], rwf[:], rw_hi[:]), inc="VE")
    p.wait("vector", t_ldfb)
    t_fb32 = p.do("vector", lambda e: e.tensor_copy(fb32_c[:], bshi[:]), inc="VE")

    def bcast_scalar(dst128, t_loaded):
        p.wait("vector", t_loaded)
        p.do("vector", lambda e: e.tensor_copy(sc_h[:], sc_f[:]))
        p.drain("vector")
        t = p.do("vector", lambda e: e.tensor_sub(sc_l[:], sc_f[:], sc_h[:]), inc="VE")
        p.wait("tensor", t)
        p.do("tensor", lambda e: e.matmul(pp_w[:, 0:1], onesrow[:], sc_h[:], start=True, stop=False))
        t = p.do("tensor", lambda e: e.matmul(pp_w[:, 0:1], onesrow[:], sc_l[:], start=False, stop=True), inc="PE")
        p.wait("vector", t)
        return p.do("vector", lambda e: e.tensor_copy(dst128[:], pp_w[:, 0:1]), inc="VE")

    t_sa = bcast_scalar(t128b, t_ldfb)
    p.drain("vector")
    p.do("vector", lambda e: e.tensor_copy(rsin[:], t128b[:]))
    p.drain("vector")
    p.wait("gpsimd", t_sa)
    t_ldk = p.do("gpsimd", lambda e: e.dma_start(out=sc_f[:], in_=d_sk), inc="WTS", n=16)
    t_sb2 = bcast_scalar(t128b, t_ldk)
    p.drain("vector")
    p.do("vector", lambda e: e.tensor_copy(rsk[:], t128b[:]))
    p.drain("vector")
    p.do("vector", lambda e: e.tensor_mul(sprod[:], rsin[:], rsk[:]))
    p.drain("vector")

    def recip_nr(dst_and_src):
        d = dst_and_src
        p.do("vector", lambda e: e.reciprocal(t128a[:], d[:]))
        p.drain("vector")
        p.do("vector", lambda e: e.tensor_mul(t128b[:], d[:], t128a[:]))
        p.drain("vector")
        p.do("vector", lambda e: e.tensor_scalar(t128b[:], t128b[:], 2.0, -1.0, ALU.subtract, ALU.mult))
        p.drain("vector")
        return p.do("vector", lambda e: e.tensor_mul(d[:], t128a[:], t128b[:]), inc="VE")

    recip_nr(rsin)
    t_rsk = recip_nr(rsk)

    p.do("vector", lambda e: e.memset(epsln[:], EPS_LN))
    p.do("vector", lambda e: e.memzero(kv_hi[:]))
    p.do("vector", lambda e: e.memzero(kv_lo[:]))
    p.do("vector", lambda e: e.memzero(ksb_hi[:]))
    t_zero = p.do("vector", lambda e: e.memzero(ksb_lo[:]), inc="VE")

    def load_split_w(dsrc, pre_chunked, t_pe_free, t_sv_free, out_hi=psh, out_lo=psl):
        p.wait("gpsimd", t_pe_free)
        p.wait("gpsimd", t_sv_free)
        if pre_chunked:
            t = p.do("gpsimd", lambda e: e.dma_start(
                out=pst[:].rearrange("p (i j f) -> p i j f", i=NCH, j=NCH),
                in_=dsrc.rearrange("(i p) (j f) -> p i j f", p=128, f=128)), inc="WTW", n=16)
        else:
            t = p.do("gpsimd", lambda e: e.dma_start(
                out=pst[:].rearrange("p (i e) -> p i e", i=NCH),
                in_=dsrc.rearrange("(i p) e -> p i e", p=128)), inc="WTW", n=16)
        p.wait("vector", t)
        p.wait("vector", t_pe_free)
        p.do("vector", lambda e, o=out_hi: e.tensor_copy(o[:], pst[:]))
        p.drain("vector")
        return p.do("vector", lambda e, o1=out_lo, o2=out_hi: e.tensor_sub(o1[:], pst[:], o2[:]), inc="VE")

    def fold_rw(t_split, dst_hi, dst_lo, t_dst_free):
        p.wait("tensor", t_split)
        p.wait("tensor", t_rw)
        p.wait("scalar", t_dst_free)
        p.wait("vector", t_dst_free)
        t_lo = {}
        t_mm = None
        for n, (ii, jj) in enumerate([(a, b) for a in range(NCH) for b in range(NCH)]):
            pw = pp_w if n % 2 == 0 else pp_w2
            sl_hi = psh[:, (jj * NCH + ii) * 128:(jj * NCH + ii + 1) * 128]
            sl_lo = psl[:, (jj * NCH + ii) * 128:(jj * NCH + ii + 1) * 128]
            r_hi = rw_hi[:, jj * 128:(jj + 1) * 128]
            r_lo = rw_lo[:, jj * 128:(jj + 1) * 128]
            p.wait("tensor", t_lo.get(n - 2))
            p.do("tensor", lambda e, a=sl_hi, b=r_hi, w=pw: e.matmul(w[:], a, b, start=True, stop=False))
            p.do("tensor", lambda e, a=sl_hi, b=r_lo, w=pw: e.matmul(w[:], a, b, start=False, stop=False))
            t_mm = p.do("tensor", lambda e, a=sl_lo, b=r_hi, w=pw: e.matmul(w[:], a, b, start=False, stop=True), inc="PE")
            o_hi = dst_hi[:, (ii * NCH + jj) * 128:(ii * NCH + jj + 1) * 128]
            o_lo = dst_lo[:, (ii * NCH + jj) * 128:(ii * NCH + jj + 1) * 128]
            p.wait("scalar", t_mm)
            t_hi = p.do("scalar", lambda e, o=o_hi, w=pw: e.activation(o, w[:], ACTF.Copy), inc="AE")
            p.wait("vector", t_hi)
            t_lo[n] = p.do("vector", lambda e, o=o_lo, h=o_hi, w=pw: e.tensor_sub(o, w[:], h), inc="VE")
        return t_lo[63], t_mm

    def bias_fold(t_bload, dst_c):
        p.wait("vector", t_bload)
        p.do("vector", lambda e: e.tensor_copy(bshi[:], bstg[:]))
        p.drain("vector")
        t_b = p.do("vector", lambda e: e.tensor_sub(bslo[:], bstg[:], bshi[:]), inc="VE")
        p.wait("tensor", t_b)
        t_mm = None
        for i in range(NCH):
            r_hi = rw_hi[:, i * 128:(i + 1) * 128]
            r_lo = rw_lo[:, i * 128:(i + 1) * 128]
            p.do("tensor", lambda e, a=r_hi, b=bshi[:, i:i + 1], c=i, s=(i == 0): e.matmul(
                pp_c[:, c:c + 1], a, b, start=s, stop=False))
            p.do("tensor", lambda e, a=r_hi, b=bslo[:, i:i + 1], c=i: e.matmul(
                pp_c[:, c:c + 1], a, b, start=False, stop=False))
            t_mm = p.do("tensor", lambda e, a=r_lo, b=bshi[:, i:i + 1], c=i, s=(i == NCH - 1): e.matmul(
                pp_c[:, c:c + 1], a, b, start=False, stop=s),
                inc=("PE" if i == NCH - 1 else None))
        p.wait("vector", t_mm)
        return p.do("vector", lambda e: e.tensor_copy(dst_c[:], pp_c[:]), inc="VE")

    def bcast_row(t_ready, dst):
        p.wait("tensor", t_ready)
        tt = None
        for h in range(2):
            p.wait("tensor", tt)
            p.do("tensor", lambda e, hh=h: e.matmul(
                pp_b[:], onesrow[:], rowh[0:1, hh * 512:hh * 512 + 512], start=True, stop=False))
            t = p.do("tensor", lambda e, hh=h: e.matmul(
                pp_b[:], onesrow[:], rowl[0:1, hh * 512:hh * 512 + 512], start=False, stop=True), inc="PE")
            p.wait("vector", t)
            tt = p.do("vector", lambda e, hh=h: e.tensor_copy(dst[:, hh * 512:hh * 512 + 512], pp_b[:]), inc="VE")
        return tt

    t_sp = load_split_w(d_wqT, True, None, None)
    t_wq_ev, t_wq_mm = fold_rw(t_sp, wout_hi, wout_lo, None)
    p.wait("gpsimd", t_wq_ev)
    p.do("gpsimd", lambda e: e.dma_start(out=d_swq_hi, in_=wout_hi[:]), inc="SVW", n=16)
    t_wq_st = p.do("gpsimd", lambda e: e.dma_start(out=d_swq_lo, in_=wout_lo[:]), inc="SVW", n=16)

    t_ldbq = p.do("gpsimd", lambda e: e.dma_start(out=bstg[:], in_=d_bq), inc="WTB", n=16)
    t_bqp = bias_fold(t_ldbq, bqp_c)

    t_sp2 = load_split_w(d_wo, True, t_wq_mm, None)
    p.wait("gpsimd", t_sp2)
    p.do("gpsimd", lambda e: e.dma_start(out=d_swo_hi, in_=psh[:]), inc="SVW", n=16)
    t_wo_st = p.do("gpsimd", lambda e: e.dma_start(out=d_swo_lo, in_=psl[:]), inc="SVW", n=16)

    t_sp3 = load_split_w(d_wkT, True, None, t_wo_st)
    t_wk_ev, t_wk_mm = fold_rw(t_sp3, wa_hi, wa_lo, None)

    p.wait("gpsimd", t_bqp)
    t_ldbk = p.do("gpsimd", lambda e: e.dma_start(out=bstg[:], in_=d_bk), inc="WTB", n=16)
    p.wait("vector", t_ldbk)
    p.do("vector", lambda e: e.tensor_copy(bshi[:], bstg[:]))
    p.drain("vector")
    t_bk = p.do("vector", lambda e: e.tensor_sub(bslo[:], bstg[:], bshi[:]), inc="VE")
    p.wait("tensor", t_bk)
    t_mm = None
    for i in range(NCH):
        r_hi = rw_hi[:, i * 128:(i + 1) * 128]
        r_lo = rw_lo[:, i * 128:(i + 1) * 128]
        slr = pp_r[0:1, i * 128:(i + 1) * 128]
        p.do("tensor", lambda e, a=bshi[:, i:i + 1], b=r_hi, s=slr, st=(i % 4 == 0): e.matmul(
            s, a, b, start=st, stop=False))
        p.do("tensor", lambda e, a=bshi[:, i:i + 1], b=r_lo, s=slr: e.matmul(s, a, b, start=False, stop=False))
        t_mm = p.do("tensor", lambda e, a=bslo[:, i:i + 1], b=r_hi, s=slr, st=(i % 4 == 3): e.matmul(
            s, a, b, start=False, stop=st),
                    inc=("PE" if i == NCH - 1 else None))
    p.wait("vector", t_mm)
    p.do("vector", lambda e: e.tensor_copy(rowf[:], pp_r[:]))
    p.drain("vector")
    p.do("vector", lambda e: e.tensor_copy(rowh[:], rowf[:]))
    p.drain("vector")
    t_row = p.do("vector", lambda e: e.tensor_sub(rowl[:], rowf[:], rowh[:]), inc="VE")
    t_bkb = bcast_row(t_row, bkb)

    p.wait("gpsimd", t_bkb)
    t_ldbv = p.do("gpsimd", lambda e: e.dma_start(out=rowf[:], in_=d_bvr), inc="WTB", n=16)
    p.wait("vector", t_ldbv)
    p.do("vector", lambda e: e.tensor_copy(rowh[:], rowf[:]))
    p.drain("vector")
    t_row2 = p.do("vector", lambda e: e.tensor_sub(rowl[:], rowf[:], rowh[:]), inc="VE")
    t_bvb = bcast_row(t_row2, bvb)

    t_wv = load_split_w(d_wv, False, t_wk_mm, None, out_hi=wb_hi, out_lo=wb_lo)

    emit_block(nc, p)
    pes.close()

    p = P(val, sems)
    kes = ExitStack()
    kbf = sb(kes, "kbf", [128, 64 * 128], BF16)
    p.wait("gpsimd", t_wv)
    p.wait("gpsimd", t_bvb)
    p.wait("gpsimd", t_wo_st)
    t_ldkq = p.do("gpsimd", lambda e: e.dma_start(
        out=kbf[:], in_=d_fk), inc="WTW", n=16)
    p.wait("scalar", t_ldkq)
    p.wait("scalar", t_rsk)
    t_kq = p.do("scalar", lambda e: e.activation(kq8[:], kbf[:], ACTF.Copy, scale=rsk[:]), inc="AE")
    emit_block(nc, p)
    kes.close()

    pro = dict(wk=t_wk_ev, wv=t_wv, kq=t_kq, bqp=t_bqp, bkb=t_bkb, bvb=t_bvb,
               zeroed=t_zero)

    p = P(val, sems)
    aes = ExitStack()
    x_tm = [sb(aes, f"x_tm{i}", [128, D], F32) for i in range(2)]
    sq = sb(aes, "sq", [128, D], BF16)
    stA = [dict(nsum=sb(aes, f"Ans{i}", [128, 2], F32), ssum=sb(aes, f"Ass{i}", [128, 2], F32),
                negm=sb(aes, f"Anm{i}", [128, 2], F32), t1=sb(aes, f"At1{i}", [128, 2], F32),
                t2=sb(aes, f"At2{i}", [128, 2], F32), rstd=sb(aes, f"Ars{i}", [128, 2], F32))
           for i in range(2)]
    stC = [dict(nsum=sb(aes, f"Cns{i}", [128, 2], F32), ssum=sb(aes, f"Css{i}", [128, 2], F32),
                negm=sb(aes, f"Cnm{i}", [128, 2], F32), t1=sb(aes, f"Ct1{i}", [128, 2], F32),
                t2=sb(aes, f"Ct2{i}", [128, 2], F32), rstd=sb(aes, f"Crs{i}", [128, 2], F32))
           for i in range(2)]
    xq_bf = [sb(aes, f"xq_bf{i}", [128, D], BF16) for i in range(2)]
    xq_fm = [sb(aes, f"xq_fm{i}", [128, NCH * GT], BF16, is_transpose_dest=True) for i in range(2)]
    xq8 = [sb(aes, f"xq8{i}", [128, NCH * GT], FP8) for i in range(2)]
    tz = [sb(aes, f"tz{i}", [128, GT], BF16) for i in range(2)]
    z_fm = [sb(aes, f"z_fm{i}", [128, NCH * GT], BF16) for i in range(2)]
    z_tm = [sb(aes, f"z_tm{i}", [128, D], BF16, is_transpose_dest=True) for i in range(2)]
    x1_tm = [sb(aes, f"x1_tm{i}", [128, D], F32) for i in range(2)]
    u_ln = [sb(aes, f"u_ln{i}", [128, D], F32) for i in range(2)]
    xlh_tm = [sb(aes, f"xlh_tm{i}", [128, D], BF16) for i in range(2)]
    xll_tm = [sb(aes, f"xll_tm{i}", [128, D], BF16) for i in range(2)]
    xlh_fm = [sb(aes, f"xlh_fm{i}", [128, NCH * GT], BF16, is_transpose_dest=True) for i in range(2)]
    xll_fm = [sb(aes, f"xll_fm{i}", [128, NCH * GT], BF16, is_transpose_dest=True) for i in range(2)]
    v_t = [sb(aes, f"v_t{i}", [128, D], F32) for i in range(2)]
    v_hi = [sb(aes, f"v_hi{i}", [128, D], BF16) for i in range(2)]
    v_lo = [sb(aes, f"v_lo{i}", [128, D], BF16) for i in range(2)]
    kp_t = [sb(aes, f"kp_t{i}", [128, D], F32) for i in range(2)]
    kp_hi = [sb(aes, f"kp_hi{i}", [128, D], BF16) for i in range(2)]
    kp_lo = [sb(aes, f"kp_lo{i}", [128, D], BF16) for i in range(2)]

    P_kv = aes.enter_context(nc.psum_tensor("P_kv", [128, D], F32))
    P_ks = aes.enter_context(nc.psum_tensor("P_ks", [128, NCH], F32))
    P_V = aes.enter_context(nc.psum_tensor("P_V", [128, D], F32))
    P_K = aes.enter_context(nc.psum_tensor("P_K", [128, D], F32))

    rel = {}

    def stats_pair(x_aps, st, scratch, t_st_free_v, t_st_free_a, t_readys):
        p.wait("vector", t_st_free_v)
        for i, x_ap in enumerate(x_aps):
            p.do("vector", lambda e, xx=x_ap, ii=i: e.reduce_sum(
                st["nsum"][:, ii:ii + 1], xx, axis=AX, negate=True))
        t_sq = None
        for i, x_ap in enumerate(x_aps):
            if i == 0:
                p.wait("scalar", t_st_free_a)
            p.wait("scalar", t_readys[i])
            p.drain("scalar")
            t_sq = p.do("scalar", lambda e, xx=x_ap, ii=i: e.activation(
                scratch[:], xx, ACTF.Square, accum_out=st["ssum"][:, ii:ii + 1]), inc="AE")
        n = len(x_aps)
        p.do("vector", lambda e: e.tensor_scalar_mul(st["negm"][:, 0:n], st["nsum"][:, 0:n], 1.0 / D))
        p.wait("vector", t_sq)
        p.do("vector", lambda e: e.tensor_scalar_mul(st["t1"][:, 0:n], st["ssum"][:, 0:n], 1.0 / D))
        p.do("vector", lambda e: e.tensor_mul(st["t2"][:, 0:n], st["negm"][:, 0:n], st["negm"][:, 0:n]))
        t_var = p.do("vector", lambda e: e.tensor_sub(
            st["nsum"][:, 0:n], st["t2"][:, 0:n], st["t1"][:, 0:n]), inc="VE")
        p.wait("scalar", t_var)
        t_s = p.do("scalar", lambda e: e.activation(st["t2"][:, 0:n], st["nsum"][:, 0:n],
                                                    ACTF.Sqrt, bias=epsln[:], scale=-1.0), inc="AE")
        p.wait("vector", t_s)
        p.do("vector", lambda e: e.reciprocal(st["t1"][:, 0:n], st["t2"][:, 0:n]))
        p.do("vector", lambda e: e.tensor_mul(st["rstd"][:, 0:n], st["t2"][:, 0:n], st["t1"][:, 0:n]))
        p.do("vector", lambda e: e.tensor_scalar(st["rstd"][:, 0:n], st["rstd"][:, 0:n], 2.0, -1.0,
                                                 ALU.subtract, ALU.mult))
        t_r = p.do("vector", lambda e: e.tensor_mul(st["rstd"][:, 0:n], st["t1"][:, 0:n],
                                                    st["rstd"][:, 0:n]), inc="VE")
        return t_r

    def proj_tm(xh_sl, xl_sl, w_h, w_l, reg_psum, i, first, last):
        t = None
        for half in range(2):
            regv = reg_psum[:, half * 512:half * 512 + 512]
            mv_h = w_h[:, i * D + half * 512:i * D + half * 512 + 512]
            mv_l = w_l[:, i * D + half * 512:i * D + half * 512 + 512]
            p.do("tensor", lambda e, a=xh_sl, m=mv_h, r=regv, s=first: e.matmul(r, a, m, start=s, stop=False))
            p.do("tensor", lambda e, a=xh_sl, m=mv_l, r=regv: e.matmul(r, a, m, start=False, stop=False))
            t = p.do("tensor", lambda e, a=xl_sl, m=mv_h, r=regv, s=last: e.matmul(r, a, m, start=False, stop=s),
                     inc=("PE" if (last and half == 1) else None))
        return t

    p.wait("gpsimd", pro["zeroed"])
    rel[("xld", 0)] = p.do("gpsimd", lambda e: e.dma_start(
        out=x_tm[0][:], in_=d_x[0, 0:128, :]), inc="LXA0", n=16)

    NGT = NB * NGRP

    def prefetch_x(tid):
        if tid + 1 >= NB * NSUB:
            return
        nb_, nk_ = divmod(tid + 1, NSUB)
        p.wait("gpsimd", rel.get(("xfree", tid - 1)))
        p.wait("gpsimd", rel.get(("stA_free_a", (tid - 1) // 2)))
        rel[("xld", tid + 1)] = p.do("gpsimd", lambda e, bb=nb_, kk=nk_: e.dma_start(
            out=x_tm[(kk % 2)][:], in_=d_x[bb, kk * 128:kk * 128 + 128, :]),
            inc=f"LXA{(tid + 1) % 2}", n=16)

    def emit_ln1_pair(G):
        b, g = divmod(G, NGRP)
        gi = g % 2
        grp = G
        tids = [b * NSUB + 2 * g, b * NSUB + 2 * g + 1]
        for tid in tids:
            p.wait("vector", rel[("xld", tid)])
        t_r = stats_pair([x_tm[0][:], x_tm[1][:]], stA[gi], sq,
                         rel.get(("stA_free", grp - 2)), rel.get(("stA_free_a", grp - 2)),
                         [rel[("xld", t)] for t in tids])
        for i, tid in enumerate(tids):
            ki = tid % 2
            k = tid % NSUB
            p.wait("vector", rel.get(("xq_tp", tid - 2)))
            t_xq = p.do("vector", lambda e, kk=ki, ii=i: e.tensor_scalar(
                xq_bf[kk][:], x_tm[kk][:], stA[gi]["negm"][:, ii:ii + 1],
                stA[gi]["rstd"][:, ii:ii + 1], ALU.add, ALU.mult), inc="VE")
            if i == 1:
                rel[("stA_free", grp)] = t_xq
            p.wait("sync", t_xq)
            p.wait("sync", rel.get(("q8_tile", tid - 2)))
            t_tp = p.do("sync", lambda e, kk=ki, gg=gi, ss=k % SPG: e.dma_start(
                out=chunk3(xq_fm[gg][:], ss, stride=GT),
                in_=chunk3(xq_bf[kk][:], 0), transpose=True), inc=f"TPQ{tid % 2}", n=16)
            rel[("xq_tp", tid)] = t_tp
            p.wait("scalar", t_tp)
            p.wait("scalar", rel.get(("ffn_mm", grp - 2)))
            t_q8 = p.do("scalar", lambda e, gg=gi, ss=k % SPG: e.activation(
                chunk3cm(xq8[gg][:], ss), chunk3(xq_fm[gg][:], ss, stride=GT),
                ACTF.Copy, scale=rsin[:]), inc="AE")
            if i == 1:
                rel[("stA_free_a", grp)] = t_q8
            rel[("q8_tile", tid)] = t_q8
            if i == 1:
                rel[("xq8_done", grp)] = t_q8

    def emit_ffn1(G):
        b, g = divmod(G, NGRP)
        gi = g % 2
        tid = b * NSUB + 2 * g + 1
        grp = G
        p.wait("tensor", rel[("xq8_done", grp)])
        if grp == 0:
            p.wait("tensor", pro["kq"])
        t_fmm = None
        t_zv = None
        for j in range(NCH):
            reg = (P_V if j % 2 == 0 else P_K)[:, 0:GT]
            if j == 0:
                p.wait("tensor", rel.get(("V_ev", tid - 3)))
            elif j == 1:
                p.wait("tensor", rel.get(("KP_ev", tid - 3)))
            else:
                p.wait("tensor", rel.get(("ffn_dq", grp, j - 2)))
            for i in range(NCH // 2):
                t_fmm = p.do("tensor", lambda e, ii=i, jj=j, gg=gi, r=reg: e.matmul(
                    r, kq8[:, (ii * NCH + jj) * 256:(ii * NCH + jj) * 256 + 256]
                        .rearrange("p (i m) -> p i m", i=2),
                    xq8[gg][:, ii * 512:(ii + 1) * 512].rearrange("p (i n) -> p i n", i=2),
                    start=(ii == 0), stop=(ii == NCH // 2 - 1), perf_mode=DR),
                    inc=("PE" if i == NCH // 2 - 1 else None))
            p.wait("scalar", t_fmm)
            p.wait("scalar", rel.get(("tz_free", grp, j - 2)))
            t_dq = p.do("scalar", lambda e, jj=j, r=reg: e.activation(
                tz[jj % 2][:], r, ACTF.Copy, scale=sprod[:]), inc="AE")
            rel[("ffn_dq", grp, j)] = t_dq
            p.wait("vector", t_dq)
            p.wait("vector", rel.get(("z_tp0", grp - 2)))
            p.wait("vector", rel.get(("z_tp1", grp - 2)))
            t_zv = p.do("vector", lambda e, jj=j, gg=gi: e.tensor_scalar(
                ch_view(z_fm[gg][:], jj), tz[jj % 2][:].rearrange("p (s f) -> p s f", s=SPG),
                fb32_c[:, jj:jj + 1], 0.5, ALU.add, ALU.mult), inc="VE")
            rel[("tz_free", grp, j)] = t_zv
        rel[("ffn_mm", grp)] = t_fmm
        rel[("ffn_zv", grp)] = t_zv

    def emit_tail_a(G):
        b, g = divmod(G, NGRP)
        gi = g % 2
        grp = G
        t_ffn = rel[("ffn_zv", grp)]
        t_x1s = []
        for k2 in (2 * g, 2 * g + 1):
            k2i = k2 % 2
            t2 = b * NSUB + k2
            ss = k2 % SPG
            p.wait("sync", t_ffn)
            p.wait("sync", rel.get(("x1_done", t2 - 2)))
            t_ztp = p.do("sync", lambda e, gg=gi, s2=ss, kk=k2i: e.dma_start(
                out=chunk3(z_tm[kk][:], 0), in_=chunk3(z_fm[gg][:], s2, stride=GT),
                transpose=True), inc=f"TPZ{k2 % 2}", n=16)
            rel[(f"z_tp{ss}", grp)] = t_ztp
            p.wait("vector", t_ztp)
            p.wait("vector", rel.get(("x1_st", t2 - 2)))
            p.wait("vector", rel.get(("stC_free_a", grp - 1)))
            t_x1 = p.do("vector", lambda e, kk=k2i: e.tensor_add(
                x1_tm[kk][:], x_tm[kk][:], z_tm[kk][:]), inc="VE")
            rel[("x1_done", t2)] = t_x1
            rel[("xfree", t2)] = t_x1
            t_x1s.append(t_x1)
            p.wait("gpsimd", t_x1)
            rel[("x1_st", t2)] = p.do("gpsimd", lambda e, bb=b, kk2=k2, kk=k2i: e.dma_start(
                out=d_sx1[bb, kk2], in_=x1_tm[kk][:]), inc=f"SV1{k2 % 2}", n=16)
        stats_pair([x1_tm[0][:], x1_tm[1][:]], stC[gi], sq,
                   rel.get(("stC_free", grp - 2)), rel.get(("stC_free_a", grp - 2)),
                   t_x1s)
        emit_tail_split(G, 0)

    def emit_tail_split(G, i):
        b, g = divmod(G, NGRP)
        gi = g % 2
        grp = G
        k2 = 2 * g + i
        k2i = k2 % 2
        t2 = b * NSUB + k2
        ss = k2 % SPG
        p.wait("vector", rel.get(("xl_hi_used_a", t2 - 2)))
        p.wait("vector", rel.get(("kp_split", t2 - 2)))
        t_u = p.do("vector", lambda e, kk=k2i, ii=i: e.tensor_scalar(
            u_ln[kk][:], x1_tm[kk][:], stC[gi]["negm"][:, ii:ii + 1],
            stC[gi]["rstd"][:, ii:ii + 1], ALU.add, ALU.mult), inc="VE")
        p.wait("vector", rel.get(("xl_tp", t2 - 2)))
        t_h = p.do("vector", lambda e, kk=k2i: e.tensor_copy(
            xlh_tm[kk][:], u_ln[kk][:]), inc="VE")
        rel[("xl_hi_used_a", t2)] = t_h
        t_l = p.do("vector", lambda e, kk=k2i: e.tensor_sub(
            xll_tm[kk][:], u_ln[kk][:], xlh_tm[kk][:]), inc="VE")
        if i == 1:
            rel[("stC_free", grp)] = t_l
            rel[("stC_free_a", grp)] = t_l
        p.wait("sync", t_l)
        p.wait("sync", rel.get(("xl_stored", grp - 2)))
        p.wait("sync", rel.get(("kmm", 2 * G - 3)))
        p.do("sync", lambda e, gg=gi, s2=ss, kk=k2i: e.dma_start(
            out=chunk3(xlh_fm[gg][:], s2, stride=GT),
            in_=chunk3(xlh_tm[kk][:], 0), transpose=True), inc=f"TPL{k2 % 2}", n=16)
        t_tpl = p.do("sync", lambda e, gg=gi, s2=ss, kk=k2i: e.dma_start(
            out=chunk3(xll_fm[gg][:], s2, stride=GT),
            in_=chunk3(xll_tm[kk][:], 0), transpose=True), inc=f"TPL{k2 % 2}", n=16)
        rel[("xl_tp", t2)] = t_tpl

    def emit_tail_b(G):
        b, g = divmod(G, NGRP)
        gi = g % 2
        grp = G
        emit_tail_split(G, 1)
        tid = b * NSUB + 2 * g + 1
        p.wait("gpsimd", rel[("xl_tp", tid - 1)])
        p.wait("gpsimd", rel[("xl_tp", tid)])
        p.do("gpsimd", lambda e, bb=b, gg2=g, gg=gi: e.dma_start(
            out=d_sxlh[bb, gg2], in_=xlh_fm[gg][:]), inc=f"SVX{gi}", n=16)
        rel[("xl_stored", grp)] = p.do("gpsimd", lambda e, bb=b, gg2=g, gg=gi: e.dma_start(
            out=d_sxll[bb, gg2], in_=xll_fm[gg][:]), inc=f"SVX{gi}", n=16)
        prefetch_x(2 * G + 1)

    def emit_vk_projs(G, which):
        b, g = divmod(G, NGRP)
        gi = g % 2
        grp = G
        for k2 in (2 * g + which,):
            k2i = k2 % 2
            t2 = b * NSUB + k2
            ss = k2 % SPG
            p.wait("tensor", rel[("xl_tp", t2)])
            if grp == 0:
                p.wait("tensor", pro["wv"])
                p.wait("tensor", pro["bvb"])
                p.wait("tensor", pro["bkb"])
                p.wait("tensor", pro["wk"])
            t_vmm = None
            p.wait("tensor", rel.get(("V_ev", t2 - 1)))
            if which == 1:
                p.wait("tensor", rel.get(("ffn_dq", grp + 1, NCH - 2)))
            xh = chunk3(xlh_fm[gi][:], ss, stride=GT)
            xl = chunk3(xll_fm[gi][:], ss, stride=GT)
            for i in range(NCH):
                t_vmm = proj_tm(xh[:, i, :], xl[:, i, :], wb_hi, wb_lo, P_V, i,
                                first=(i == 0), last=(i == NCH - 1)) or t_vmm
            p.wait("vector", t_vmm)
            p.wait("vector", rel.get(("v_split", t2 - 2)))
            p.wait("vector", rel.get(("kv_mm", t2 - 2)))
            t_vt = p.do("vector", lambda e, kk=k2i: e.tensor_add(
                v_t[kk][:], P_V[:], bvb[:]), inc="VE")
            rel[("V_ev", t2)] = t_vt
            p.wait("vector", rel.get(("kv_mm", t2 - 2)))
            p.do("vector", lambda e, kk=k2i: e.tensor_copy(v_hi[kk][:], v_t[kk][:]))
            t_vl = p.do("vector", lambda e, kk=k2i: e.tensor_sub(
                v_lo[kk][:], v_t[kk][:], v_hi[kk][:]), inc="VE")
            rel[("v_split", t2)] = t_vl

            p.wait("tensor", rel.get(("KP_ev", t2 - 1)))
            if which == 1:
                p.wait("tensor", rel.get(("ffn_dq", grp + 1, NCH - 1)))
            t_kmm = None
            for i in range(NCH):
                t_kmm = proj_tm(xh[:, i, :], xl[:, i, :], wa_hi, wa_lo, P_K, i,
                                first=(i == 0), last=(i == NCH - 1)) or t_kmm
            p.wait("vector", t_kmm)
            p.wait("vector", rel.get(("kp_split", t2 - 2)))
            p.wait("vector", rel.get(("kv_mm", t2 - 2)))
            t_kt = p.do("vector", lambda e, kk=k2i: e.tensor_add(
                kp_t[kk][:], P_K[:], bkb[:]), inc="VE")
            rel[("KP_ev", t2)] = t_kt
            p.wait("scalar", t_kt)
            t_kr = p.do("scalar", lambda e, kk=k2i: e.activation(
                u_ln[kk][:], kp_t[kk][:], ACTF.Relu), inc="AE")
            p.wait("vector", t_kr)
            t_ke = p.do("vector", lambda e, kk=k2i: e.tensor_scalar_add(
                kp_t[kk][:], u_ln[kk][:], EPS_AT), inc="VE")
            p.do("vector", lambda e, kk=k2i: e.tensor_copy(kp_hi[kk][:], kp_t[kk][:]))
            t_kl = p.do("vector", lambda e, kk=k2i: e.tensor_sub(
                kp_lo[kk][:], kp_t[kk][:], kp_hi[kk][:]), inc="VE")
            rel[("kp_split", t2)] = t_kl
            rel[("kmm", t2)] = t_kmm

    def emit_kv(G, which):
        b, g = divmod(G, NGRP)
        grp = G
        k2 = 2 * g + which
        k2i = k2 % 2
        t2 = b * NSUB + k2
        p.wait("tensor", rel[("kp_split", t2)])
        p.wait("tensor", rel[("v_split", t2)])
        if b == 1 and k2 == 0:
            p.wait("tensor", rel[("kv_done", 0)])
        for c in range(NCH):
            kvreg = P_kv[:, c * 128:(c + 1) * 128]
            ksreg = P_ks[:, c:c + 1]
            sth = kp_hi[k2i][:, c * 128:(c + 1) * 128]
            stl = kp_lo[k2i][:, c * 128:(c + 1) * 128]
            mvh = v_hi[k2i][:, c * 128:(c + 1) * 128]
            mvl = v_lo[k2i][:, c * 128:(c + 1) * 128]
            kv_start = (k2 == 0 and c % 4 == 0)
            kv_stop = (k2 == NSUB - 1 and c % 4 == 3)
            ks_start = (k2 == 0 and c == 0)
            ks_stop = (k2 == NSUB - 1 and c == NCH - 1)
            p.do("tensor", lambda e, a=sth, m=mvh, r=kvreg, s=kv_start: e.matmul(r, a, m, start=s, stop=False))
            p.do("tensor", lambda e, a=sth, m=mvl, r=kvreg: e.matmul(r, a, m, start=False, stop=False))
            p.do("tensor", lambda e, a=sth, r=ksreg, s=ks_start: e.matmul(r, a, ones128[:], start=s, stop=False))
            p.do("tensor", lambda e, a=stl, m=mvh, r=kvreg, s=kv_stop: e.matmul(r, a, m, start=False, stop=s))
            t_kv = p.do("tensor", lambda e, a=stl, r=ksreg, s=ks_stop: e.matmul(
                r, a, ones128[:], start=False, stop=s),
                inc=("PE" if c == NCH - 1 else None))
        rel[("kv_mm", t2)] = t_kv
        if k2 == NSUB - 1:
            t_kv_last = t_kv
            p.wait("scalar", t_kv_last)
            p.wait("vector", t_kv_last)
            t_ev = None
            for c in range(NCH):
                for (r0, r1, c0, c1) in [(0, 64, 0, 64), (64, 128, 64, 128)]:
                    srcp = P_kv[r0:r1, c * 128 + c0:c * 128 + c1]
                    dh = kv_hi[r0:r1, b * D + c * 128 + c0:b * D + c * 128 + c1]
                    dl = kv_lo[r0:r1, b * D + c * 128 + c0:b * D + c * 128 + c1]
                    t_a = p.do("scalar", lambda e, s=srcp, d=dh: e.activation(d, s, ACTF.Copy), inc="AE")
                    p.wait("vector", t_a)
                    t_ev = p.do("vector", lambda e, s=srcp, d=dl, h=dh: e.tensor_sub(d, s, h), inc="VE")
                base = (b * NCH + c) * 16
                for (r0, r1, off) in [(0, 64, 0), (64, 128, 1)]:
                    srcp = P_ks[r0:r1, c:c + 1]
                    dh = ksb_hi[r0:r1, base + 2 * c + off:base + 2 * c + off + 1]
                    dl = ksb_lo[r0:r1, base + 2 * c + off:base + 2 * c + off + 1]
                    t_a = p.do("scalar", lambda e, s=srcp, d=dh: e.activation(d, s, ACTF.Copy), inc="AE")
                    p.wait("vector", t_a)
                    t_ev = p.do("vector", lambda e, s=srcp, d=dl, h=dh: e.tensor_sub(d, s, h), inc="VE")
            rel[("kv_done", b)] = t_ev

    rel[("xld", 1)] = p.do("gpsimd", lambda e: e.dma_start(
        out=x_tm[1][:], in_=d_x[0, 128:256, :]), inc="LXA1", n=16)
    emit_ln1_pair(0)
    for G in range(NGT):
        if G >= 1:
            emit_vk_projs(G - 1, 0)
        emit_ffn1(G)
        emit_tail_a(G)
        if G >= 1:
            emit_vk_projs(G - 1, 1)
            emit_kv(G - 1, 0)
            emit_kv(G - 1, 1)
        emit_tail_b(G)
        if G + 1 < NGT:
            prefetch_x(2 * G + 2)
            emit_ln1_pair(G + 1)
    emit_vk_projs(NGT - 1, 0)
    emit_vk_projs(NGT - 1, 1)
    emit_kv(NGT - 1, 0)
    emit_kv(NGT - 1, 1)

    a_last = dict(kv=rel[("kv_done", 1)], pe=p.cur("PE"),
                  sv1a=p.cur("SV10"), sv1b=p.cur("SV11"),
                  svxa=p.cur("SVX0"), svxb=p.cur("SVX1"))
    emit_block(nc, p)
    aes.close()

    p = P(val, sems)
    bes = ExitStack()
    x1b = [sb(bes, f"x1b{i}", [128, D], F32) for i in range(2)]
    xlhb = [sb(bes, f"xlhb{i}", [128, NCH * GT], BF16) for i in range(2)]
    xllb = [sb(bes, f"xllb{i}", [128, NCH * GT], BF16) for i in range(2)]
    qp_hi = sb(bes, "qp_hi", [128, NCH * GT], BF16)
    qp_lo = sb(bes, "qp_lo", [128, NCH * GT], BF16)
    tq = [sb(bes, f"tq{i}", [128, GT], F32) for i in range(2)]
    uq = [sb(bes, f"uq{i}", [128, GT], F32) for i in range(2)]
    atmp = [sb(bes, f"atmp{i}", [128, GT], F32) for i in range(2)]
    dnm = sb(bes, "dnm", [16, GT], F32)
    rc0 = sb(bes, "rc0", [16, GT], F32)
    rc1 = sb(bes, "rc1", [16, GT], F32)
    r_hi = sb(bes, "r_hi", [16, GT], BF16)
    r_lo = sb(bes, "r_lo", [16, GT], BF16)
    rb_sb = sb(bes, "rb_sb", [128, 2 * GT], F32)
    at_hi = sb(bes, "at_hi", [128, NCH * GT], BF16)
    at_lo = sb(bes, "at_lo", [128, NCH * GT], BF16)
    o_hi = sb(bes, "o_hi", [128, NCH * GT], BF16)
    o_lo = sb(bes, "o_lo", [128, NCH * GT], BF16)
    oh_tm = [sb(bes, f"oh_tm{i}", [128, D], BF16, is_transpose_dest=True) for i in range(2)]
    ol_tm = [sb(bes, f"ol_tm{i}", [128, D], BF16, is_transpose_dest=True) for i in range(2)]
    x2_tm = [sb(bes, f"x2_tm{i}", [128, D], F32) for i in range(4)]
    sqb = sb(bes, "sqb", [128, D], BF16)
    stB = [dict(nsum=sb(bes, f"Bns{i}", [128, 2], F32), ssum=sb(bes, f"Bss{i}", [128, 2], F32),
                negm=sb(bes, f"Bnm{i}", [128, 2], F32), t1=sb(bes, f"Bt1{i}", [128, 2], F32),
                t2=sb(bes, f"Bt2{i}", [128, 2], F32), rstd=sb(bes, f"Brs{i}", [128, 2], F32))
           for i in range(2)]
    xl3_bf = [sb(bes, "xl3_bf0", [128, D], BF16)]
    xl3_fm = [sb(bes, "xl3_fm0", [128, NCH * GT], BF16, is_transpose_dest=True)]
    xq8b = [sb(bes, f"xq8b{i}", [128, NCH * GT], FP8) for i in range(2)]
    tzb = [sb(bes, f"tzb{i}", [128, GT], BF16) for i in range(2)]
    z2_fm = [sb(bes, "z2_fm0", [128, NCH * GT], BF16)]
    z2_tm = [sb(bes, f"z2_tm{i}", [128, D], BF16, is_transpose_dest=True) for i in range(2)]
    out_tm = [sb(bes, f"out_tm{i}", [128, D], F32) for i in range(2)]

    PBp = [bes.enter_context(nc.psum_tensor(f"PBp{i}", [128, GT], F32)) for i in range(6)]
    P_n = bes.enter_context(nc.psum_tensor("P_n", [16, GT], F32))

    pr = {}

    def stats_pair_b(x_aps, st, t_free_v, t_free_a, t_readys):
        p.wait("vector", t_free_v)
        for i, x_ap in enumerate(x_aps):
            p.do("vector", lambda e, xx=x_ap, ii=i: e.reduce_sum(
                st["nsum"][:, ii:ii + 1], xx, axis=AX, negate=True))
        t_sq = None
        for i, x_ap in enumerate(x_aps):
            if i == 0:
                p.wait("scalar", t_free_a)
            p.wait("scalar", t_readys[i])
            p.drain("scalar")
            t_sq = p.do("scalar", lambda e, xx=x_ap, ii=i: e.activation(
                sqb[:], xx, ACTF.Square, accum_out=st["ssum"][:, ii:ii + 1]), inc="AE")
        n = len(x_aps)
        p.do("vector", lambda e: e.tensor_scalar_mul(st["negm"][:, 0:n], st["nsum"][:, 0:n], 1.0 / D))
        p.wait("vector", t_sq)
        p.do("vector", lambda e: e.tensor_scalar_mul(st["t1"][:, 0:n], st["ssum"][:, 0:n], 1.0 / D))
        p.do("vector", lambda e: e.tensor_mul(st["t2"][:, 0:n], st["negm"][:, 0:n], st["negm"][:, 0:n]))
        t_var = p.do("vector", lambda e: e.tensor_sub(
            st["nsum"][:, 0:n], st["t2"][:, 0:n], st["t1"][:, 0:n]), inc="VE")
        p.wait("scalar", t_var)
        t_s = p.do("scalar", lambda e: e.activation(st["t2"][:, 0:n], st["nsum"][:, 0:n],
                                                    ACTF.Sqrt, bias=epsln[:], scale=-1.0), inc="AE")
        p.wait("vector", t_s)
        p.do("vector", lambda e: e.reciprocal(st["t1"][:, 0:n], st["t2"][:, 0:n]))
        p.do("vector", lambda e: e.tensor_mul(st["rstd"][:, 0:n], st["t2"][:, 0:n], st["t1"][:, 0:n]))
        p.do("vector", lambda e: e.tensor_scalar(st["rstd"][:, 0:n], st["rstd"][:, 0:n], 2.0, -1.0,
                                                 ALU.subtract, ALU.mult))
        p.do("vector", lambda e: e.tensor_mul(st["rstd"][:, 0:n], st["t1"][:, 0:n],
                                              st["rstd"][:, 0:n]), inc="VE")
        return t_sq

    p.wait("gpsimd", a_last["sv1a"])
    p.wait("gpsimd", a_last["sv1b"])
    p.wait("gpsimd", a_last["svxa"])
    p.wait("gpsimd", a_last["svxb"])
    p.wait("gpsimd", a_last["kv"])
    p.wait("gpsimd", a_last["pe"])
    t_wload = None
    for dst, src in [(wa_hi[:], d_swq_hi), (wa_lo[:], d_swq_lo),
                     (wb_hi[:], d_swo_hi), (wb_lo[:], d_swo_lo)]:
        t_wload = p.do("gpsimd", lambda e, d=dst, s=src: e.dma_start(out=d, in_=s), inc="WTW", n=16)

    NGT = NB * NGRP

    def emit_xl_load(G):
        if G >= NGT:
            return
        b, g = divmod(G, NGRP)
        gi = g % 2
        p.wait("gpsimd", pr.get(("xl_used", G - 2)))
        p.do("gpsimd", lambda e, bb=b, gg=g, ii=gi: e.dma_start(
            out=xlhb[ii][:], in_=d_sxlh[bb, gg]), inc=f"LXB{gi}", n=16)
        pr[("xlld", G)] = p.do("gpsimd", lambda e, bb=b, gg=g, ii=gi: e.dma_start(
            out=xllb[ii][:], in_=d_sxll[bb, gg]), inc=f"LXB{gi}", n=16)

    def emit_x1_load(G):
        if G >= NGT:
            return
        b, g = divmod(G, NGRP)
        p.wait("scalar", pr.get(("x1_used", G - 1)))
        for s2 in range(SPG):
            pr[("x1ld", G, s2)] = p.do("scalar", lambda e, bb=b, kk=g * SPG + s2, ii=s2: e.dma_start(
                out=x1b[ii][:], in_=d_sx1[bb, kk]), inc=f"LX1{s2}", n=16)

    def emit_qno(G, tail_G=None):
        b, g = divmod(G, NGRP)
        gi = g % 2
        grp = G
        p.wait("tensor", t_wload)
        p.wait("tensor", pr[("xlld", G)])
        if grp >= 2:
            p.wait("tensor", pr[("ffn2_dq", grp - 2, 4)])
        elif grp == 1:
            p.wait("tensor", pr[("o_hi", grp - 1, NCH - 1)])
        t_qmm = None
        for j in range(NCH):
            reg = PBp[j % 2][:]
            p.wait("tensor", pr.get(("qp_hi", grp, j - 2)))
            if j == 1 and grp >= 2:
                p.wait("tensor", pr[("ffn2_dq", grp - 2, 5)])
            for i in range(NCH):
                sh = wa_hi[:, (i * NCH + j) * 128:(i * NCH + j + 1) * 128]
                sl = wa_lo[:, (i * NCH + j) * 128:(i * NCH + j + 1) * 128]
                mh = ch_view(xlhb[gi][:], i)
                ml = ch_view(xllb[gi][:], i)
                p.do("tensor", lambda e, a=sh, m=mh, r=reg, ii=i: e.matmul(r, a, m, start=(ii == 0), stop=False))
                p.do("tensor", lambda e, a=sh, m=ml, r=reg: e.matmul(r, a, m, start=False, stop=False))
                t_qmm = p.do("tensor", lambda e, a=sl, m=mh, r=reg, ii=i: e.matmul(
                    r, a, m, start=False, stop=(ii == NCH - 1)),
                    inc=("PE" if i == NCH - 1 else None))
            p.wait("scalar", t_qmm)
            p.wait("scalar", pr.get(("tq_used", grp, j - 2)))
            if j < 2:
                p.wait("scalar", pr.get(("to_used", grp - 1, NCH - 2 + j)))
            p.wait("scalar", pr.get(("qkv_mm", grp - 1)))
            t_qr = p.do("scalar", lambda e, jj=j, r=reg: e.activation(
                tq[jj % 2][:], r, ACTF.Relu, bias=bqp_c[:, jj:jj + 1]), inc="AE")
            pr[("qp_hi", grp, j)] = t_qr
            p.wait("vector", t_qr)
            p.wait("vector", pr.get(("uq_used", grp, j - 2)))
            p.wait("vector", pr.get(("qkv_mm", grp - 1)))
            t_qe = p.do("vector", lambda e, jj=j: e.tensor_scalar_add(
                uq[jj % 2][:], tq[jj % 2][:], EPS_AT), inc="VE")
            pr[("tq_used", grp, j)] = t_qe
            p.do("vector", lambda e, jj=j: e.tensor_copy(
                qp_hi[:, jj * GT:(jj + 1) * GT], uq[jj % 2][:]))
            t_ql = p.do("vector", lambda e, jj=j: e.tensor_sub(
                qp_lo[:, jj * GT:(jj + 1) * GT], uq[jj % 2][:],
                qp_hi[:, jj * GT:(jj + 1) * GT]), inc="VE")
            pr[("uq_used", grp, j)] = t_ql
        t_qp = t_ql
        pr[("xl_used", grp)] = t_qmm
        if tail_G is not None:
            tailB2_quant(tail_G, 0)
            tailB2_quant(tail_G, 1)

        p.wait("tensor", t_qp)
        p.wait("tensor", pr.get(("dnm_read", grp - 1)))
        for c in range(NCH):
            base = (b * NCH + c) * 16
            sh = ksb_hi[:, base:base + 16]
            sl = ksb_lo[:, base:base + 16]
            mh = qp_hi[:, c * GT:(c + 1) * GT]
            ml = qp_lo[:, c * GT:(c + 1) * GT]
            p.do("tensor", lambda e, a=sh, m=mh, cc=c: e.matmul(P_n[:], a, m, start=(cc == 0), stop=False))
            p.do("tensor", lambda e, a=sh, m=ml: e.matmul(P_n[:], a, m, start=False, stop=False))
            t_nmm = p.do("tensor", lambda e, a=sl, m=mh, cc=c: e.matmul(
                P_n[:], a, m, start=False, stop=(cc == NCH - 1)),
                inc=("PE" if c == NCH - 1 else None))

        p.wait("vector", t_nmm)
        t_dn = p.do("vector", lambda e: e.tensor_scalar_add(dnm[:], P_n[:], EPS_AT), inc="VE")
        pr[("dnm_read", grp)] = t_dn
        p.do("vector", lambda e: e.reciprocal(rc0[:], dnm[:]))
        p.drain("vector")
        p.do("vector", lambda e: e.tensor_mul(rc1[:], dnm[:], rc0[:]))
        p.drain("vector")
        p.do("vector", lambda e: e.tensor_scalar(rc1[:], rc1[:], 2.0, -1.0, ALU.subtract, ALU.mult))
        p.drain("vector")
        t_rc = p.do("vector", lambda e: e.tensor_mul(rc0[:], rc0[:], rc1[:]), inc="VE")
        p.wait("scalar", t_rc)
        t_rh = p.do("scalar", lambda e: e.activation(r_hi[:], rc0[:], ACTF.Copy), inc="AE")
        p.wait("vector", t_rh)
        t_rl = p.do("vector", lambda e: e.tensor_sub(r_lo[:], rc0[:], r_hi[:]), inc="VE")
        p.wait("tensor", t_rl)
        for c in range(NCH):
            reg = PBp[4 + c % 2][:]
            p.wait("tensor", pr.get(("rb_ev", grp, c - 2)))
            if c == 4:
                p.wait("tensor", pr[("at_hi", grp, 2)])
            elif c == 6:
                p.wait("tensor", pr[("at_hi", grp, 5)])
            if c == 0:
                p.wait("tensor", pr.get(("at_done", grp - 1, 5)))
                p.wait("tensor", pr.get(("ffn2_dq", grp - 2, 6)))
            elif c == 1:
                p.wait("tensor", pr.get(("rb_ev", grp - 1, NCH - 1)))
                p.wait("tensor", pr.get(("ffn2_dq", grp - 2, 7)))
            p.do("tensor", lambda e, cc=c, r=reg: e.matmul(
                r, masks[:, cc * 128:(cc + 1) * 128], r_hi[:], start=True, stop=False))
            t_bc = p.do("tensor", lambda e, cc=c, r=reg: e.matmul(
                r, masks[:, cc * 128:(cc + 1) * 128], r_lo[:], start=False, stop=True), inc="PE")
            p.wait("vector", t_bc)
            pr[("rb_ev", grp, c)] = p.do("vector", lambda e, cc=c % 2, r=reg: e.tensor_copy(
                rb_sb[:, cc * GT:(cc + 1) * GT], r), inc="VE")
            reg = PBp[2 + c % 3][:]
            p.wait("tensor", pr.get(("at_hi", grp, c - 3)))
            if c == 0:
                p.wait("tensor", pr.get(("at_done", grp - 1, 6)))
            elif c == 1:
                p.wait("tensor", pr.get(("at_done", grp - 1, 7)))
            elif c == 2:
                p.wait("tensor", pr[("rb_ev", grp, 2)])
            elif c == 5:
                p.wait("tensor", pr[("rb_ev", grp, 4)])
            sh = kv_hi[:, b * D + c * 128:b * D + (c + 1) * 128]
            sl = kv_lo[:, b * D + c * 128:b * D + (c + 1) * 128]
            mh = qp_hi[:, c * GT:(c + 1) * GT]
            ml = qp_lo[:, c * GT:(c + 1) * GT]
            p.do("tensor", lambda e, a=sh, m=mh, r=reg: e.matmul(r, a, m, start=True, stop=False))
            p.do("tensor", lambda e, a=sh, m=ml, r=reg: e.matmul(r, a, m, start=False, stop=False))
            t_omm = p.do("tensor", lambda e, a=sl, m=mh, r=reg: e.matmul(r, a, m, start=False, stop=True), inc="PE")
            if c == NCH - 1:
                pr[("qkv_mm", grp)] = t_omm
            p.wait("vector", t_omm)
            p.wait("vector", pr[("rb_ev", grp, c)])
            p.wait("vector", pr.get(("at_split", grp, c - 2)))
            p.wait("vector", pr.get(("o_mm", grp - 1)))
            t_at = p.do("vector", lambda e, cc=c, r=reg: e.tensor_mul(
                atmp[cc % 2][:], r, rb_sb[:, (cc % 2) * GT:(cc % 2 + 1) * GT]), inc="VE")
            pr[("at_hi", grp, c)] = t_at
            p.do("vector", lambda e, cc=c: e.tensor_copy(
                at_hi[:, cc * GT:(cc + 1) * GT], atmp[cc % 2][:]))
            t_al = p.do("vector", lambda e, cc=c: e.tensor_sub(
                at_lo[:, cc * GT:(cc + 1) * GT], atmp[cc % 2][:],
                at_hi[:, cc * GT:(cc + 1) * GT]), inc="VE")
            pr[("at_split", grp, c)] = t_al
            pr[("at_done", grp, c)] = t_al
        t_attn = t_al

        p.wait("tensor", t_attn)
        t_omm2 = None
        for j in range(NCH):
            reg = PBp[j % 2][:]
            p.wait("tensor", pr.get(("o_hi", grp, j - 2)))
            if j < 2:
                p.wait("tensor", pr[("qp_hi", grp, NCH - 2 + j)])
            for i in range(NCH):
                sh = wb_hi[:, (i * NCH + j) * 128:(i * NCH + j + 1) * 128]
                sl = wb_lo[:, (i * NCH + j) * 128:(i * NCH + j + 1) * 128]
                mh = at_hi[:, i * GT:(i + 1) * GT]
                ml = at_lo[:, i * GT:(i + 1) * GT]
                p.do("tensor", lambda e, a=sh, m=mh, r=reg, ii=i: e.matmul(r, a, m, start=(ii == 0), stop=False))
                p.do("tensor", lambda e, a=sh, m=ml, r=reg: e.matmul(r, a, m, start=False, stop=False))
                t_omm2 = p.do("tensor", lambda e, a=sl, m=mh, r=reg, ii=i: e.matmul(
                    r, a, m, start=False, stop=(ii == NCH - 1)),
                    inc=("PE" if i == NCH - 1 else None))
            if j == NCH - 1:
                pr[("o_mm", grp)] = t_omm2
            p.wait("scalar", t_omm2)
            p.wait("scalar", pr.get(("to_used", grp, j - 2)))
            if j < 2:
                p.wait("scalar", pr[("tq_used", grp, NCH - 2 + j)])
            p.wait("scalar", pr.get(("o_tp0", grp - 1)))
            p.wait("scalar", pr.get(("o_tp1", grp - 1)))
            t_to = p.do("scalar", lambda e, jj=j, r=reg: e.activation(
                tq[jj % 2][:], r, ACTF.Identity, bias=bo_sb[:, jj:jj + 1]), inc="AE")
            pr[("o_hi", grp, j)] = t_to
            p.wait("vector", t_to)
            p.wait("vector", pr.get(("o_tp0", grp - 1)))
            p.wait("vector", pr.get(("o_tp1", grp - 1)))
            p.do("vector", lambda e, jj=j: e.tensor_copy(
                ch_view(o_hi[:], jj), tq[jj % 2][:].rearrange("p (s f) -> p s f", s=SPG)))
            t_ol = p.do("vector", lambda e, jj=j: e.tensor_sub(
                ch_view(o_lo[:], jj), tq[jj % 2][:].rearrange("p (s f) -> p s f", s=SPG),
                ch_view(o_hi[:], jj)), inc="VE")
            pr[("to_used", grp, j)] = t_ol
            if j == 3:
                emit_otp_half(grp, 0)
        pr[("o_ev", grp)] = t_ol

    def emit_otp_half(G, half):
        b, g = divmod(G, NGRP)
        grp = G
        lo, hi = (0, NCH // 2) if half == 0 else (NCH // 2, NCH)
        t_oev = pr[("to_used", grp, 3)] if half == 0 else pr[("o_ev", grp)]
        t_otp = None
        for s2 in range(SPG):
            k2 = g * SPG + s2
            t2 = b * NSUB + k2
            p.wait("sync", t_oev)
            if half == 0:
                p.wait("sync", pr.get(("x2A", t2 - 2)))
            else:
                p.wait("sync", pr.get(("x2_done", t2 - 2)))
            sem = f"TPO{s2}" if half else f"TPOA{s2}"
            p.do("sync", lambda e, ss=s2, kk=s2, l=lo, h=hi: e.dma_start(
                out=chunk3(oh_tm[kk][:], 0)[:, l:h], in_=chunk3(o_hi[:], ss, stride=GT)[:, l:h],
                transpose=True), inc=sem, n=16)
            t_otp = p.do("sync", lambda e, ss=s2, kk=s2, l=lo, h=hi: e.dma_start(
                out=chunk3(ol_tm[kk][:], 0)[:, l:h], in_=chunk3(o_lo[:], ss, stride=GT)[:, l:h],
                transpose=True), inc=sem, n=16)
            pr[(("o_tpA", t2) if half == 0 else ("o_tpt", t2))] = t_otp

    def emit_tailB1(G):
        emit_otp_half(G, 1)

    tb2 = {}

    def tailB2_x2h(G, half):
        b, g = divmod(G, NGRP)
        grp = G
        x2s = [x2_tm[s2 + 2 * (G % 2)] for s2 in range(SPG)]
        c0, c1 = (0, D // 2) if half == 0 else (D // 2, D)
        t_x2s = []
        for s2 in range(SPG):
            k2 = g * SPG + s2
            t2 = b * NSUB + k2
            if half == 0:
                p.wait("vector", pr[("o_tpA", t2)])
                p.wait("vector", pr[("x1ld", G, s2)])
                p.wait("vector", pr.get(("x3_done", t2 - 4)))
                p.wait("vector", pr.get(("stB_free_a", grp - 1)))
            else:
                p.wait("vector", pr[("o_tpt", t2)])
            p.do("vector", lambda e, kk=s2, xv=x2s[s2]: e.tensor_add(
                xv[:, c0:c1], x1b[kk][:, c0:c1], oh_tm[kk][:, c0:c1]))
            p.drain("vector")
            t_x2 = p.do("vector", lambda e, kk=s2, xv=x2s[s2]: e.tensor_add(
                xv[:, c0:c1], xv[:, c0:c1], ol_tm[kk][:, c0:c1]), inc="VE")
            if half == 0:
                pr[("x2A", t2)] = t_x2
            else:
                pr[("x2_done", t2)] = t_x2
                t_x2s.append(t_x2)
                if s2 == SPG - 1:
                    pr[("x1_used", grp)] = t_x2
        if half == 1:
            tb2[G] = (x2s, t_x2s)

    def tailB2_stats(G):
        g = G % NGRP
        x2s, t_x2s = tb2[G]
        pr[("stB_free_a", G)] = stats_pair_b(
            [x2s[0][:], x2s[1][:]], stB[g % 2],
            pr.get(("stB_free", G - 2)), pr.get(("stB_free_a", G - 2)), t_x2s)

    def tailB2_apply(G, s2):
        b, g = divmod(G, NGRP)
        gi = g % 2
        x2s, _ = tb2[G]
        k2 = g * SPG + s2
        t2 = b * NSUB + k2
        p.wait("vector", pr.get(("xl3_tp", t2 - 1)))
        t_x3 = p.do("vector", lambda e, ii=s2, xv=x2s[s2]: e.tensor_scalar(
            xl3_bf[0][:], xv[:], stB[gi]["negm"][:, ii:ii + 1],
            stB[gi]["rstd"][:, ii:ii + 1], ALU.add, ALU.mult), inc="VE")
        if s2 == SPG - 1:
            pr[("stB_free", G)] = t_x3
        p.wait("sync", t_x3)
        p.wait("sync", pr.get(("xq8b_done", G - 1)))
        t_t3 = p.do("sync", lambda e, ss=s2: e.dma_start(
            out=chunk3(xl3_fm[0][:], ss, stride=GT),
            in_=chunk3(xl3_bf[0][:], 0), transpose=True), inc=f"TP3{s2}", n=16)
        pr[("xl3_tp", t2)] = t_t3

    def tailB2_quant(G, s2):
        b, g = divmod(G, NGRP)
        t2 = b * NSUB + g * SPG + s2
        p.wait("scalar", pr[("xl3_tp", t2)])
        p.wait("scalar", pr.get(("ffn2_mm", G - 2)))
        t_q3 = p.do("scalar", lambda e, ss=s2, gg=G % 2: e.activation(
            chunk3cm(xq8b[gg][:], ss), chunk3(xl3_fm[0][:], ss, stride=GT),
            ACTF.Copy, scale=rsin[:]), inc="AE")
        if s2 == SPG - 1:
            pr[("xq8b_done", G)] = t_q3

    def emit_tailB2(G):
        tailB2_x2h(G, 0)
        tailB2_x2h(G, 1)
        tailB2_stats(G)
        tailB2_apply(G, 0)
        tailB2_apply(G, 1)

    def emit_ffn2(G, tail_G=None):
        grp = G
        p.wait("tensor", pr[("xq8b_done", grp)])
        p.wait("tensor", pr[("o_hi", min(grp + 1, NGT - 1), NCH - 1)])
        p.wait("tensor", pr.get(("ffn2_dq_last", grp - 1)))
        t_zmm = None
        t_z2 = None
        if tail_G is not None:
            tailB2_x2h(tail_G, 0)
        for j in range(NCH):
            reg = PBp[j % 2][:] if j < 6 else PBp[j - 2][:]
            if j < 6:
                p.wait("tensor", pr.get(("ffn2_dq", grp, j - 2)))
            else:
                p.wait("tensor", pr.get(("rb_ev", grp + 1, j)))
            for i in range(NCH // 2):
                t_zmm = p.do("tensor", lambda e, ii=i, jj=j, gg=G % 2, r=reg: e.matmul(
                    r, kq8[:, (ii * NCH + jj) * 256:(ii * NCH + jj) * 256 + 256]
                        .rearrange("p (i m) -> p i m", i=2),
                    xq8b[gg][:, ii * 512:(ii + 1) * 512].rearrange("p (i n) -> p i n", i=2),
                    start=(ii == 0), stop=(ii == NCH // 2 - 1), perf_mode=DR),
                    inc=("PE" if i == NCH // 2 - 1 else None))
            p.wait("scalar", t_zmm)
            p.wait("scalar", pr.get(("tzb_used", grp, j - 2)))
            if j < 2:
                p.wait("scalar", pr.get(("tzb_used", grp - 1, NCH - 2 + j)))
            t_dq = p.do("scalar", lambda e, jj=j, r=reg: e.activation(
                tzb[jj % 2][:], r, ACTF.Copy, scale=sprod[:]), inc="AE")
            pr[("ffn2_dq", grp, j)] = t_dq
            if j == NCH - 1:
                pr[("ffn2_dq_last", grp)] = t_dq
            p.wait("vector", t_dq)
            p.wait("vector", pr.get(("z2_tp0", grp - 1)))
            p.wait("vector", pr.get(("z2_tp1", grp - 1)))
            t_z2 = p.do("vector", lambda e, jj=j, gg=0: e.tensor_scalar(
                ch_view(z2_fm[gg][:], jj), tzb[jj % 2][:].rearrange("p (s f) -> p s f", s=SPG),
                fb32_c[:, jj:jj + 1], 0.5, ALU.add, ALU.mult), inc="VE")
            pr[("tzb_used", grp, j)] = t_z2
            if tail_G is not None and j == 1:
                tailB2_x2h(tail_G, 1)
        pr[("ffn2_mm", grp)] = t_zmm
        pr[("ffn2_zv", grp)] = t_z2
        if tail_G is not None:
            tailB2_stats(tail_G)
            tailB2_apply(tail_G, 0)
            tailB2_apply(tail_G, 1)

    def emit_f2tail(G):
        b, g = divmod(G, NGRP)
        grp = G
        t_ffn2 = pr[("ffn2_zv", grp)]
        t_z2tp = None
        for s2 in range(SPG):
            k2 = g * SPG + s2
            t2 = b * NSUB + k2
            p.wait("sync", t_ffn2)
            p.wait("sync", pr.get(("out_done", t2 - 2)))
            t_z2tp = p.do("sync", lambda e, ss=s2, kk=s2, gg=0: e.dma_start(
                out=chunk3(z2_tm[kk][:], 0), in_=chunk3(z2_fm[gg][:], ss, stride=GT),
                transpose=True), inc=f"TPZ2{s2}", n=16)
            pr[(f"z2_tp{s2}", grp)] = t_z2tp
            p.wait("vector", t_z2tp)
            p.wait("vector", pr.get(("out_st", t2 - 2)))
            t_out = p.do("vector", lambda e, kk=s2, xv=x2_tm[s2 + 2 * (G % 2)]: e.tensor_add(
                out_tm[kk][:], xv[:], z2_tm[kk][:]), inc="VE")
            pr[("out_done", t2)] = t_out
            pr[("x3_done", t2)] = t_out
            p.wait("gpsimd", t_out)
            pr[("out_st", t2)] = p.do("gpsimd", lambda e, bb=b, kk=k2, kki=s2: e.dma_start(
                out=d_out[bb, kk * 128:kk * 128 + 128, :], in_=out_tm[kki][:]), inc=f"SVO{s2}", n=16)

    emit_xl_load(0)
    emit_x1_load(0)
    for G in range(NGT):
        if G >= 2:
            emit_ffn2(G - 2, tail_G=G - 1)
        elif G >= 1:
            emit_tailB2(G - 1)
        emit_qno(G, tail_G=(G - 1 if G >= 1 else None))
        emit_xl_load(G + 1)
        if G >= 2:
            emit_f2tail(G - 2)
        emit_tailB1(G)
        if G >= 1:
            emit_x1_load(G)
    emit_ffn2(NGT - 2, tail_G=NGT - 1)
    tailB2_quant(NGT - 1, 0)
    tailB2_quant(NGT - 1, 1)
    emit_f2tail(NGT - 2)
    emit_ffn2(NGT - 1)
    emit_f2tail(NGT - 1)

    p.wait("gpsimd", p.cur("SVO0"))
    p.wait("gpsimd", p.cur("SVO1"))
    emit_block(nc, p)
    bes.close()
    es.close()
    return nc


def host_inputs(core_idx, inputs):
    x = np.ascontiguousarray(np.asarray(inputs["L0_attn_x"])[2 * core_idx:2 * core_idx + 2])
    wq = np.asarray(inputs["L0_attn_Wq"])
    wk = np.asarray(inputs["L0_attn_Wk"])
    rw = np.asarray(inputs["L0_attn_rw"])
    rwbd = np.zeros((NCH, 128, 128), dtype=np.float32)
    for h in range(16):
        c, r = divmod(h, 2)
        rwbd[c, r * 64:r * 64 + 64, r * 64:r * 64 + 64] = rw[h]

    def col8(v):
        return np.ascontiguousarray(np.asarray(v).reshape(NCH, 128).T)

    masks = np.zeros((16, NCH * 128), dtype=ml_dtypes.bfloat16)
    for c in range(NCH):
        masks[2 * c, c * 128:c * 128 + 64] = 1
        masks[2 * c + 1, c * 128 + 64:c * 128 + 128] = 1
    return {
        "x": x,
        "wqT": np.ascontiguousarray(wq.T),
        "wkT": np.ascontiguousarray(wk.T),
        "wv": np.ascontiguousarray(np.asarray(inputs["L0_attn_Wv"])),
        "wo": np.ascontiguousarray(np.asarray(inputs["L0_attn_Wo"])),
        "rwbd": rwbd,
        "bq_c": col8(inputs["L0_attn_bq"]),
        "bk_c": col8(inputs["L0_attn_bk"]),
        "bv_row": np.ascontiguousarray(np.asarray(inputs["L0_attn_bv"]).reshape(1, D)),
        "bo_c": col8(inputs["L0_attn_bo"]),
        "fk": np.ascontiguousarray(
            np.asarray(inputs["L0_ffn_kernel"]).reshape(4, 2, 128, 8, 128)
            .transpose(2, 0, 3, 1, 4).reshape(128, 64 * 128)),
        "fb_c": np.ascontiguousarray(np.asarray(inputs["L0_ffn_bias"]).reshape(NCH, 128).T),
        "s_in": np.asarray(inputs["L0_ffn_input_scale"], dtype=np.float32).reshape(1, 1),
        "s_k": np.asarray(inputs["L0_ffn_kernel_scale"], dtype=np.float32).reshape(1, 1),
        "c_ones128": np.ones((128, 1), dtype=ml_dtypes.bfloat16),
        "c_onesrow": np.ones((1, 128), dtype=ml_dtypes.bfloat16),
        "c_masks": masks,
    }


def kernel(**inputs):
    nc = build_core_program()
    in_maps = [host_inputs(c, inputs) for c in range(NCORES)]
    res = run_bass_kernel_spmd(nc, in_maps, core_ids=list(range(NCORES)))
    out = np.concatenate([r["out"] for r in res.results], axis=0)
    return np.ascontiguousarray(out)


# version 25
# speedup vs baseline: 1.0298x; 1.0298x over previous
import numpy as np
import ml_dtypes
from contextlib import ExitStack

import concourse.bass as bass
import concourse.mybir as mybir
from concourse.bass_utils import run_bass_kernel_spmd

F32 = mybir.dt.float32
BF16 = mybir.dt.bfloat16
FP8 = mybir.dt.float8e4
ALU = mybir.AluOpType
ACTF = mybir.ActivationFunctionType
AX = mybir.AxisListType.X

NCORES = 8
NB = 2
S = 4096
D = 1024
ST = 128
NSUB = S // ST
GT = 256
SPG = GT // ST
NGRP = S // GT
NCH = 8
EPS_LN = 1e-5
EPS_AT = 1e-6

ENGINES = ("sync", "vector", "scalar", "tensor", "gpsimd")
DR = mybir.MatmulPerfMode.DoubleRow


class P:

    AUTODRAIN = ("vector", "scalar")

    def __init__(self, val, sems):
        self.stream = {e: [] for e in ENGINES}
        self.val = val
        self.sems = sems
        self._last_drained = {e: True for e in ENGINES}

    def raw(self, eng, fn):
        self.stream[eng].append(fn)

    def wait(self, eng, tok):
        if tok and tok[1] > 0:
            h = self.sems[tok[0]]
            self.raw(eng, lambda e, s=h, v=tok[1]: e.wait_ge(s, v))

    def do(self, eng, fn, inc=None, n=1):
        if eng in self.AUTODRAIN:
            if not self._last_drained[eng]:
                self.raw(eng, lambda e: e.drain())
            self._last_drained[eng] = False
        if inc is None:
            self.raw(eng, fn)
            return None
        h = self.sems[inc]
        self.val[inc] = self.val.get(inc, 0) + n
        v = self.val[inc]
        self.raw(eng, lambda e, f=fn, hh=h, nn=n: f(e).then_inc(hh, nn))
        return (inc, v)

    def drain(self, eng):
        if not self._last_drained.get(eng, False):
            self.raw(eng, lambda e: e.drain())
            self._last_drained[eng] = True

    def cur(self, name):
        return (name, self.val.get(name, 0))


def emit_block(nc, p):
    with nc.Block() as blk:
        for eng in ENGINES:
            fns = p.stream[eng]
            if fns:
                getattr(blk, eng)(lambda e, fns=fns: [f(e) for f in fns])


def chunk3(ap_tile, sub, stride=ST):
    nsub = stride // ST
    v = ap_tile.rearrange("p (s c f) -> p s c f", s=nsub, c=NCH)
    return v[:, sub]


def chunk3cm(ap_tile, sub, nsub=SPG):
    v = ap_tile.rearrange("p (c s f) -> p s c f", c=NCH, s=nsub)
    return v[:, sub]


def ch_view(ap_tile, i, nsub=SPG):
    v = ap_tile.rearrange("p (s c f) -> p c s f", s=nsub, c=NCH)
    return v[:, i]


def build_core_program():
    nc = bass.Bass("TRN2", target_bir_lowering=False)

    d_x = nc.dram_tensor("x", [NB, S, D], F32, kind="ExternalInput").ap()
    d_wqT = nc.dram_tensor("wqT", [D, D], F32, kind="ExternalInput").ap()
    d_wkT = nc.dram_tensor("wkT", [D, D], F32, kind="ExternalInput").ap()
    d_wv = nc.dram_tensor("wv", [D, D], F32, kind="ExternalInput").ap()
    d_wo = nc.dram_tensor("wo", [D, D], F32, kind="ExternalInput").ap()
    d_rw = nc.dram_tensor("rwbd", [NCH, 128, 128], F32, kind="ExternalInput").ap()
    d_bq = nc.dram_tensor("bq_c", [128, NCH], F32, kind="ExternalInput").ap()
    d_bk = nc.dram_tensor("bk_c", [128, NCH], F32, kind="ExternalInput").ap()
    d_bvr = nc.dram_tensor("bv_row", [1, D], F32, kind="ExternalInput").ap()
    d_bo = nc.dram_tensor("bo_c", [128, NCH], F32, kind="ExternalInput").ap()
    d_fk = nc.dram_tensor("fk", [128, 64 * 128], BF16, kind="ExternalInput").ap()
    d_fb = nc.dram_tensor("fb_c", [128, NCH], BF16, kind="ExternalInput").ap()
    d_sin = nc.dram_tensor("s_in", [1, 1], F32, kind="ExternalInput").ap()
    d_sk = nc.dram_tensor("s_k", [1, 1], F32, kind="ExternalInput").ap()
    d_ones128 = nc.dram_tensor("c_ones128", [128, 1], BF16, kind="ExternalInput").ap()
    d_onesrow = nc.dram_tensor("c_onesrow", [1, 128], BF16, kind="ExternalInput").ap()
    d_masks = nc.dram_tensor("c_masks", [16, NCH * 128], BF16, kind="ExternalInput").ap()

    d_out = nc.dram_tensor("out", [NB, S, D], F32, kind="ExternalOutput").ap()
    d_swq_hi = nc.dram_tensor("swq_hi", [128, 64 * 128], BF16, kind="Internal").ap()
    d_swq_lo = nc.dram_tensor("swq_lo", [128, 64 * 128], BF16, kind="Internal").ap()
    d_swo_hi = nc.dram_tensor("swo_hi", [128, 64 * 128], BF16, kind="Internal").ap()
    d_swo_lo = nc.dram_tensor("swo_lo", [128, 64 * 128], BF16, kind="Internal").ap()
    d_sx1 = nc.dram_tensor("sx1", [NB, NSUB, 128, D], F32, kind="Internal").ap()
    d_sxlh = nc.dram_tensor("sxlh", [NB, NGRP, 128, NCH * GT], BF16, kind="Internal").ap()
    d_sxll = nc.dram_tensor("sxll", [NB, NGRP, 128, NCH * GT], BF16, kind="Internal").ap()

    es = ExitStack()
    sems = {n: es.enter_context(nc.semaphore(n))
            for n in ("WT", "WTS", "WTB", "WTW",
                      "LXA0", "LXA1", "LXB0", "LXB1", "LX10", "LX11",
                      "TPQ0", "TPQ1", "TPZ0", "TPZ1", "TPL0", "TPL1",
                      "TPO0", "TPO1", "TPOA0", "TPOA1", "TP30", "TP31", "TPZ20", "TPZ21",
                      "SVW", "SV10", "SV11", "SVX0", "SVX1", "SVO0", "SVO1",
                      "VE", "AE", "PE", "GE")}
    val = {}

    def sb(stack, name, shape, dtype, **kw):
        return stack.enter_context(nc.sbuf_tensor(name, shape, dtype, **kw))

    wa_hi = sb(es, "wa_hi", [128, 64 * 128], BF16)
    wa_lo = sb(es, "wa_lo", [128, 64 * 128], BF16)
    wb_hi = sb(es, "wb_hi", [128, 64 * 128], BF16)
    wb_lo = sb(es, "wb_lo", [128, 64 * 128], BF16)
    kq8 = sb(es, "kq8", [128, 64 * 128], FP8)
    bqp_c = sb(es, "bqp_c", [128, NCH], F32)
    bo_sb = sb(es, "bo_sb", [128, NCH], F32)
    fb32_c = sb(es, "fb32_c", [128, NCH], F32)
    bkb = sb(es, "bkb", [128, D], F32)
    bvb = sb(es, "bvb", [128, D], F32)
    epsln = sb(es, "epsln", [128, 1], F32)
    rsin = sb(es, "rsin", [128, 1], F32)
    rsk = sb(es, "rsk", [128, 1], F32)
    sprod = sb(es, "sprod", [128, 1], F32)
    ones128 = sb(es, "ones128", [128, 1], BF16)
    onesrow = sb(es, "onesrow", [1, 128], BF16)
    masks = sb(es, "masks", [16, NCH * 128], BF16)
    kv_hi = sb(es, "kv_hi", [128, NB * D], BF16)
    kv_lo = sb(es, "kv_lo", [128, NB * D], BF16)
    ksb_hi = sb(es, "ksb_hi", [128, NB * NCH * 16], BF16)
    ksb_lo = sb(es, "ksb_lo", [128, NB * NCH * 16], BF16)

    p = P(val, sems)
    pes = ExitStack()
    pst = sb(pes, "pst", [128, 64 * 128], F32)
    psh = sb(pes, "psh", [128, 64 * 128], BF16)
    psl = sb(pes, "psl", [128, 64 * 128], BF16)
    wout_hi = sb(pes, "wout_hi", [128, 64 * 128], BF16)
    wout_lo = sb(pes, "wout_lo", [128, 64 * 128], BF16)
    rwf = sb(pes, "rwf", [128, NCH * 128], F32)
    rw_hi = sb(pes, "rw_hi", [128, NCH * 128], BF16)
    rw_lo = sb(pes, "rw_lo", [128, NCH * 128], BF16)
    bstg = sb(pes, "bstg", [128, NCH], F32)
    bshi = sb(pes, "bshi", [128, NCH], BF16)
    bslo = sb(pes, "bslo", [128, NCH], BF16)
    rowf = sb(pes, "rowf", [1, D], F32)
    rowh = sb(pes, "rowh", [1, D], BF16)
    rowl = sb(pes, "rowl", [1, D], BF16)
    sc_f = sb(pes, "sc_f", [1, 1], F32)
    sc_h = sb(pes, "sc_h", [1, 1], BF16)
    sc_l = sb(pes, "sc_l", [1, 1], BF16)
    t128a = sb(pes, "t128a", [128, 1], F32)
    t128b = sb(pes, "t128b", [128, 1], F32)
    pp_w = pes.enter_context(nc.psum_tensor("pp_w", [128, 128], F32))
    pp_w2 = pes.enter_context(nc.psum_tensor("pp_w2", [128, 128], F32))
    pp_r = pes.enter_context(nc.psum_tensor("pp_r", [1, D], F32))
    pp_b = pes.enter_context(nc.psum_tensor("pp_b", [128, 512], F32))
    pp_c = pes.enter_context(nc.psum_tensor("pp_c", [128, NCH], F32))

    t_ld0 = None
    for dst, src in [
        (ones128[:], d_ones128), (onesrow[:], d_onesrow), (masks[:], d_masks),
        (rwf[:].rearrange("p (c f) -> p c f", c=NCH), d_rw.rearrange("c r f -> r c f")),
        (bo_sb[:], d_bo), (sc_f[:], d_sin),
    ]:
        t_ld0 = p.do("gpsimd", lambda e, d=dst, s=src: e.dma_start(out=d, in_=s), inc="WT", n=16)
    t_ldfb = p.do("gpsimd", lambda e: e.dma_start(out=bshi[:], in_=d_fb), inc="WT", n=16)

    p.wait("vector", t_ldfb)
    p.do("vector", lambda e: e.tensor_copy(rw_hi[:], rwf[:]))
    p.drain("vector")
    t_rw = p.do("vector", lambda e: e.tensor_sub(rw_lo[:], rwf[:], rw_hi[:]), inc="VE")
    p.wait("vector", t_ldfb)
    t_fb32 = p.do("vector", lambda e: e.tensor_copy(fb32_c[:], bshi[:]), inc="VE")

    def bcast_scalar(dst128, t_loaded):
        p.wait("vector", t_loaded)
        p.do("vector", lambda e: e.tensor_copy(sc_h[:], sc_f[:]))
        p.drain("vector")
        t = p.do("vector", lambda e: e.tensor_sub(sc_l[:], sc_f[:], sc_h[:]), inc="VE")
        p.wait("tensor", t)
        p.do("tensor", lambda e: e.matmul(pp_w[:, 0:1], onesrow[:], sc_h[:], start=True, stop=False))
        t = p.do("tensor", lambda e: e.matmul(pp_w[:, 0:1], onesrow[:], sc_l[:], start=False, stop=True), inc="PE")
        p.wait("vector", t)
        return p.do("vector", lambda e: e.tensor_copy(dst128[:], pp_w[:, 0:1]), inc="VE")

    t_sa = bcast_scalar(t128b, t_ldfb)
    p.drain("vector")
    p.do("vector", lambda e: e.tensor_copy(rsin[:], t128b[:]))
    p.drain("vector")
    p.wait("gpsimd", t_sa)
    t_ldk = p.do("gpsimd", lambda e: e.dma_start(out=sc_f[:], in_=d_sk), inc="WTS", n=16)
    t_sb2 = bcast_scalar(t128b, t_ldk)
    p.drain("vector")
    p.do("vector", lambda e: e.tensor_copy(rsk[:], t128b[:]))
    p.drain("vector")
    p.do("vector", lambda e: e.tensor_mul(sprod[:], rsin[:], rsk[:]))
    p.drain("vector")

    def recip_nr(dst_and_src):
        d = dst_and_src
        p.do("vector", lambda e: e.reciprocal(t128a[:], d[:]))
        p.drain("vector")
        p.do("vector", lambda e: e.tensor_mul(t128b[:], d[:], t128a[:]))
        p.drain("vector")
        p.do("vector", lambda e: e.tensor_scalar(t128b[:], t128b[:], 2.0, -1.0, ALU.subtract, ALU.mult))
        p.drain("vector")
        return p.do("vector", lambda e: e.tensor_mul(d[:], t128a[:], t128b[:]), inc="VE")

    recip_nr(rsin)
    t_rsk = recip_nr(rsk)

    p.do("vector", lambda e: e.memset(epsln[:], EPS_LN))
    p.do("vector", lambda e: e.memzero(kv_hi[:]))
    p.do("vector", lambda e: e.memzero(kv_lo[:]))
    p.do("vector", lambda e: e.memzero(ksb_hi[:]))
    t_zero = p.do("vector", lambda e: e.memzero(ksb_lo[:]), inc="VE")

    def load_split_w(dsrc, pre_chunked, t_pe_free, t_sv_free, out_hi=psh, out_lo=psl):
        p.wait("gpsimd", t_pe_free)
        p.wait("gpsimd", t_sv_free)
        if pre_chunked:
            t = p.do("gpsimd", lambda e: e.dma_start(
                out=pst[:].rearrange("p (i j f) -> p i j f", i=NCH, j=NCH),
                in_=dsrc.rearrange("(i p) (j f) -> p i j f", p=128, f=128)), inc="WTW", n=16)
        else:
            t = p.do("gpsimd", lambda e: e.dma_start(
                out=pst[:].rearrange("p (i e) -> p i e", i=NCH),
                in_=dsrc.rearrange("(i p) e -> p i e", p=128)), inc="WTW", n=16)
        p.wait("vector", t)
        p.wait("vector", t_pe_free)
        p.do("vector", lambda e, o=out_hi: e.tensor_copy(o[:], pst[:]))
        p.drain("vector")
        return p.do("vector", lambda e, o1=out_lo, o2=out_hi: e.tensor_sub(o1[:], pst[:], o2[:]), inc="VE")

    def fold_rw(t_split, dst_hi, dst_lo, t_dst_free):
        p.wait("tensor", t_split)
        p.wait("tensor", t_rw)
        p.wait("scalar", t_dst_free)
        p.wait("vector", t_dst_free)
        t_lo = {}
        t_mm = None
        for n, (ii, jj) in enumerate([(a, b) for a in range(NCH) for b in range(NCH)]):
            pw = pp_w if n % 2 == 0 else pp_w2
            sl_hi = psh[:, (jj * NCH + ii) * 128:(jj * NCH + ii + 1) * 128]
            sl_lo = psl[:, (jj * NCH + ii) * 128:(jj * NCH + ii + 1) * 128]
            r_hi = rw_hi[:, jj * 128:(jj + 1) * 128]
            r_lo = rw_lo[:, jj * 128:(jj + 1) * 128]
            p.wait("tensor", t_lo.get(n - 2))
            p.do("tensor", lambda e, a=sl_hi, b=r_hi, w=pw: e.matmul(w[:], a, b, start=True, stop=False))
            p.do("tensor", lambda e, a=sl_hi, b=r_lo, w=pw: e.matmul(w[:], a, b, start=False, stop=False))
            t_mm = p.do("tensor", lambda e, a=sl_lo, b=r_hi, w=pw: e.matmul(w[:], a, b, start=False, stop=True), inc="PE")
            o_hi = dst_hi[:, (ii * NCH + jj) * 128:(ii * NCH + jj + 1) * 128]
            o_lo = dst_lo[:, (ii * NCH + jj) * 128:(ii * NCH + jj + 1) * 128]
            p.wait("scalar", t_mm)
            t_hi = p.do("scalar", lambda e, o=o_hi, w=pw: e.activation(o, w[:], ACTF.Copy), inc="AE")
            p.wait("vector", t_hi)
            t_lo[n] = p.do("vector", lambda e, o=o_lo, h=o_hi, w=pw: e.tensor_sub(o, w[:], h), inc="VE")
        return t_lo[63], t_mm

    def bias_fold(t_bload, dst_c):
        p.wait("vector", t_bload)
        p.do("vector", lambda e: e.tensor_copy(bshi[:], bstg[:]))
        p.drain("vector")
        t_b = p.do("vector", lambda e: e.tensor_sub(bslo[:], bstg[:], bshi[:]), inc="VE")
        p.wait("tensor", t_b)
        t_mm = None
        for i in range(NCH):
            r_hi = rw_hi[:, i * 128:(i + 1) * 128]
            r_lo = rw_lo[:, i * 128:(i + 1) * 128]
            p.do("tensor", lambda e, a=r_hi, b=bshi[:, i:i + 1], c=i, s=(i == 0): e.matmul(
                pp_c[:, c:c + 1], a, b, start=s, stop=False))
            p.do("tensor", lambda e, a=r_hi, b=bslo[:, i:i + 1], c=i: e.matmul(
                pp_c[:, c:c + 1], a, b, start=False, stop=False))
            t_mm = p.do("tensor", lambda e, a=r_lo, b=bshi[:, i:i + 1], c=i, s=(i == NCH - 1): e.matmul(
                pp_c[:, c:c + 1], a, b, start=False, stop=s),
                inc=("PE" if i == NCH - 1 else None))
        p.wait("vector", t_mm)
        return p.do("vector", lambda e: e.tensor_copy(dst_c[:], pp_c[:]), inc="VE")

    def bcast_row(t_ready, dst):
        p.wait("tensor", t_ready)
        tt = None
        for h in range(2):
            p.wait("tensor", tt)
            p.do("tensor", lambda e, hh=h: e.matmul(
                pp_b[:], onesrow[:], rowh[0:1, hh * 512:hh * 512 + 512], start=True, stop=False))
            t = p.do("tensor", lambda e, hh=h: e.matmul(
                pp_b[:], onesrow[:], rowl[0:1, hh * 512:hh * 512 + 512], start=False, stop=True), inc="PE")
            p.wait("vector", t)
            tt = p.do("vector", lambda e, hh=h: e.tensor_copy(dst[:, hh * 512:hh * 512 + 512], pp_b[:]), inc="VE")
        return tt

    t_sp = load_split_w(d_wqT, True, None, None)
    t_wq_ev, t_wq_mm = fold_rw(t_sp, wout_hi, wout_lo, None)
    p.wait("gpsimd", t_wq_ev)
    p.do("gpsimd", lambda e: e.dma_start(out=d_swq_hi, in_=wout_hi[:]), inc="SVW", n=16)
    t_wq_st = p.do("gpsimd", lambda e: e.dma_start(out=d_swq_lo, in_=wout_lo[:]), inc="SVW", n=16)

    t_ldbq = p.do("gpsimd", lambda e: e.dma_start(out=bstg[:], in_=d_bq), inc="WTB", n=16)
    t_bqp = bias_fold(t_ldbq, bqp_c)

    t_sp2 = load_split_w(d_wo, True, t_wq_mm, None)
    p.wait("gpsimd", t_sp2)
    p.do("gpsimd", lambda e: e.dma_start(out=d_swo_hi, in_=psh[:]), inc="SVW", n=16)
    t_wo_st = p.do("gpsimd", lambda e: e.dma_start(out=d_swo_lo, in_=psl[:]), inc="SVW", n=16)

    t_sp3 = load_split_w(d_wkT, True, None, t_wo_st)
    t_wk_ev, t_wk_mm = fold_rw(t_sp3, wa_hi, wa_lo, None)

    p.wait("gpsimd", t_bqp)
    t_ldbk = p.do("gpsimd", lambda e: e.dma_start(out=bstg[:], in_=d_bk), inc="WTB", n=16)
    p.wait("vector", t_ldbk)
    p.do("vector", lambda e: e.tensor_copy(bshi[:], bstg[:]))
    p.drain("vector")
    t_bk = p.do("vector", lambda e: e.tensor_sub(bslo[:], bstg[:], bshi[:]), inc="VE")
    p.wait("tensor", t_bk)
    t_mm = None
    for i in range(NCH):
        r_hi = rw_hi[:, i * 128:(i + 1) * 128]
        r_lo = rw_lo[:, i * 128:(i + 1) * 128]
        slr = pp_r[0:1, i * 128:(i + 1) * 128]
        p.do("tensor", lambda e, a=bshi[:, i:i + 1], b=r_hi, s=slr, st=(i % 4 == 0): e.matmul(
            s, a, b, start=st, stop=False))
        p.do("tensor", lambda e, a=bshi[:, i:i + 1], b=r_lo, s=slr: e.matmul(s, a, b, start=False, stop=False))
        t_mm = p.do("tensor", lambda e, a=bslo[:, i:i + 1], b=r_hi, s=slr, st=(i % 4 == 3): e.matmul(
            s, a, b, start=False, stop=st),
                    inc=("PE" if i == NCH - 1 else None))
    p.wait("vector", t_mm)
    p.do("vector", lambda e: e.tensor_copy(rowf[:], pp_r[:]))
    p.drain("vector")
    p.do("vector", lambda e: e.tensor_copy(rowh[:], rowf[:]))
    p.drain("vector")
    t_row = p.do("vector", lambda e: e.tensor_sub(rowl[:], rowf[:], rowh[:]), inc="VE")
    t_bkb = bcast_row(t_row, bkb)

    p.wait("gpsimd", t_bkb)
    t_ldbv = p.do("gpsimd", lambda e: e.dma_start(out=rowf[:], in_=d_bvr), inc="WTB", n=16)
    p.wait("vector", t_ldbv)
    p.do("vector", lambda e: e.tensor_copy(rowh[:], rowf[:]))
    p.drain("vector")
    t_row2 = p.do("vector", lambda e: e.tensor_sub(rowl[:], rowf[:], rowh[:]), inc="VE")
    t_bvb = bcast_row(t_row2, bvb)

    t_wv = load_split_w(d_wv, False, t_wk_mm, None, out_hi=wb_hi, out_lo=wb_lo)

    emit_block(nc, p)
    pes.close()

    p = P(val, sems)
    kes = ExitStack()
    kbf = sb(kes, "kbf", [128, 64 * 128], BF16)
    p.wait("gpsimd", t_wv)
    p.wait("gpsimd", t_bvb)
    p.wait("gpsimd", t_wo_st)
    t_ldkq = p.do("gpsimd", lambda e: e.dma_start(
        out=kbf[:], in_=d_fk), inc="WTW", n=16)
    p.wait("scalar", t_ldkq)
    p.wait("scalar", t_rsk)
    t_kq = p.do("scalar", lambda e: e.activation(kq8[:], kbf[:], ACTF.Copy, scale=rsk[:]), inc="AE")
    emit_block(nc, p)
    kes.close()

    pro = dict(wk=t_wk_ev, wv=t_wv, kq=t_kq, bqp=t_bqp, bkb=t_bkb, bvb=t_bvb,
               zeroed=t_zero)

    p = P(val, sems)
    aes = ExitStack()
    x_tm = [sb(aes, f"x_tm{i}", [128, D], F32) for i in range(2)]
    sq = sb(aes, "sq", [128, D], BF16)
    stA = [dict(nsum=sb(aes, f"Ans{i}", [128, 2], F32), ssum=sb(aes, f"Ass{i}", [128, 2], F32),
                negm=sb(aes, f"Anm{i}", [128, 2], F32), t1=sb(aes, f"At1{i}", [128, 2], F32),
                t2=sb(aes, f"At2{i}", [128, 2], F32), rstd=sb(aes, f"Ars{i}", [128, 2], F32))
           for i in range(2)]
    stC = [dict(nsum=sb(aes, f"Cns{i}", [128, 2], F32), ssum=sb(aes, f"Css{i}", [128, 2], F32),
                negm=sb(aes, f"Cnm{i}", [128, 2], F32), t1=sb(aes, f"Ct1{i}", [128, 2], F32),
                t2=sb(aes, f"Ct2{i}", [128, 2], F32), rstd=sb(aes, f"Crs{i}", [128, 2], F32))
           for i in range(2)]
    xq_bf = [sb(aes, f"xq_bf{i}", [128, D], BF16) for i in range(2)]
    xq_fm = [sb(aes, f"xq_fm{i}", [128, NCH * GT], BF16, is_transpose_dest=True) for i in range(2)]
    xq8 = [sb(aes, f"xq8{i}", [128, NCH * GT], FP8) for i in range(2)]
    tz = [sb(aes, f"tz{i}", [128, GT], BF16) for i in range(2)]
    z_fm = [sb(aes, f"z_fm{i}", [128, NCH * GT], BF16) for i in range(2)]
    z_tm = [sb(aes, f"z_tm{i}", [128, D], BF16, is_transpose_dest=True) for i in range(2)]
    x1_tm = [sb(aes, f"x1_tm{i}", [128, D], F32) for i in range(2)]
    u_ln = [sb(aes, f"u_ln{i}", [128, D], F32) for i in range(2)]
    xlh_tm = [sb(aes, f"xlh_tm{i}", [128, D], BF16) for i in range(2)]
    xll_tm = [sb(aes, f"xll_tm{i}", [128, D], BF16) for i in range(2)]
    xlh_fm = [sb(aes, f"xlh_fm{i}", [128, NCH * GT], BF16, is_transpose_dest=True) for i in range(2)]
    xll_fm = [sb(aes, f"xll_fm{i}", [128, NCH * GT], BF16, is_transpose_dest=True) for i in range(2)]
    v_t = [sb(aes, f"v_t{i}", [128, D], F32) for i in range(2)]
    v_hi = [sb(aes, f"v_hi{i}", [128, D], BF16) for i in range(2)]
    v_lo = [sb(aes, f"v_lo{i}", [128, D], BF16) for i in range(2)]
    kp_t = [sb(aes, f"kp_t{i}", [128, D], F32) for i in range(2)]
    kp_hi = [sb(aes, f"kp_hi{i}", [128, D], BF16) for i in range(2)]
    kp_lo = [sb(aes, f"kp_lo{i}", [128, D], BF16) for i in range(2)]

    P_kv = aes.enter_context(nc.psum_tensor("P_kv", [128, D], F32))
    P_ks = aes.enter_context(nc.psum_tensor("P_ks", [128, NCH], F32))
    P_V = aes.enter_context(nc.psum_tensor("P_V", [128, D], F32))
    P_K = aes.enter_context(nc.psum_tensor("P_K", [128, D], F32))

    rel = {}

    def stats_pair(x_aps, st, scratch, t_st_free_v, t_st_free_a, t_readys):
        p.wait("vector", t_st_free_v)
        for i, x_ap in enumerate(x_aps):
            p.do("vector", lambda e, xx=x_ap, ii=i: e.reduce_sum(
                st["nsum"][:, ii:ii + 1], xx, axis=AX, negate=True))
        t_sq = None
        for i, x_ap in enumerate(x_aps):
            if i == 0:
                p.wait("scalar", t_st_free_a)
            p.wait("scalar", t_readys[i])
            p.drain("scalar")
            t_sq = p.do("scalar", lambda e, xx=x_ap, ii=i: e.activation(
                scratch[:], xx, ACTF.Square, accum_out=st["ssum"][:, ii:ii + 1]), inc="AE")
        n = len(x_aps)
        p.do("vector", lambda e: e.tensor_scalar_mul(st["negm"][:, 0:n], st["nsum"][:, 0:n], 1.0 / D))
        p.wait("vector", t_sq)
        p.do("vector", lambda e: e.tensor_scalar_mul(st["t1"][:, 0:n], st["ssum"][:, 0:n], 1.0 / D))
        p.do("vector", lambda e: e.tensor_mul(st["t2"][:, 0:n], st["negm"][:, 0:n], st["negm"][:, 0:n]))
        t_var = p.do("vector", lambda e: e.tensor_sub(
            st["nsum"][:, 0:n], st["t2"][:, 0:n], st["t1"][:, 0:n]), inc="VE")
        p.wait("scalar", t_var)
        t_s = p.do("scalar", lambda e: e.activation(st["t2"][:, 0:n], st["nsum"][:, 0:n],
                                                    ACTF.Sqrt, bias=epsln[:], scale=-1.0), inc="AE")
        p.wait("vector", t_s)
        p.do("vector", lambda e: e.reciprocal(st["t1"][:, 0:n], st["t2"][:, 0:n]))
        p.do("vector", lambda e: e.tensor_mul(st["rstd"][:, 0:n], st["t2"][:, 0:n], st["t1"][:, 0:n]))
        p.do("vector", lambda e: e.tensor_scalar(st["rstd"][:, 0:n], st["rstd"][:, 0:n], 2.0, -1.0,
                                                 ALU.subtract, ALU.mult))
        t_r = p.do("vector", lambda e: e.tensor_mul(st["rstd"][:, 0:n], st["t1"][:, 0:n],
                                                    st["rstd"][:, 0:n]), inc="VE")
        return t_r

    def proj_tm(xh_sl, xl_sl, w_h, w_l, reg_psum, i, first, last):
        t = None
        for half in range(2):
            regv = reg_psum[:, half * 512:half * 512 + 512]
            mv_h = w_h[:, i * D + half * 512:i * D + half * 512 + 512]
            mv_l = w_l[:, i * D + half * 512:i * D + half * 512 + 512]
            p.do("tensor", lambda e, a=xh_sl, m=mv_h, r=regv, s=first: e.matmul(r, a, m, start=s, stop=False))
            p.do("tensor", lambda e, a=xh_sl, m=mv_l, r=regv: e.matmul(r, a, m, start=False, stop=False))
            t = p.do("tensor", lambda e, a=xl_sl, m=mv_h, r=regv, s=last: e.matmul(r, a, m, start=False, stop=s),
                     inc=("PE" if (last and half == 1) else None))
        return t

    p.wait("gpsimd", pro["zeroed"])
    rel[("xld", 0)] = p.do("gpsimd", lambda e: e.dma_start(
        out=x_tm[0][:], in_=d_x[0, 0:128, :]), inc="LXA0", n=16)

    NGT = NB * NGRP

    def prefetch_x(tid):
        if tid + 1 >= NB * NSUB:
            return
        nb_, nk_ = divmod(tid + 1, NSUB)
        p.wait("gpsimd", rel.get(("xfree", tid - 1)))
        p.wait("gpsimd", rel.get(("stA_free_a", (tid - 1) // 2)))
        rel[("xld", tid + 1)] = p.do("gpsimd", lambda e, bb=nb_, kk=nk_: e.dma_start(
            out=x_tm[(kk % 2)][:], in_=d_x[bb, kk * 128:kk * 128 + 128, :]),
            inc=f"LXA{(tid + 1) % 2}", n=16)

    def emit_ln1_pair(G):
        b, g = divmod(G, NGRP)
        gi = g % 2
        grp = G
        tids = [b * NSUB + 2 * g, b * NSUB + 2 * g + 1]
        for tid in tids:
            p.wait("vector", rel[("xld", tid)])
        t_r = stats_pair([x_tm[0][:], x_tm[1][:]], stA[gi], sq,
                         rel.get(("stA_free", grp - 2)), rel.get(("stA_free_a", grp - 2)),
                         [rel[("xld", t)] for t in tids])
        for i, tid in enumerate(tids):
            ki = tid % 2
            k = tid % NSUB
            p.wait("vector", rel.get(("xq_tp", tid - 2)))
            t_xq = p.do("vector", lambda e, kk=ki, ii=i: e.tensor_scalar(
                xq_bf[kk][:], x_tm[kk][:], stA[gi]["negm"][:, ii:ii + 1],
                stA[gi]["rstd"][:, ii:ii + 1], ALU.add, ALU.mult), inc="VE")
            if i == 1:
                rel[("stA_free", grp)] = t_xq
            p.wait("sync", t_xq)
            p.wait("sync", rel.get(("q8_tile", tid - 2)))
            t_tp = p.do("sync", lambda e, kk=ki, gg=gi, ss=k % SPG: e.dma_start(
                out=chunk3(xq_fm[gg][:], ss, stride=GT),
                in_=chunk3(xq_bf[kk][:], 0), transpose=True), inc=f"TPQ{tid % 2}", n=16)
            rel[("xq_tp", tid)] = t_tp
            p.wait("scalar", t_tp)
            p.wait("scalar", rel.get(("ffn_mm", grp - 2)))
            t_q8 = p.do("scalar", lambda e, gg=gi, ss=k % SPG: e.activation(
                chunk3cm(xq8[gg][:], ss), chunk3(xq_fm[gg][:], ss, stride=GT),
                ACTF.Copy, scale=rsin[:]), inc="AE")
            if i == 1:
                rel[("stA_free_a", grp)] = t_q8
            rel[("q8_tile", tid)] = t_q8
            if i == 1:
                rel[("xq8_done", grp)] = t_q8

    def emit_ffn1(G):
        b, g = divmod(G, NGRP)
        gi = g % 2
        tid = b * NSUB + 2 * g + 1
        grp = G
        p.wait("tensor", rel[("xq8_done", grp)])
        if grp == 0:
            p.wait("tensor", pro["kq"])
        t_fmm = None
        t_zv = None
        for j in range(NCH):
            reg = (P_V if j % 2 == 0 else P_K)[:, 0:GT]
            if j == 0:
                p.wait("tensor", rel.get(("V_ev", tid - 3)))
            elif j == 1:
                p.wait("tensor", rel.get(("KP_ev", tid - 3)))
            else:
                p.wait("tensor", rel.get(("ffn_dq", grp, j - 2)))
            for i in range(NCH // 2):
                t_fmm = p.do("tensor", lambda e, ii=i, jj=j, gg=gi, r=reg: e.matmul(
                    r, kq8[:, (ii * NCH + jj) * 256:(ii * NCH + jj) * 256 + 256]
                        .rearrange("p (i m) -> p i m", i=2),
                    xq8[gg][:, ii * 512:(ii + 1) * 512].rearrange("p (i n) -> p i n", i=2),
                    start=(ii == 0), stop=(ii == NCH // 2 - 1), perf_mode=DR),
                    inc=("PE" if i == NCH // 2 - 1 else None))
            p.wait("scalar", t_fmm)
            p.wait("scalar", rel.get(("tz_free", grp, j - 2)))
            t_dq = p.do("scalar", lambda e, jj=j, r=reg: e.activation(
                tz[jj % 2][:], r, ACTF.Copy, scale=sprod[:]), inc="AE")
            rel[("ffn_dq", grp, j)] = t_dq
            p.wait("vector", t_dq)
            p.wait("vector", rel.get(("z_tp0", grp - 2)))
            p.wait("vector", rel.get(("z_tp1", grp - 2)))
            t_zv = p.do("vector", lambda e, jj=j, gg=gi: e.tensor_scalar(
                ch_view(z_fm[gg][:], jj), tz[jj % 2][:].rearrange("p (s f) -> p s f", s=SPG),
                fb32_c[:, jj:jj + 1], 0.5, ALU.add, ALU.mult), inc="VE")
            rel[("tz_free", grp, j)] = t_zv
        rel[("ffn_mm", grp)] = t_fmm
        rel[("ffn_zv", grp)] = t_zv

    def emit_tail_a(G):
        b, g = divmod(G, NGRP)
        gi = g % 2
        grp = G
        t_ffn = rel[("ffn_zv", grp)]
        t_x1s = []
        for k2 in (2 * g, 2 * g + 1):
            k2i = k2 % 2
            t2 = b * NSUB + k2
            ss = k2 % SPG
            p.wait("sync", t_ffn)
            p.wait("sync", rel.get(("x1_done", t2 - 2)))
            t_ztp = p.do("sync", lambda e, gg=gi, s2=ss, kk=k2i: e.dma_start(
                out=chunk3(z_tm[kk][:], 0), in_=chunk3(z_fm[gg][:], s2, stride=GT),
                transpose=True), inc=f"TPZ{k2 % 2}", n=16)
            rel[(f"z_tp{ss}", grp)] = t_ztp
            p.wait("vector", t_ztp)
            p.wait("vector", rel.get(("x1_st", t2 - 2)))
            p.wait("vector", rel.get(("stC_free_a", grp - 1)))
            t_x1 = p.do("vector", lambda e, kk=k2i: e.tensor_add(
                x1_tm[kk][:], x_tm[kk][:], z_tm[kk][:]), inc="VE")
            rel[("x1_done", t2)] = t_x1
            rel[("xfree", t2)] = t_x1
            t_x1s.append(t_x1)
            p.wait("gpsimd", t_x1)
            rel[("x1_st", t2)] = p.do("gpsimd", lambda e, bb=b, kk2=k2, kk=k2i: e.dma_start(
                out=d_sx1[bb, kk2], in_=x1_tm[kk][:]), inc=f"SV1{k2 % 2}", n=16)
        stats_pair([x1_tm[0][:], x1_tm[1][:]], stC[gi], sq,
                   rel.get(("stC_free", grp - 2)), rel.get(("stC_free_a", grp - 2)),
                   t_x1s)
        emit_tail_split(G, 0)

    def emit_tail_split(G, i):
        b, g = divmod(G, NGRP)
        gi = g % 2
        grp = G
        k2 = 2 * g + i
        k2i = k2 % 2
        t2 = b * NSUB + k2
        ss = k2 % SPG
        p.wait("vector", rel.get(("xl_hi_used_a", t2 - 2)))
        p.wait("vector", rel.get(("kp_split", t2 - 2)))
        t_u = p.do("vector", lambda e, kk=k2i, ii=i: e.tensor_scalar(
            u_ln[kk][:], x1_tm[kk][:], stC[gi]["negm"][:, ii:ii + 1],
            stC[gi]["rstd"][:, ii:ii + 1], ALU.add, ALU.mult), inc="VE")
        p.wait("vector", rel.get(("xl_tp", t2 - 2)))
        t_h = p.do("vector", lambda e, kk=k2i: e.tensor_copy(
            xlh_tm[kk][:], u_ln[kk][:]), inc="VE")
        rel[("xl_hi_used_a", t2)] = t_h
        t_l = p.do("vector", lambda e, kk=k2i: e.tensor_sub(
            xll_tm[kk][:], u_ln[kk][:], xlh_tm[kk][:]), inc="VE")
        if i == 1:
            rel[("stC_free", grp)] = t_l
            rel[("stC_free_a", grp)] = t_l
        p.wait("sync", t_l)
        p.wait("sync", rel.get(("xl_stored", grp - 2)))
        p.wait("sync", rel.get(("kmm", 2 * G - 3)))
        p.do("sync", lambda e, gg=gi, s2=ss, kk=k2i: e.dma_start(
            out=chunk3(xlh_fm[gg][:], s2, stride=GT),
            in_=chunk3(xlh_tm[kk][:], 0), transpose=True), inc=f"TPL{k2 % 2}", n=16)
        t_tpl = p.do("sync", lambda e, gg=gi, s2=ss, kk=k2i: e.dma_start(
            out=chunk3(xll_fm[gg][:], s2, stride=GT),
            in_=chunk3(xll_tm[kk][:], 0), transpose=True), inc=f"TPL{k2 % 2}", n=16)
        rel[("xl_tp", t2)] = t_tpl

    def emit_tail_b(G):
        b, g = divmod(G, NGRP)
        gi = g % 2
        grp = G
        emit_tail_split(G, 1)
        tid = b * NSUB + 2 * g + 1
        p.wait("gpsimd", rel[("xl_tp", tid - 1)])
        p.wait("gpsimd", rel[("xl_tp", tid)])
        p.do("gpsimd", lambda e, bb=b, gg2=g, gg=gi: e.dma_start(
            out=d_sxlh[bb, gg2], in_=xlh_fm[gg][:]), inc=f"SVX{gi}", n=16)
        rel[("xl_stored", grp)] = p.do("gpsimd", lambda e, bb=b, gg2=g, gg=gi: e.dma_start(
            out=d_sxll[bb, gg2], in_=xll_fm[gg][:]), inc=f"SVX{gi}", n=16)
        prefetch_x(2 * G + 1)

    def emit_vk_projs(G, which):
        b, g = divmod(G, NGRP)
        gi = g % 2
        grp = G
        for k2 in (2 * g + which,):
            k2i = k2 % 2
            t2 = b * NSUB + k2
            ss = k2 % SPG
            p.wait("tensor", rel[("xl_tp", t2)])
            if grp == 0:
                p.wait("tensor", pro["wv"])
                p.wait("tensor", pro["bvb"])
                p.wait("tensor", pro["bkb"])
                p.wait("tensor", pro["wk"])
            t_vmm = None
            p.wait("tensor", rel.get(("V_ev", t2 - 1)))
            if which == 1:
                p.wait("tensor", rel.get(("ffn_dq", grp + 1, NCH - 2)))
            xh = chunk3(xlh_fm[gi][:], ss, stride=GT)
            xl = chunk3(xll_fm[gi][:], ss, stride=GT)
            for i in range(NCH):
                t_vmm = proj_tm(xh[:, i, :], xl[:, i, :], wb_hi, wb_lo, P_V, i,
                                first=(i == 0), last=(i == NCH - 1)) or t_vmm
            p.wait("vector", t_vmm)
            p.wait("vector", rel.get(("v_split", t2 - 2)))
            p.wait("vector", rel.get(("kv_mm", t2 - 2)))
            t_vt = p.do("vector", lambda e, kk=k2i: e.tensor_add(
                v_t[kk][:], P_V[:], bvb[:]), inc="VE")
            rel[("V_ev", t2)] = t_vt
            p.wait("vector", rel.get(("kv_mm", t2 - 2)))
            p.do("vector", lambda e, kk=k2i: e.tensor_copy(v_hi[kk][:], v_t[kk][:]))
            t_vl = p.do("vector", lambda e, kk=k2i: e.tensor_sub(
                v_lo[kk][:], v_t[kk][:], v_hi[kk][:]), inc="VE")
            rel[("v_split", t2)] = t_vl

            p.wait("tensor", rel.get(("KP_ev", t2 - 1)))
            if which == 1:
                p.wait("tensor", rel.get(("ffn_dq", grp + 1, NCH - 1)))
            t_kmm = None
            for i in range(NCH):
                t_kmm = proj_tm(xh[:, i, :], xl[:, i, :], wa_hi, wa_lo, P_K, i,
                                first=(i == 0), last=(i == NCH - 1)) or t_kmm
            p.wait("vector", t_kmm)
            p.wait("vector", rel.get(("kp_split", t2 - 2)))
            p.wait("vector", rel.get(("kv_mm", t2 - 2)))
            t_kt = p.do("vector", lambda e, kk=k2i: e.tensor_add(
                kp_t[kk][:], P_K[:], bkb[:]), inc="VE")
            rel[("KP_ev", t2)] = t_kt
            p.wait("scalar", t_kt)
            t_kr = p.do("scalar", lambda e, kk=k2i: e.activation(
                u_ln[kk][:], kp_t[kk][:], ACTF.Relu), inc="AE")
            p.wait("vector", t_kr)
            t_ke = p.do("vector", lambda e, kk=k2i: e.tensor_scalar_add(
                kp_t[kk][:], u_ln[kk][:], EPS_AT), inc="VE")
            p.do("vector", lambda e, kk=k2i: e.tensor_copy(kp_hi[kk][:], kp_t[kk][:]))
            t_kl = p.do("vector", lambda e, kk=k2i: e.tensor_sub(
                kp_lo[kk][:], kp_t[kk][:], kp_hi[kk][:]), inc="VE")
            rel[("kp_split", t2)] = t_kl
            rel[("kmm", t2)] = t_kmm

    def emit_kv(G, which):
        b, g = divmod(G, NGRP)
        grp = G
        k2 = 2 * g + which
        k2i = k2 % 2
        t2 = b * NSUB + k2
        p.wait("tensor", rel[("kp_split", t2)])
        p.wait("tensor", rel[("v_split", t2)])
        if b == 1 and k2 == 0:
            p.wait("tensor", rel[("kv_done", 0)])
        for c in range(NCH):
            kvreg = P_kv[:, c * 128:(c + 1) * 128]
            ksreg = P_ks[:, c:c + 1]
            sth = kp_hi[k2i][:, c * 128:(c + 1) * 128]
            stl = kp_lo[k2i][:, c * 128:(c + 1) * 128]
            mvh = v_hi[k2i][:, c * 128:(c + 1) * 128]
            mvl = v_lo[k2i][:, c * 128:(c + 1) * 128]
            kv_start = (k2 == 0 and c % 4 == 0)
            kv_stop = (k2 == NSUB - 1 and c % 4 == 3)
            ks_start = (k2 == 0 and c == 0)
            ks_stop = (k2 == NSUB - 1 and c == NCH - 1)
            p.do("tensor", lambda e, a=sth, m=mvh, r=kvreg, s=kv_start: e.matmul(r, a, m, start=s, stop=False))
            p.do("tensor", lambda e, a=sth, m=mvl, r=kvreg: e.matmul(r, a, m, start=False, stop=False))
            p.do("tensor", lambda e, a=sth, r=ksreg, s=ks_start: e.matmul(r, a, ones128[:], start=s, stop=False))
            p.do("tensor", lambda e, a=stl, m=mvh, r=kvreg, s=kv_stop: e.matmul(r, a, m, start=False, stop=s))
            t_kv = p.do("tensor", lambda e, a=stl, r=ksreg, s=ks_stop: e.matmul(
                r, a, ones128[:], start=False, stop=s),
                inc=("PE" if c == NCH - 1 else None))
        rel[("kv_mm", t2)] = t_kv
        if k2 == NSUB - 1:
            t_kv_last = t_kv
            p.wait("scalar", t_kv_last)
            p.wait("vector", t_kv_last)
            t_ev = None
            for c in range(NCH):
                for (r0, r1, c0, c1) in [(0, 64, 0, 64), (64, 128, 64, 128)]:
                    srcp = P_kv[r0:r1, c * 128 + c0:c * 128 + c1]
                    dh = kv_hi[r0:r1, b * D + c * 128 + c0:b * D + c * 128 + c1]
                    dl = kv_lo[r0:r1, b * D + c * 128 + c0:b * D + c * 128 + c1]
                    t_a = p.do("scalar", lambda e, s=srcp, d=dh: e.activation(d, s, ACTF.Copy), inc="AE")
                    p.wait("vector", t_a)
                    t_ev = p.do("vector", lambda e, s=srcp, d=dl, h=dh: e.tensor_sub(d, s, h), inc="VE")
                base = (b * NCH + c) * 16
                for (r0, r1, off) in [(0, 64, 0), (64, 128, 1)]:
                    srcp = P_ks[r0:r1, c:c + 1]
                    dh = ksb_hi[r0:r1, base + 2 * c + off:base + 2 * c + off + 1]
                    dl = ksb_lo[r0:r1, base + 2 * c + off:base + 2 * c + off + 1]
                    t_a = p.do("scalar", lambda e, s=srcp, d=dh: e.activation(d, s, ACTF.Copy), inc="AE")
                    p.wait("vector", t_a)
                    t_ev = p.do("vector", lambda e, s=srcp, d=dl, h=dh: e.tensor_sub(d, s, h), inc="VE")
            rel[("kv_done", b)] = t_ev

    rel[("xld", 1)] = p.do("gpsimd", lambda e: e.dma_start(
        out=x_tm[1][:], in_=d_x[0, 128:256, :]), inc="LXA1", n=16)
    emit_ln1_pair(0)
    for G in range(NGT):
        if G >= 1:
            emit_vk_projs(G - 1, 0)
        emit_ffn1(G)
        emit_tail_a(G)
        if G >= 1:
            emit_vk_projs(G - 1, 1)
            emit_kv(G - 1, 0)
            emit_kv(G - 1, 1)
        emit_tail_b(G)
        if G + 1 < NGT:
            prefetch_x(2 * G + 2)
            emit_ln1_pair(G + 1)
    emit_vk_projs(NGT - 1, 0)
    emit_vk_projs(NGT - 1, 1)
    emit_kv(NGT - 1, 0)
    emit_kv(NGT - 1, 1)

    a_last = dict(kv=rel[("kv_done", 1)], pe=p.cur("PE"),
                  sv1a=p.cur("SV10"), sv1b=p.cur("SV11"),
                  svxa=p.cur("SVX0"), svxb=p.cur("SVX1"))
    emit_block(nc, p)
    aes.close()

    p = P(val, sems)
    bes = ExitStack()
    x1b = [sb(bes, f"x1b{i}", [128, D], F32) for i in range(2)]
    xlhb = [sb(bes, f"xlhb{i}", [128, NCH * GT], BF16) for i in range(2)]
    xllb = [sb(bes, f"xllb{i}", [128, NCH * GT], BF16) for i in range(2)]
    qp_hi = sb(bes, "qp_hi", [128, NCH * GT], BF16)
    qp_lo = sb(bes, "qp_lo", [128, NCH * GT], BF16)
    tq = [sb(bes, f"tq{i}", [128, GT], F32) for i in range(2)]
    uq = [sb(bes, f"uq{i}", [128, GT], F32) for i in range(2)]
    atmp = [sb(bes, f"atmp{i}", [128, GT], F32) for i in range(2)]
    dnm = sb(bes, "dnm", [16, GT], F32)
    rc0 = sb(bes, "rc0", [16, GT], F32)
    rc1 = sb(bes, "rc1", [16, GT], F32)
    r_hi = sb(bes, "r_hi", [16, GT], BF16)
    r_lo = sb(bes, "r_lo", [16, GT], BF16)
    rb_sb = sb(bes, "rb_sb", [128, 2 * GT], F32)
    at_hi = sb(bes, "at_hi", [128, NCH * GT], BF16)
    at_lo = sb(bes, "at_lo", [128, NCH * GT], BF16)
    o_hi = sb(bes, "o_hi", [128, NCH * GT], BF16)
    o_lo = sb(bes, "o_lo", [128, NCH * GT], BF16)
    oh_tm = [sb(bes, f"oh_tm{i}", [128, D], BF16, is_transpose_dest=True) for i in range(2)]
    ol_tm = [sb(bes, f"ol_tm{i}", [128, D], BF16, is_transpose_dest=True) for i in range(2)]
    x2_tm = [sb(bes, f"x2_tm{i}", [128, D], F32) for i in range(4)]
    sqb = sb(bes, "sqb", [128, D], BF16)
    stB = [dict(nsum=sb(bes, f"Bns{i}", [128, 2], F32), ssum=sb(bes, f"Bss{i}", [128, 2], F32),
                negm=sb(bes, f"Bnm{i}", [128, 2], F32), t1=sb(bes, f"Bt1{i}", [128, 2], F32),
                t2=sb(bes, f"Bt2{i}", [128, 2], F32), rstd=sb(bes, f"Brs{i}", [128, 2], F32))
           for i in range(2)]
    xl3_bf = [sb(bes, "xl3_bf0", [128, D], BF16)]
    xl3_fm = [sb(bes, "xl3_fm0", [128, NCH * GT], BF16, is_transpose_dest=True)]
    xq8b = [sb(bes, f"xq8b{i}", [128, NCH * GT], FP8) for i in range(2)]
    tzb = [sb(bes, f"tzb{i}", [128, GT], BF16) for i in range(2)]
    z2_fm = [sb(bes, "z2_fm0", [128, NCH * GT], BF16)]
    z2_tm = [sb(bes, f"z2_tm{i}", [128, D], BF16, is_transpose_dest=True) for i in range(2)]
    out_tm = [sb(bes, f"out_tm{i}", [128, D], F32) for i in range(2)]

    PBp = [bes.enter_context(nc.psum_tensor(f"PBp{i}", [128, GT], F32)) for i in range(6)]
    P_n = bes.enter_context(nc.psum_tensor("P_n", [16, GT], F32))

    pr = {}

    def stats_pair_b(x_aps, st, t_free_v, t_free_a, t_readys):
        p.wait("vector", t_free_v)
        for i, x_ap in enumerate(x_aps):
            p.do("vector", lambda e, xx=x_ap, ii=i: e.reduce_sum(
                st["nsum"][:, ii:ii + 1], xx, axis=AX, negate=True))
        t_sq = None
        for i, x_ap in enumerate(x_aps):
            if i == 0:
                p.wait("scalar", t_free_a)
            p.wait("scalar", t_readys[i])
            p.drain("scalar")
            t_sq = p.do("scalar", lambda e, xx=x_ap, ii=i: e.activation(
                sqb[:], xx, ACTF.Square, accum_out=st["ssum"][:, ii:ii + 1]), inc="AE")
        n = len(x_aps)
        p.do("vector", lambda e: e.tensor_scalar_mul(st["negm"][:, 0:n], st["nsum"][:, 0:n], 1.0 / D))
        p.wait("vector", t_sq)
        p.do("vector", lambda e: e.tensor_scalar_mul(st["t1"][:, 0:n], st["ssum"][:, 0:n], 1.0 / D))
        p.do("vector", lambda e: e.tensor_mul(st["t2"][:, 0:n], st["negm"][:, 0:n], st["negm"][:, 0:n]))
        t_var = p.do("vector", lambda e: e.tensor_sub(
            st["nsum"][:, 0:n], st["t2"][:, 0:n], st["t1"][:, 0:n]), inc="VE")
        p.wait("scalar", t_var)
        t_s = p.do("scalar", lambda e: e.activation(st["t2"][:, 0:n], st["nsum"][:, 0:n],
                                                    ACTF.Sqrt, bias=epsln[:], scale=-1.0), inc="AE")
        p.wait("vector", t_s)
        p.do("vector", lambda e: e.reciprocal(st["t1"][:, 0:n], st["t2"][:, 0:n]))
        p.do("vector", lambda e: e.tensor_mul(st["rstd"][:, 0:n], st["t2"][:, 0:n], st["t1"][:, 0:n]))
        p.do("vector", lambda e: e.tensor_scalar(st["rstd"][:, 0:n], st["rstd"][:, 0:n], 2.0, -1.0,
                                                 ALU.subtract, ALU.mult))
        p.do("vector", lambda e: e.tensor_mul(st["rstd"][:, 0:n], st["t1"][:, 0:n],
                                              st["rstd"][:, 0:n]), inc="VE")
        return t_sq

    p.wait("gpsimd", a_last["sv1a"])
    p.wait("gpsimd", a_last["sv1b"])
    p.wait("gpsimd", a_last["svxa"])
    p.wait("gpsimd", a_last["svxb"])
    p.wait("gpsimd", a_last["kv"])
    p.wait("gpsimd", a_last["pe"])
    t_wload = None
    for dst, src in [(wa_hi[:], d_swq_hi), (wa_lo[:], d_swq_lo),
                     (wb_hi[:], d_swo_hi), (wb_lo[:], d_swo_lo)]:
        t_wload = p.do("gpsimd", lambda e, d=dst, s=src: e.dma_start(out=d, in_=s), inc="WTW", n=16)

    NGT = NB * NGRP

    def emit_xl_load(G):
        if G >= NGT:
            return
        b, g = divmod(G, NGRP)
        gi = g % 2
        p.wait("gpsimd", pr.get(("xl_used", G - 2)))
        p.do("gpsimd", lambda e, bb=b, gg=g, ii=gi: e.dma_start(
            out=xlhb[ii][:], in_=d_sxlh[bb, gg]), inc=f"LXB{gi}", n=16)
        pr[("xlld", G)] = p.do("gpsimd", lambda e, bb=b, gg=g, ii=gi: e.dma_start(
            out=xllb[ii][:], in_=d_sxll[bb, gg]), inc=f"LXB{gi}", n=16)

    def emit_x1_load(G):
        if G >= NGT:
            return
        b, g = divmod(G, NGRP)
        p.wait("scalar", pr.get(("x1_used", G - 1)))
        for s2 in range(SPG):
            pr[("x1ld", G, s2)] = p.do("scalar", lambda e, bb=b, kk=g * SPG + s2, ii=s2: e.dma_start(
                out=x1b[ii][:], in_=d_sx1[bb, kk]), inc=f"LX1{s2}", n=16)

    def emit_qno(G, tail_G=None):
        b, g = divmod(G, NGRP)
        gi = g % 2
        grp = G
        p.wait("tensor", t_wload)
        p.wait("tensor", pr[("xlld", G)])
        if grp >= 2:
            p.wait("tensor", pr[("ffn2_dq", grp - 2, 4)])
        elif grp == 1:
            p.wait("tensor", pr[("o_hi", grp - 1, NCH - 1)])
        t_qmm = None
        for j in range(NCH):
            reg = PBp[j % 2][:]
            p.wait("tensor", pr.get(("qp_hi", grp, j - 2)))
            if j == 1 and grp >= 2:
                p.wait("tensor", pr[("ffn2_dq", grp - 2, 5)])
            for i in range(NCH):
                sh = wa_hi[:, (i * NCH + j) * 128:(i * NCH + j + 1) * 128]
                sl = wa_lo[:, (i * NCH + j) * 128:(i * NCH + j + 1) * 128]
                mh = ch_view(xlhb[gi][:], i)
                ml = ch_view(xllb[gi][:], i)
                p.do("tensor", lambda e, a=sh, m=mh, r=reg, ii=i: e.matmul(r, a, m, start=(ii == 0), stop=False))
                p.do("tensor", lambda e, a=sh, m=ml, r=reg: e.matmul(r, a, m, start=False, stop=False))
                t_qmm = p.do("tensor", lambda e, a=sl, m=mh, r=reg, ii=i: e.matmul(
                    r, a, m, start=False, stop=(ii == NCH - 1)),
                    inc=("PE" if i == NCH - 1 else None))
            p.wait("scalar", t_qmm)
            p.wait("scalar", pr.get(("tq_used", grp, j - 2)))
            if j < 2:
                p.wait("scalar", pr.get(("to_used", grp - 1, NCH - 2 + j)))
            p.wait("scalar", pr.get(("qkv_mm", grp - 1)))
            t_qr = p.do("scalar", lambda e, jj=j, r=reg: e.activation(
                tq[jj % 2][:], r, ACTF.Relu, bias=bqp_c[:, jj:jj + 1]), inc="AE")
            pr[("qp_hi", grp, j)] = t_qr
            p.wait("vector", t_qr)
            p.wait("vector", pr.get(("uq_used", grp, j - 2)))
            p.wait("vector", pr.get(("qkv_mm", grp - 1)))
            t_qe = p.do("vector", lambda e, jj=j: e.tensor_scalar_add(
                uq[jj % 2][:], tq[jj % 2][:], EPS_AT), inc="VE")
            pr[("tq_used", grp, j)] = t_qe
            p.do("vector", lambda e, jj=j: e.tensor_copy(
                qp_hi[:, jj * GT:(jj + 1) * GT], uq[jj % 2][:]))
            t_ql = p.do("vector", lambda e, jj=j: e.tensor_sub(
                qp_lo[:, jj * GT:(jj + 1) * GT], uq[jj % 2][:],
                qp_hi[:, jj * GT:(jj + 1) * GT]), inc="VE")
            pr[("uq_used", grp, j)] = t_ql
        t_qp = t_ql
        pr[("xl_used", grp)] = t_qmm

        p.wait("tensor", t_qp)
        p.wait("tensor", pr.get(("dnm_read", grp - 1)))
        for c in range(NCH):
            base = (b * NCH + c) * 16
            sh = ksb_hi[:, base:base + 16]
            sl = ksb_lo[:, base:base + 16]
            mh = qp_hi[:, c * GT:(c + 1) * GT]
            ml = qp_lo[:, c * GT:(c + 1) * GT]
            p.do("tensor", lambda e, a=sh, m=mh, cc=c: e.matmul(P_n[:], a, m, start=(cc == 0), stop=False))
            p.do("tensor", lambda e, a=sh, m=ml: e.matmul(P_n[:], a, m, start=False, stop=False))
            t_nmm = p.do("tensor", lambda e, a=sl, m=mh, cc=c: e.matmul(
                P_n[:], a, m, start=False, stop=(cc == NCH - 1)),
                inc=("PE" if c == NCH - 1 else None))

        p.wait("vector", t_nmm)
        t_dn = p.do("vector", lambda e: e.tensor_scalar_add(dnm[:], P_n[:], EPS_AT), inc="VE")
        pr[("dnm_read", grp)] = t_dn
        p.do("vector", lambda e: e.reciprocal(rc0[:], dnm[:]))
        p.drain("vector")
        p.do("vector", lambda e: e.tensor_mul(rc1[:], dnm[:], rc0[:]))
        p.drain("vector")
        p.do("vector", lambda e: e.tensor_scalar(rc1[:], rc1[:], 2.0, -1.0, ALU.subtract, ALU.mult))
        p.drain("vector")
        t_rc = p.do("vector", lambda e: e.tensor_mul(rc0[:], rc0[:], rc1[:]), inc="VE")
        p.wait("scalar", t_rc)
        t_rh = p.do("scalar", lambda e: e.activation(r_hi[:], rc0[:], ACTF.Copy), inc="AE")
        p.wait("vector", t_rh)
        t_rl = p.do("vector", lambda e: e.tensor_sub(r_lo[:], rc0[:], r_hi[:]), inc="VE")
        p.wait("tensor", t_rl)
        for c in range(NCH):
            reg = PBp[4 + c % 2][:]
            p.wait("tensor", pr.get(("rb_ev", grp, c - 2)))
            if c == 4:
                p.wait("tensor", pr[("at_hi", grp, 2)])
            elif c == 6:
                p.wait("tensor", pr[("at_hi", grp, 5)])
            if c == 0:
                p.wait("tensor", pr.get(("at_done", grp - 1, 5)))
                p.wait("tensor", pr.get(("ffn2_dq", grp - 2, 6)))
            elif c == 1:
                p.wait("tensor", pr.get(("rb_ev", grp - 1, NCH - 1)))
                p.wait("tensor", pr.get(("ffn2_dq", grp - 2, 7)))
            p.do("tensor", lambda e, cc=c, r=reg: e.matmul(
                r, masks[:, cc * 128:(cc + 1) * 128], r_hi[:], start=True, stop=False))
            t_bc = p.do("tensor", lambda e, cc=c, r=reg: e.matmul(
                r, masks[:, cc * 128:(cc + 1) * 128], r_lo[:], start=False, stop=True), inc="PE")
            p.wait("vector", t_bc)
            pr[("rb_ev", grp, c)] = p.do("vector", lambda e, cc=c % 2, r=reg: e.tensor_copy(
                rb_sb[:, cc * GT:(cc + 1) * GT], r), inc="VE")
            reg = PBp[2 + c % 3][:]
            p.wait("tensor", pr.get(("at_hi", grp, c - 3)))
            if c == 0:
                p.wait("tensor", pr.get(("at_done", grp - 1, 6)))
            elif c == 1:
                p.wait("tensor", pr.get(("at_done", grp - 1, 7)))
            elif c == 2:
                p.wait("tensor", pr[("rb_ev", grp, 2)])
            elif c == 5:
                p.wait("tensor", pr[("rb_ev", grp, 4)])
            sh = kv_hi[:, b * D + c * 128:b * D + (c + 1) * 128]
            sl = kv_lo[:, b * D + c * 128:b * D + (c + 1) * 128]
            mh = qp_hi[:, c * GT:(c + 1) * GT]
            ml = qp_lo[:, c * GT:(c + 1) * GT]
            p.do("tensor", lambda e, a=sh, m=mh, r=reg: e.matmul(r, a, m, start=True, stop=False))
            p.do("tensor", lambda e, a=sh, m=ml, r=reg: e.matmul(r, a, m, start=False, stop=False))
            t_omm = p.do("tensor", lambda e, a=sl, m=mh, r=reg: e.matmul(r, a, m, start=False, stop=True), inc="PE")
            if c == NCH - 1:
                pr[("qkv_mm", grp)] = t_omm
            p.wait("vector", t_omm)
            p.wait("vector", pr[("rb_ev", grp, c)])
            p.wait("vector", pr.get(("at_split", grp, c - 2)))
            p.wait("vector", pr.get(("o_mm", grp - 1)))
            t_at = p.do("vector", lambda e, cc=c, r=reg: e.tensor_mul(
                atmp[cc % 2][:], r, rb_sb[:, (cc % 2) * GT:(cc % 2 + 1) * GT]), inc="VE")
            pr[("at_hi", grp, c)] = t_at
            p.do("vector", lambda e, cc=c: e.tensor_copy(
                at_hi[:, cc * GT:(cc + 1) * GT], atmp[cc % 2][:]))
            t_al = p.do("vector", lambda e, cc=c: e.tensor_sub(
                at_lo[:, cc * GT:(cc + 1) * GT], atmp[cc % 2][:],
                at_hi[:, cc * GT:(cc + 1) * GT]), inc="VE")
            pr[("at_split", grp, c)] = t_al
            pr[("at_done", grp, c)] = t_al
        t_attn = t_al
        if tail_G is not None:
            tailB2_quant(tail_G, 0)
            tailB2_quant(tail_G, 1)

        p.wait("tensor", t_attn)
        t_omm2 = None
        for j in range(NCH):
            reg = PBp[j % 2][:]
            p.wait("tensor", pr.get(("o_hi", grp, j - 2)))
            if j < 2:
                p.wait("tensor", pr[("qp_hi", grp, NCH - 2 + j)])
            for i in range(NCH):
                sh = wb_hi[:, (i * NCH + j) * 128:(i * NCH + j + 1) * 128]
                sl = wb_lo[:, (i * NCH + j) * 128:(i * NCH + j + 1) * 128]
                mh = at_hi[:, i * GT:(i + 1) * GT]
                ml = at_lo[:, i * GT:(i + 1) * GT]
                p.do("tensor", lambda e, a=sh, m=mh, r=reg, ii=i: e.matmul(r, a, m, start=(ii == 0), stop=False))
                p.do("tensor", lambda e, a=sh, m=ml, r=reg: e.matmul(r, a, m, start=False, stop=False))
                t_omm2 = p.do("tensor", lambda e, a=sl, m=mh, r=reg, ii=i: e.matmul(
                    r, a, m, start=False, stop=(ii == NCH - 1)),
                    inc=("PE" if i == NCH - 1 else None))
            if j == NCH - 1:
                pr[("o_mm", grp)] = t_omm2
            p.wait("scalar", t_omm2)
            p.wait("scalar", pr.get(("to_used", grp, j - 2)))
            if j < 2:
                p.wait("scalar", pr[("tq_used", grp, NCH - 2 + j)])
            p.wait("scalar", pr.get(("o_tp0", grp - 1)))
            p.wait("scalar", pr.get(("o_tp1", grp - 1)))
            t_to = p.do("scalar", lambda e, jj=j, r=reg: e.activation(
                tq[jj % 2][:], r, ACTF.Identity, bias=bo_sb[:, jj:jj + 1]), inc="AE")
            pr[("o_hi", grp, j)] = t_to
            p.wait("vector", t_to)
            p.wait("vector", pr.get(("o_tp0", grp - 1)))
            p.wait("vector", pr.get(("o_tp1", grp - 1)))
            p.do("vector", lambda e, jj=j: e.tensor_copy(
                ch_view(o_hi[:], jj), tq[jj % 2][:].rearrange("p (s f) -> p s f", s=SPG)))
            t_ol = p.do("vector", lambda e, jj=j: e.tensor_sub(
                ch_view(o_lo[:], jj), tq[jj % 2][:].rearrange("p (s f) -> p s f", s=SPG),
                ch_view(o_hi[:], jj)), inc="VE")
            pr[("to_used", grp, j)] = t_ol
            if j == 3:
                emit_otp_half(grp, 0)
        pr[("o_ev", grp)] = t_ol

    def emit_otp_half(G, half):
        b, g = divmod(G, NGRP)
        grp = G
        lo, hi = (0, NCH // 2) if half == 0 else (NCH // 2, NCH)
        t_oev = pr[("to_used", grp, 3)] if half == 0 else pr[("o_ev", grp)]
        t_otp = None
        for s2 in range(SPG):
            k2 = g * SPG + s2
            t2 = b * NSUB + k2
            p.wait("sync", t_oev)
            if half == 0:
                p.wait("sync", pr.get(("x2A", t2 - 2)))
            else:
                p.wait("sync", pr.get(("x2_done", t2 - 2)))
            sem = f"TPO{s2}" if half else f"TPOA{s2}"
            p.do("sync", lambda e, ss=s2, kk=s2, l=lo, h=hi: e.dma_start(
                out=chunk3(oh_tm[kk][:], 0)[:, l:h], in_=chunk3(o_hi[:], ss, stride=GT)[:, l:h],
                transpose=True), inc=sem, n=16)
            t_otp = p.do("sync", lambda e, ss=s2, kk=s2, l=lo, h=hi: e.dma_start(
                out=chunk3(ol_tm[kk][:], 0)[:, l:h], in_=chunk3(o_lo[:], ss, stride=GT)[:, l:h],
                transpose=True), inc=sem, n=16)
            pr[(("o_tpA", t2) if half == 0 else ("o_tpt", t2))] = t_otp

    def emit_tailB1(G):
        emit_otp_half(G, 1)

    tb2 = {}

    def tailB2_x2h(G, half):
        b, g = divmod(G, NGRP)
        grp = G
        x2s = [x2_tm[s2 + 2 * (G % 2)] for s2 in range(SPG)]
        c0, c1 = (0, D // 2) if half == 0 else (D // 2, D)
        t_x2s = []
        for s2 in range(SPG):
            k2 = g * SPG + s2
            t2 = b * NSUB + k2
            if half == 0:
                p.wait("vector", pr[("o_tpA", t2)])
                p.wait("vector", pr[("x1ld", G, s2)])
                p.wait("vector", pr.get(("x3_done", t2 - 4)))
                p.wait("vector", pr.get(("stB_free_a", grp - 1)))
            else:
                p.wait("vector", pr[("o_tpt", t2)])
            p.do("vector", lambda e, kk=s2, xv=x2s[s2]: e.tensor_add(
                xv[:, c0:c1], x1b[kk][:, c0:c1], oh_tm[kk][:, c0:c1]))
            p.drain("vector")
            t_x2 = p.do("vector", lambda e, kk=s2, xv=x2s[s2]: e.tensor_add(
                xv[:, c0:c1], xv[:, c0:c1], ol_tm[kk][:, c0:c1]), inc="VE")
            if half == 0:
                pr[("x2A", t2)] = t_x2
            else:
                pr[("x2_done", t2)] = t_x2
                t_x2s.append(t_x2)
                if s2 == SPG - 1:
                    pr[("x1_used", grp)] = t_x2
        if half == 1:
            tb2[G] = (x2s, t_x2s)

    def tailB2_stats(G):
        g = G % NGRP
        x2s, t_x2s = tb2[G]
        pr[("stB_free_a", G)] = stats_pair_b(
            [x2s[0][:], x2s[1][:]], stB[g % 2],
            pr.get(("stB_free", G - 2)), pr.get(("stB_free_a", G - 2)), t_x2s)

    def tailB2_apply(G, s2):
        b, g = divmod(G, NGRP)
        gi = g % 2
        x2s, _ = tb2[G]
        k2 = g * SPG + s2
        t2 = b * NSUB + k2
        p.wait("vector", pr.get(("xl3_tp", t2 - 1)))
        t_x3 = p.do("vector", lambda e, ii=s2, xv=x2s[s2]: e.tensor_scalar(
            xl3_bf[0][:], xv[:], stB[gi]["negm"][:, ii:ii + 1],
            stB[gi]["rstd"][:, ii:ii + 1], ALU.add, ALU.mult), inc="VE")
        if s2 == SPG - 1:
            pr[("stB_free", G)] = t_x3
        p.wait("sync", t_x3)
        p.wait("sync", pr.get(("xq8b_done", G - 1)))
        t_t3 = p.do("sync", lambda e, ss=s2: e.dma_start(
            out=chunk3(xl3_fm[0][:], ss, stride=GT),
            in_=chunk3(xl3_bf[0][:], 0), transpose=True), inc=f"TP3{s2}", n=16)
        pr[("xl3_tp", t2)] = t_t3

    def tailB2_quant(G, s2):
        b, g = divmod(G, NGRP)
        t2 = b * NSUB + g * SPG + s2
        p.wait("scalar", pr[("xl3_tp", t2)])
        p.wait("scalar", pr.get(("ffn2_mm", G - 2)))
        t_q3 = p.do("scalar", lambda e, ss=s2, gg=G % 2: e.activation(
            chunk3cm(xq8b[gg][:], ss), chunk3(xl3_fm[0][:], ss, stride=GT),
            ACTF.Copy, scale=rsin[:]), inc="AE")
        if s2 == SPG - 1:
            pr[("xq8b_done", G)] = t_q3

    def emit_tailB2(G):
        tailB2_x2h(G, 0)
        tailB2_x2h(G, 1)
        tailB2_stats(G)
        tailB2_apply(G, 0)
        tailB2_apply(G, 1)

    def emit_ffn2(G, tail_G=None):
        grp = G
        p.wait("tensor", pr[("xq8b_done", grp)])
        p.wait("tensor", pr[("o_hi", min(grp + 1, NGT - 1), NCH - 1)])
        p.wait("tensor", pr.get(("ffn2_dq_last", grp - 1)))
        t_zmm = None
        t_z2 = None
        if tail_G is not None:
            tailB2_x2h(tail_G, 0)
        for j in range(NCH):
            reg = PBp[j % 2][:] if j < 6 else PBp[j - 2][:]
            if j < 6:
                p.wait("tensor", pr.get(("ffn2_dq", grp, j - 2)))
            else:
                p.wait("tensor", pr.get(("rb_ev", grp + 1, j)))
            for i in range(NCH // 2):
                t_zmm = p.do("tensor", lambda e, ii=i, jj=j, gg=G % 2, r=reg: e.matmul(
                    r, kq8[:, (ii * NCH + jj) * 256:(ii * NCH + jj) * 256 + 256]
                        .rearrange("p (i m) -> p i m", i=2),
                    xq8b[gg][:, ii * 512:(ii + 1) * 512].rearrange("p (i n) -> p i n", i=2),
                    start=(ii == 0), stop=(ii == NCH // 2 - 1), perf_mode=DR),
                    inc=("PE" if i == NCH // 2 - 1 else None))
            p.wait("scalar", t_zmm)
            p.wait("scalar", pr.get(("tzb_used", grp, j - 2)))
            if j < 2:
                p.wait("scalar", pr.get(("tzb_used", grp - 1, NCH - 2 + j)))
            t_dq = p.do("scalar", lambda e, jj=j, r=reg: e.activation(
                tzb[jj % 2][:], r, ACTF.Copy, scale=sprod[:]), inc="AE")
            pr[("ffn2_dq", grp, j)] = t_dq
            if j == NCH - 1:
                pr[("ffn2_dq_last", grp)] = t_dq
            p.wait("vector", t_dq)
            p.wait("vector", pr.get(("z2_tp0", grp - 1)))
            p.wait("vector", pr.get(("z2_tp1", grp - 1)))
            t_z2 = p.do("vector", lambda e, jj=j, gg=0: e.tensor_scalar(
                ch_view(z2_fm[gg][:], jj), tzb[jj % 2][:].rearrange("p (s f) -> p s f", s=SPG),
                fb32_c[:, jj:jj + 1], 0.5, ALU.add, ALU.mult), inc="VE")
            pr[("tzb_used", grp, j)] = t_z2
            if tail_G is not None and j == 1:
                tailB2_x2h(tail_G, 1)
        pr[("ffn2_mm", grp)] = t_zmm
        pr[("ffn2_zv", grp)] = t_z2
        if tail_G is not None:
            tailB2_stats(tail_G)
            tailB2_apply(tail_G, 0)
            tailB2_apply(tail_G, 1)

    def emit_f2tail(G):
        b, g = divmod(G, NGRP)
        grp = G
        t_ffn2 = pr[("ffn2_zv", grp)]
        t_z2tp = None
        for s2 in range(SPG):
            k2 = g * SPG + s2
            t2 = b * NSUB + k2
            p.wait("sync", t_ffn2)
            p.wait("sync", pr.get(("out_done", t2 - 2)))
            t_z2tp = p.do("sync", lambda e, ss=s2, kk=s2, gg=0: e.dma_start(
                out=chunk3(z2_tm[kk][:], 0), in_=chunk3(z2_fm[gg][:], ss, stride=GT),
                transpose=True), inc=f"TPZ2{s2}", n=16)
            pr[(f"z2_tp{s2}", grp)] = t_z2tp
            p.wait("vector", t_z2tp)
            p.wait("vector", pr.get(("out_st", t2 - 2)))
            t_out = p.do("vector", lambda e, kk=s2, xv=x2_tm[s2 + 2 * (G % 2)]: e.tensor_add(
                out_tm[kk][:], xv[:], z2_tm[kk][:]), inc="VE")
            pr[("out_done", t2)] = t_out
            pr[("x3_done", t2)] = t_out
            p.wait("gpsimd", t_out)
            pr[("out_st", t2)] = p.do("gpsimd", lambda e, bb=b, kk=k2, kki=s2: e.dma_start(
                out=d_out[bb, kk * 128:kk * 128 + 128, :], in_=out_tm[kki][:]), inc=f"SVO{s2}", n=16)

    emit_xl_load(0)
    emit_x1_load(0)
    for G in range(NGT):
        if G >= 2:
            emit_ffn2(G - 2, tail_G=G - 1)
        elif G >= 1:
            emit_tailB2(G - 1)
        emit_qno(G, tail_G=(G - 1 if G >= 1 else None))
        emit_xl_load(G + 1)
        if G >= 2:
            emit_f2tail(G - 2)
        emit_tailB1(G)
        if G >= 1:
            emit_x1_load(G)
    emit_ffn2(NGT - 2, tail_G=NGT - 1)
    tailB2_quant(NGT - 1, 0)
    tailB2_quant(NGT - 1, 1)
    emit_f2tail(NGT - 2)
    emit_ffn2(NGT - 1)
    emit_f2tail(NGT - 1)

    p.wait("gpsimd", p.cur("SVO0"))
    p.wait("gpsimd", p.cur("SVO1"))
    emit_block(nc, p)
    bes.close()
    es.close()
    return nc


def host_inputs(core_idx, inputs):
    x = np.ascontiguousarray(np.asarray(inputs["L0_attn_x"])[2 * core_idx:2 * core_idx + 2])
    wq = np.asarray(inputs["L0_attn_Wq"])
    wk = np.asarray(inputs["L0_attn_Wk"])
    rw = np.asarray(inputs["L0_attn_rw"])
    rwbd = np.zeros((NCH, 128, 128), dtype=np.float32)
    for h in range(16):
        c, r = divmod(h, 2)
        rwbd[c, r * 64:r * 64 + 64, r * 64:r * 64 + 64] = rw[h]

    def col8(v):
        return np.ascontiguousarray(np.asarray(v).reshape(NCH, 128).T)

    masks = np.zeros((16, NCH * 128), dtype=ml_dtypes.bfloat16)
    for c in range(NCH):
        masks[2 * c, c * 128:c * 128 + 64] = 1
        masks[2 * c + 1, c * 128 + 64:c * 128 + 128] = 1
    return {
        "x": x,
        "wqT": np.ascontiguousarray(wq.T),
        "wkT": np.ascontiguousarray(wk.T),
        "wv": np.ascontiguousarray(np.asarray(inputs["L0_attn_Wv"])),
        "wo": np.ascontiguousarray(np.asarray(inputs["L0_attn_Wo"])),
        "rwbd": rwbd,
        "bq_c": col8(inputs["L0_attn_bq"]),
        "bk_c": col8(inputs["L0_attn_bk"]),
        "bv_row": np.ascontiguousarray(np.asarray(inputs["L0_attn_bv"]).reshape(1, D)),
        "bo_c": col8(inputs["L0_attn_bo"]),
        "fk": np.ascontiguousarray(
            np.asarray(inputs["L0_ffn_kernel"]).reshape(4, 2, 128, 8, 128)
            .transpose(2, 0, 3, 1, 4).reshape(128, 64 * 128)),
        "fb_c": np.ascontiguousarray(np.asarray(inputs["L0_ffn_bias"]).reshape(NCH, 128).T),
        "s_in": np.asarray(inputs["L0_ffn_input_scale"], dtype=np.float32).reshape(1, 1),
        "s_k": np.asarray(inputs["L0_ffn_kernel_scale"], dtype=np.float32).reshape(1, 1),
        "c_ones128": np.ones((128, 1), dtype=ml_dtypes.bfloat16),
        "c_onesrow": np.ones((1, 128), dtype=ml_dtypes.bfloat16),
        "c_masks": masks,
    }


def kernel(**inputs):
    nc = build_core_program()
    in_maps = [host_inputs(c, inputs) for c in range(NCORES)]
    res = run_bass_kernel_spmd(nc, in_maps, core_ids=list(range(NCORES)))
    out = np.concatenate([r["out"] for r in res.results], axis=0)
    return np.ascontiguousarray(out)


# version 26 (final)
# speedup vs baseline: 1.0623x; 1.0315x over previous
import numpy as np
import ml_dtypes
from contextlib import ExitStack

import concourse.bass as bass
import concourse.mybir as mybir
from concourse.bass_utils import run_bass_kernel_spmd

F32 = mybir.dt.float32
BF16 = mybir.dt.bfloat16
FP8 = mybir.dt.float8e4
ALU = mybir.AluOpType
ACTF = mybir.ActivationFunctionType
AX = mybir.AxisListType.X

NCORES = 8
NB = 2
S = 4096
D = 1024
ST = 128
NSUB = S // ST
GT = 256
SPG = GT // ST
NGRP = S // GT
NCH = 8
EPS_LN = 1e-5
EPS_AT = 1e-6

ENGINES = ("sync", "vector", "scalar", "tensor", "gpsimd")
DR = mybir.MatmulPerfMode.DoubleRow


class P:

    AUTODRAIN = ("vector", "scalar")

    def __init__(self, val, sems):
        self.stream = {e: [] for e in ENGINES}
        self.val = val
        self.sems = sems
        self._last_drained = {e: True for e in ENGINES}

    def raw(self, eng, fn):
        self.stream[eng].append(fn)

    def wait(self, eng, tok):
        if tok and tok[1] > 0:
            h = self.sems[tok[0]]
            self.raw(eng, lambda e, s=h, v=tok[1]: e.wait_ge(s, v))

    def do(self, eng, fn, inc=None, n=1):
        if eng in self.AUTODRAIN:
            if not self._last_drained[eng]:
                self.raw(eng, lambda e: e.drain())
            self._last_drained[eng] = False
        if inc is None:
            self.raw(eng, fn)
            return None
        h = self.sems[inc]
        self.val[inc] = self.val.get(inc, 0) + n
        v = self.val[inc]
        self.raw(eng, lambda e, f=fn, hh=h, nn=n: f(e).then_inc(hh, nn))
        return (inc, v)

    def drain(self, eng):
        if not self._last_drained.get(eng, False):
            self.raw(eng, lambda e: e.drain())
            self._last_drained[eng] = True

    def cur(self, name):
        return (name, self.val.get(name, 0))


def emit_block(nc, p):
    with nc.Block() as blk:
        for eng in ENGINES:
            fns = p.stream[eng]
            if fns:
                getattr(blk, eng)(lambda e, fns=fns: [f(e) for f in fns])


def chunk3(ap_tile, sub, stride=ST):
    nsub = stride // ST
    v = ap_tile.rearrange("p (s c f) -> p s c f", s=nsub, c=NCH)
    return v[:, sub]


def chunk3cm(ap_tile, sub, nsub=SPG):
    v = ap_tile.rearrange("p (c s f) -> p s c f", c=NCH, s=nsub)
    return v[:, sub]


def ch_view(ap_tile, i, nsub=SPG):
    v = ap_tile.rearrange("p (s c f) -> p c s f", s=nsub, c=NCH)
    return v[:, i]


def build_core_program():
    nc = bass.Bass("TRN2", target_bir_lowering=False)

    d_x = nc.dram_tensor("x", [NB, S, D], F32, kind="ExternalInput").ap()
    d_wqT = nc.dram_tensor("wqT", [D, D], F32, kind="ExternalInput").ap()
    d_wkT = nc.dram_tensor("wkT", [D, D], F32, kind="ExternalInput").ap()
    d_wv = nc.dram_tensor("wv", [D, D], F32, kind="ExternalInput").ap()
    d_wo = nc.dram_tensor("wo", [D, D], F32, kind="ExternalInput").ap()
    d_rw = nc.dram_tensor("rwbd", [NCH, 128, 128], F32, kind="ExternalInput").ap()
    d_bq = nc.dram_tensor("bq_c", [128, NCH], F32, kind="ExternalInput").ap()
    d_bk = nc.dram_tensor("bk_c", [128, NCH], F32, kind="ExternalInput").ap()
    d_bvr = nc.dram_tensor("bv_row", [1, D], F32, kind="ExternalInput").ap()
    d_bo = nc.dram_tensor("bo_c", [128, NCH], F32, kind="ExternalInput").ap()
    d_fk = nc.dram_tensor("fk", [128, 64 * 128], BF16, kind="ExternalInput").ap()
    d_fb = nc.dram_tensor("fb_c", [128, NCH], BF16, kind="ExternalInput").ap()
    d_sin = nc.dram_tensor("s_in", [1, 1], F32, kind="ExternalInput").ap()
    d_sk = nc.dram_tensor("s_k", [1, 1], F32, kind="ExternalInput").ap()
    d_ones128 = nc.dram_tensor("c_ones128", [128, 1], BF16, kind="ExternalInput").ap()
    d_onesrow = nc.dram_tensor("c_onesrow", [1, 128], BF16, kind="ExternalInput").ap()
    d_masks = nc.dram_tensor("c_masks", [16, NCH * 128], BF16, kind="ExternalInput").ap()

    d_out = nc.dram_tensor("out", [NB, S, D], F32, kind="ExternalOutput").ap()
    d_swq_hi = nc.dram_tensor("swq_hi", [128, 64 * 128], BF16, kind="Internal").ap()
    d_swq_lo = nc.dram_tensor("swq_lo", [128, 64 * 128], BF16, kind="Internal").ap()
    d_swo_hi = nc.dram_tensor("swo_hi", [128, 64 * 128], BF16, kind="Internal").ap()
    d_swo_lo = nc.dram_tensor("swo_lo", [128, 64 * 128], BF16, kind="Internal").ap()
    d_sx1 = nc.dram_tensor("sx1", [NB, NSUB, 128, D], F32, kind="Internal").ap()
    d_sxlh = nc.dram_tensor("sxlh", [NB, NGRP, 128, NCH * GT], BF16, kind="Internal").ap()
    d_sxll = nc.dram_tensor("sxll", [NB, NGRP, 128, NCH * GT], BF16, kind="Internal").ap()

    es = ExitStack()
    sems = {n: es.enter_context(nc.semaphore(n))
            for n in ("WT", "WTS", "WTB", "WTW",
                      "LXA0", "LXA1", "LXB0", "LXB1", "LX10", "LX11",
                      "TPQ0", "TPQ1", "TPZ0", "TPZ1", "TPL0", "TPL1",
                      "TPO0", "TPO1", "TPOA0", "TPOA1", "TP30", "TP31", "TPZ20", "TPZ21",
                      "SVW", "SV10", "SV11", "SVX0", "SVX1", "SVO0", "SVO1",
                      "VE", "AE", "PE", "GE")}
    val = {}

    def sb(stack, name, shape, dtype, **kw):
        return stack.enter_context(nc.sbuf_tensor(name, shape, dtype, **kw))

    wa_hi = sb(es, "wa_hi", [128, 64 * 128], BF16)
    wa_lo = sb(es, "wa_lo", [128, 64 * 128], BF16)
    wb_hi = sb(es, "wb_hi", [128, 64 * 128], BF16)
    wb_lo = sb(es, "wb_lo", [128, 64 * 128], BF16)
    kq8 = sb(es, "kq8", [128, 64 * 128], FP8)
    bqp_c = sb(es, "bqp_c", [128, NCH], F32)
    bo_sb = sb(es, "bo_sb", [128, NCH], F32)
    fb32_c = sb(es, "fb32_c", [128, NCH], F32)
    bkb = sb(es, "bkb", [128, D], F32)
    bvb = sb(es, "bvb", [128, D], F32)
    epsln = sb(es, "epsln", [128, 1], F32)
    rsin = sb(es, "rsin", [128, 1], F32)
    rsk = sb(es, "rsk", [128, 1], F32)
    sprod = sb(es, "sprod", [128, 1], F32)
    ones128 = sb(es, "ones128", [128, 1], BF16)
    onesrow = sb(es, "onesrow", [1, 128], BF16)
    masks = sb(es, "masks", [16, NCH * 128], BF16)
    kv_hi = sb(es, "kv_hi", [128, NB * D], BF16)
    kv_lo = sb(es, "kv_lo", [128, NB * D], BF16)
    ksb_hi = sb(es, "ksb_hi", [128, NB * NCH * 16], BF16)
    ksb_lo = sb(es, "ksb_lo", [128, NB * NCH * 16], BF16)

    p = P(val, sems)
    pes = ExitStack()
    pst = sb(pes, "pst", [128, 64 * 128], F32)
    psh = sb(pes, "psh", [128, 64 * 128], BF16)
    psl = sb(pes, "psl", [128, 64 * 128], BF16)
    wout_hi = sb(pes, "wout_hi", [128, 64 * 128], BF16)
    wout_lo = sb(pes, "wout_lo", [128, 64 * 128], BF16)
    rwf = sb(pes, "rwf", [128, NCH * 128], F32)
    rw_hi = sb(pes, "rw_hi", [128, NCH * 128], BF16)
    rw_lo = sb(pes, "rw_lo", [128, NCH * 128], BF16)
    bstg = sb(pes, "bstg", [128, NCH], F32)
    bshi = sb(pes, "bshi", [128, NCH], BF16)
    bslo = sb(pes, "bslo", [128, NCH], BF16)
    rowf = sb(pes, "rowf", [1, D], F32)
    rowh = sb(pes, "rowh", [1, D], BF16)
    rowl = sb(pes, "rowl", [1, D], BF16)
    sc_f = sb(pes, "sc_f", [1, 1], F32)
    sc_h = sb(pes, "sc_h", [1, 1], BF16)
    sc_l = sb(pes, "sc_l", [1, 1], BF16)
    t128a = sb(pes, "t128a", [128, 1], F32)
    t128b = sb(pes, "t128b", [128, 1], F32)
    pp_w = pes.enter_context(nc.psum_tensor("pp_w", [128, 128], F32))
    pp_w2 = pes.enter_context(nc.psum_tensor("pp_w2", [128, 128], F32))
    pp_r = pes.enter_context(nc.psum_tensor("pp_r", [1, D], F32))
    pp_b = pes.enter_context(nc.psum_tensor("pp_b", [128, 512], F32))
    pp_c = pes.enter_context(nc.psum_tensor("pp_c", [128, NCH], F32))

    t_ld0 = None
    for dst, src in [
        (ones128[:], d_ones128), (onesrow[:], d_onesrow), (masks[:], d_masks),
        (rwf[:].rearrange("p (c f) -> p c f", c=NCH), d_rw.rearrange("c r f -> r c f")),
        (bo_sb[:], d_bo), (sc_f[:], d_sin),
    ]:
        t_ld0 = p.do("gpsimd", lambda e, d=dst, s=src: e.dma_start(out=d, in_=s), inc="WT", n=16)
    t_ldfb = p.do("gpsimd", lambda e: e.dma_start(out=bshi[:], in_=d_fb), inc="WT", n=16)

    p.wait("vector", t_ldfb)
    p.do("vector", lambda e: e.tensor_copy(rw_hi[:], rwf[:]))
    p.drain("vector")
    t_rw = p.do("vector", lambda e: e.tensor_sub(rw_lo[:], rwf[:], rw_hi[:]), inc="VE")
    p.wait("vector", t_ldfb)
    t_fb32 = p.do("vector", lambda e: e.tensor_copy(fb32_c[:], bshi[:]), inc="VE")

    def bcast_scalar(dst128, t_loaded):
        p.wait("vector", t_loaded)
        p.do("vector", lambda e: e.tensor_copy(sc_h[:], sc_f[:]))
        p.drain("vector")
        t = p.do("vector", lambda e: e.tensor_sub(sc_l[:], sc_f[:], sc_h[:]), inc="VE")
        p.wait("tensor", t)
        p.do("tensor", lambda e: e.matmul(pp_w[:, 0:1], onesrow[:], sc_h[:], start=True, stop=False))
        t = p.do("tensor", lambda e: e.matmul(pp_w[:, 0:1], onesrow[:], sc_l[:], start=False, stop=True), inc="PE")
        p.wait("vector", t)
        return p.do("vector", lambda e: e.tensor_copy(dst128[:], pp_w[:, 0:1]), inc="VE")

    t_sa = bcast_scalar(t128b, t_ldfb)
    p.drain("vector")
    p.do("vector", lambda e: e.tensor_copy(rsin[:], t128b[:]))
    p.drain("vector")
    p.wait("gpsimd", t_sa)
    t_ldk = p.do("gpsimd", lambda e: e.dma_start(out=sc_f[:], in_=d_sk), inc="WTS", n=16)
    t_sb2 = bcast_scalar(t128b, t_ldk)
    p.drain("vector")
    p.do("vector", lambda e: e.tensor_copy(rsk[:], t128b[:]))
    p.drain("vector")
    p.do("vector", lambda e: e.tensor_mul(sprod[:], rsin[:], rsk[:]))
    p.drain("vector")

    def recip_nr(dst_and_src):
        d = dst_and_src
        p.do("vector", lambda e: e.reciprocal(t128a[:], d[:]))
        p.drain("vector")
        p.do("vector", lambda e: e.tensor_mul(t128b[:], d[:], t128a[:]))
        p.drain("vector")
        p.do("vector", lambda e: e.tensor_scalar(t128b[:], t128b[:], 2.0, -1.0, ALU.subtract, ALU.mult))
        p.drain("vector")
        return p.do("vector", lambda e: e.tensor_mul(d[:], t128a[:], t128b[:]), inc="VE")

    recip_nr(rsin)
    t_rsk = recip_nr(rsk)

    p.do("vector", lambda e: e.memset(epsln[:], EPS_LN))
    p.do("vector", lambda e: e.memzero(kv_hi[:]))
    p.do("vector", lambda e: e.memzero(kv_lo[:]))
    p.do("vector", lambda e: e.memzero(ksb_hi[:]))
    t_zero = p.do("vector", lambda e: e.memzero(ksb_lo[:]), inc="VE")

    def load_split_w(dsrc, pre_chunked, t_pe_free, t_sv_free, out_hi=psh, out_lo=psl):
        p.wait("gpsimd", t_pe_free)
        p.wait("gpsimd", t_sv_free)
        if pre_chunked:
            t = p.do("gpsimd", lambda e: e.dma_start(
                out=pst[:].rearrange("p (i j f) -> p i j f", i=NCH, j=NCH),
                in_=dsrc.rearrange("(i p) (j f) -> p i j f", p=128, f=128)), inc="WTW", n=16)
        else:
            t = p.do("gpsimd", lambda e: e.dma_start(
                out=pst[:].rearrange("p (i e) -> p i e", i=NCH),
                in_=dsrc.rearrange("(i p) e -> p i e", p=128)), inc="WTW", n=16)
        p.wait("vector", t)
        p.wait("vector", t_pe_free)
        p.do("vector", lambda e, o=out_hi: e.tensor_copy(o[:], pst[:]))
        p.drain("vector")
        return p.do("vector", lambda e, o1=out_lo, o2=out_hi: e.tensor_sub(o1[:], pst[:], o2[:]), inc="VE")

    def fold_rw(t_split, dst_hi, dst_lo, t_dst_free):
        p.wait("tensor", t_split)
        p.wait("tensor", t_rw)
        p.wait("scalar", t_dst_free)
        p.wait("vector", t_dst_free)
        t_lo = {}
        t_mm = None
        for n, (ii, jj) in enumerate([(a, b) for a in range(NCH) for b in range(NCH)]):
            pw = pp_w if n % 2 == 0 else pp_w2
            sl_hi = psh[:, (jj * NCH + ii) * 128:(jj * NCH + ii + 1) * 128]
            sl_lo = psl[:, (jj * NCH + ii) * 128:(jj * NCH + ii + 1) * 128]
            r_hi = rw_hi[:, jj * 128:(jj + 1) * 128]
            r_lo = rw_lo[:, jj * 128:(jj + 1) * 128]
            p.wait("tensor", t_lo.get(n - 2))
            p.do("tensor", lambda e, a=sl_hi, b=r_hi, w=pw: e.matmul(w[:], a, b, start=True, stop=False))
            p.do("tensor", lambda e, a=sl_hi, b=r_lo, w=pw: e.matmul(w[:], a, b, start=False, stop=False))
            t_mm = p.do("tensor", lambda e, a=sl_lo, b=r_hi, w=pw: e.matmul(w[:], a, b, start=False, stop=True), inc="PE")
            o_hi = dst_hi[:, (ii * NCH + jj) * 128:(ii * NCH + jj + 1) * 128]
            o_lo = dst_lo[:, (ii * NCH + jj) * 128:(ii * NCH + jj + 1) * 128]
            p.wait("scalar", t_mm)
            t_hi = p.do("scalar", lambda e, o=o_hi, w=pw: e.activation(o, w[:], ACTF.Copy), inc="AE")
            p.wait("vector", t_hi)
            t_lo[n] = p.do("vector", lambda e, o=o_lo, h=o_hi, w=pw: e.tensor_sub(o, w[:], h), inc="VE")
        return t_lo[63], t_mm

    def bias_fold(t_bload, dst_c):
        p.wait("vector", t_bload)
        p.do("vector", lambda e: e.tensor_copy(bshi[:], bstg[:]))
        p.drain("vector")
        t_b = p.do("vector", lambda e: e.tensor_sub(bslo[:], bstg[:], bshi[:]), inc="VE")
        p.wait("tensor", t_b)
        t_mm = None
        for i in range(NCH):
            r_hi = rw_hi[:, i * 128:(i + 1) * 128]
            r_lo = rw_lo[:, i * 128:(i + 1) * 128]
            p.do("tensor", lambda e, a=r_hi, b=bshi[:, i:i + 1], c=i, s=(i == 0): e.matmul(
                pp_c[:, c:c + 1], a, b, start=s, stop=False))
            p.do("tensor", lambda e, a=r_hi, b=bslo[:, i:i + 1], c=i: e.matmul(
                pp_c[:, c:c + 1], a, b, start=False, stop=False))
            t_mm = p.do("tensor", lambda e, a=r_lo, b=bshi[:, i:i + 1], c=i, s=(i == NCH - 1): e.matmul(
                pp_c[:, c:c + 1], a, b, start=False, stop=s),
                inc=("PE" if i == NCH - 1 else None))
        p.wait("vector", t_mm)
        return p.do("vector", lambda e: e.tensor_copy(dst_c[:], pp_c[:]), inc="VE")

    def bcast_row(t_ready, dst):
        p.wait("tensor", t_ready)
        tt = None
        for h in range(2):
            p.wait("tensor", tt)
            p.do("tensor", lambda e, hh=h: e.matmul(
                pp_b[:], onesrow[:], rowh[0:1, hh * 512:hh * 512 + 512], start=True, stop=False))
            t = p.do("tensor", lambda e, hh=h: e.matmul(
                pp_b[:], onesrow[:], rowl[0:1, hh * 512:hh * 512 + 512], start=False, stop=True), inc="PE")
            p.wait("vector", t)
            tt = p.do("vector", lambda e, hh=h: e.tensor_copy(dst[:, hh * 512:hh * 512 + 512], pp_b[:]), inc="VE")
        return tt

    t_sp = load_split_w(d_wqT, True, None, None)
    t_wq_ev, t_wq_mm = fold_rw(t_sp, wout_hi, wout_lo, None)
    p.wait("gpsimd", t_wq_ev)
    p.do("gpsimd", lambda e: e.dma_start(out=d_swq_hi, in_=wout_hi[:]), inc="SVW", n=16)
    t_wq_st = p.do("gpsimd", lambda e: e.dma_start(out=d_swq_lo, in_=wout_lo[:]), inc="SVW", n=16)

    t_ldbq = p.do("gpsimd", lambda e: e.dma_start(out=bstg[:], in_=d_bq), inc="WTB", n=16)
    t_bqp = bias_fold(t_ldbq, bqp_c)

    t_sp2 = load_split_w(d_wo, True, t_wq_mm, None)
    p.wait("gpsimd", t_sp2)
    p.do("gpsimd", lambda e: e.dma_start(out=d_swo_hi, in_=psh[:]), inc="SVW", n=16)
    t_wo_st = p.do("gpsimd", lambda e: e.dma_start(out=d_swo_lo, in_=psl[:]), inc="SVW", n=16)

    t_sp3 = load_split_w(d_wkT, True, None, t_wo_st)
    t_wk_ev, t_wk_mm = fold_rw(t_sp3, wa_hi, wa_lo, None)

    p.wait("gpsimd", t_bqp)
    t_ldbk = p.do("gpsimd", lambda e: e.dma_start(out=bstg[:], in_=d_bk), inc="WTB", n=16)
    p.wait("vector", t_ldbk)
    p.do("vector", lambda e: e.tensor_copy(bshi[:], bstg[:]))
    p.drain("vector")
    t_bk = p.do("vector", lambda e: e.tensor_sub(bslo[:], bstg[:], bshi[:]), inc="VE")
    p.wait("tensor", t_bk)
    t_mm = None
    for i in range(NCH):
        r_hi = rw_hi[:, i * 128:(i + 1) * 128]
        r_lo = rw_lo[:, i * 128:(i + 1) * 128]
        slr = pp_r[0:1, i * 128:(i + 1) * 128]
        p.do("tensor", lambda e, a=bshi[:, i:i + 1], b=r_hi, s=slr, st=(i % 4 == 0): e.matmul(
            s, a, b, start=st, stop=False))
        p.do("tensor", lambda e, a=bshi[:, i:i + 1], b=r_lo, s=slr: e.matmul(s, a, b, start=False, stop=False))
        t_mm = p.do("tensor", lambda e, a=bslo[:, i:i + 1], b=r_hi, s=slr, st=(i % 4 == 3): e.matmul(
            s, a, b, start=False, stop=st),
                    inc=("PE" if i == NCH - 1 else None))
    p.wait("vector", t_mm)
    p.do("vector", lambda e: e.tensor_copy(rowf[:], pp_r[:]))
    p.drain("vector")
    p.do("vector", lambda e: e.tensor_copy(rowh[:], rowf[:]))
    p.drain("vector")
    t_row = p.do("vector", lambda e: e.tensor_sub(rowl[:], rowf[:], rowh[:]), inc="VE")
    t_bkb = bcast_row(t_row, bkb)

    p.wait("gpsimd", t_bkb)
    t_ldbv = p.do("gpsimd", lambda e: e.dma_start(out=rowf[:], in_=d_bvr), inc="WTB", n=16)
    p.wait("vector", t_ldbv)
    p.do("vector", lambda e: e.tensor_copy(rowh[:], rowf[:]))
    p.drain("vector")
    t_row2 = p.do("vector", lambda e: e.tensor_sub(rowl[:], rowf[:], rowh[:]), inc="VE")
    t_bvb = bcast_row(t_row2, bvb)

    t_wv = load_split_w(d_wv, False, t_wk_mm, None, out_hi=wb_hi, out_lo=wb_lo)

    emit_block(nc, p)
    pes.close()

    p = P(val, sems)
    kes = ExitStack()
    kbf = sb(kes, "kbf", [128, 64 * 128], BF16)
    p.wait("gpsimd", t_wv)
    p.wait("gpsimd", t_bvb)
    p.wait("gpsimd", t_wo_st)
    t_ldkq = p.do("gpsimd", lambda e: e.dma_start(
        out=kbf[:], in_=d_fk), inc="WTW", n=16)
    p.wait("scalar", t_ldkq)
    p.wait("scalar", t_rsk)
    t_kq = p.do("scalar", lambda e: e.activation(kq8[:], kbf[:], ACTF.Copy, scale=rsk[:]), inc="AE")
    emit_block(nc, p)
    kes.close()

    pro = dict(wk=t_wk_ev, wv=t_wv, kq=t_kq, bqp=t_bqp, bkb=t_bkb, bvb=t_bvb,
               zeroed=t_zero)

    p = P(val, sems)
    aes = ExitStack()
    x_tm = [sb(aes, f"x_tm{i}", [128, D], F32) for i in range(2)]
    sq = sb(aes, "sq", [128, D], BF16)
    stA = [dict(nsum=sb(aes, f"Ans{i}", [128, 2], F32), ssum=sb(aes, f"Ass{i}", [128, 2], F32),
                negm=sb(aes, f"Anm{i}", [128, 2], F32), t1=sb(aes, f"At1{i}", [128, 2], F32),
                t2=sb(aes, f"At2{i}", [128, 2], F32), rstd=sb(aes, f"Ars{i}", [128, 2], F32))
           for i in range(2)]
    stC = [dict(nsum=sb(aes, f"Cns{i}", [128, 2], F32), ssum=sb(aes, f"Css{i}", [128, 2], F32),
                negm=sb(aes, f"Cnm{i}", [128, 2], F32), t1=sb(aes, f"Ct1{i}", [128, 2], F32),
                t2=sb(aes, f"Ct2{i}", [128, 2], F32), rstd=sb(aes, f"Crs{i}", [128, 2], F32))
           for i in range(2)]
    xq_bf = [sb(aes, f"xq_bf{i}", [128, D], BF16) for i in range(2)]
    xq_fm = [sb(aes, f"xq_fm{i}", [128, NCH * GT], BF16, is_transpose_dest=True) for i in range(2)]
    xq8 = [sb(aes, f"xq8{i}", [128, NCH * GT], FP8) for i in range(2)]
    tz = [sb(aes, f"tz{i}", [128, GT], BF16) for i in range(2)]
    z_fm = [sb(aes, f"z_fm{i}", [128, NCH * GT], BF16) for i in range(2)]
    z_tm = [sb(aes, f"z_tm{i}", [128, D], BF16, is_transpose_dest=True) for i in range(2)]
    x1_tm = [sb(aes, f"x1_tm{i}", [128, D], F32) for i in range(2)]
    u_ln = [sb(aes, f"u_ln{i}", [128, D], F32) for i in range(2)]
    xlh_tm = [sb(aes, f"xlh_tm{i}", [128, D], BF16) for i in range(2)]
    xll_tm = [sb(aes, f"xll_tm{i}", [128, D], BF16) for i in range(2)]
    xlh_fm = [sb(aes, f"xlh_fm{i}", [128, NCH * GT], BF16, is_transpose_dest=True) for i in range(2)]
    xll_fm = [sb(aes, f"xll_fm{i}", [128, NCH * GT], BF16, is_transpose_dest=True) for i in range(2)]
    v_t = [sb(aes, f"v_t{i}", [128, D], F32) for i in range(2)]
    v_hi = [sb(aes, f"v_hi{i}", [128, D], BF16) for i in range(2)]
    v_lo = [sb(aes, f"v_lo{i}", [128, D], BF16) for i in range(2)]
    kp_t = [sb(aes, f"kp_t{i}", [128, D], F32) for i in range(2)]
    kp_hi = [sb(aes, f"kp_hi{i}", [128, D], BF16) for i in range(2)]
    kp_lo = [sb(aes, f"kp_lo{i}", [128, D], BF16) for i in range(2)]

    P_kv = aes.enter_context(nc.psum_tensor("P_kv", [128, D], F32))
    P_ks = aes.enter_context(nc.psum_tensor("P_ks", [128, NCH], F32))
    P_V = aes.enter_context(nc.psum_tensor("P_V", [128, D], F32))
    P_K = aes.enter_context(nc.psum_tensor("P_K", [128, D], F32))

    rel = {}

    def stats_pair(x_aps, st, scratch, t_st_free_v, t_st_free_a, t_readys):
        p.wait("vector", t_st_free_v)
        for i, x_ap in enumerate(x_aps):
            p.do("vector", lambda e, xx=x_ap, ii=i: e.reduce_sum(
                st["nsum"][:, ii:ii + 1], xx, axis=AX, negate=True))
        t_sq = None
        for i, x_ap in enumerate(x_aps):
            if i == 0:
                p.wait("scalar", t_st_free_a)
            p.wait("scalar", t_readys[i])
            p.drain("scalar")
            t_sq = p.do("scalar", lambda e, xx=x_ap, ii=i: e.activation(
                scratch[:], xx, ACTF.Square, accum_out=st["ssum"][:, ii:ii + 1]), inc="AE")
        n = len(x_aps)
        p.do("vector", lambda e: e.tensor_scalar_mul(st["negm"][:, 0:n], st["nsum"][:, 0:n], 1.0 / D))
        p.wait("vector", t_sq)
        p.do("vector", lambda e: e.tensor_scalar_mul(st["t1"][:, 0:n], st["ssum"][:, 0:n], 1.0 / D))
        p.do("vector", lambda e: e.tensor_mul(st["t2"][:, 0:n], st["negm"][:, 0:n], st["negm"][:, 0:n]))
        t_var = p.do("vector", lambda e: e.tensor_sub(
            st["nsum"][:, 0:n], st["t2"][:, 0:n], st["t1"][:, 0:n]), inc="VE")
        p.wait("scalar", t_var)
        t_s = p.do("scalar", lambda e: e.activation(st["t2"][:, 0:n], st["nsum"][:, 0:n],
                                                    ACTF.Sqrt, bias=epsln[:], scale=-1.0), inc="AE")
        p.wait("vector", t_s)
        p.do("vector", lambda e: e.reciprocal(st["t1"][:, 0:n], st["t2"][:, 0:n]))
        p.do("vector", lambda e: e.tensor_mul(st["rstd"][:, 0:n], st["t2"][:, 0:n], st["t1"][:, 0:n]))
        p.do("vector", lambda e: e.tensor_scalar(st["rstd"][:, 0:n], st["rstd"][:, 0:n], 2.0, -1.0,
                                                 ALU.subtract, ALU.mult))
        t_r = p.do("vector", lambda e: e.tensor_mul(st["rstd"][:, 0:n], st["t1"][:, 0:n],
                                                    st["rstd"][:, 0:n]), inc="VE")
        return t_r

    def proj_tm(xh_sl, xl_sl, w_h, w_l, reg_psum, i, first, last):
        t = None
        for half in range(2):
            regv = reg_psum[:, half * 512:half * 512 + 512]
            mv_h = w_h[:, i * D + half * 512:i * D + half * 512 + 512]
            mv_l = w_l[:, i * D + half * 512:i * D + half * 512 + 512]
            p.do("tensor", lambda e, a=xh_sl, m=mv_h, r=regv, s=first: e.matmul(r, a, m, start=s, stop=False))
            p.do("tensor", lambda e, a=xh_sl, m=mv_l, r=regv: e.matmul(r, a, m, start=False, stop=False))
            t = p.do("tensor", lambda e, a=xl_sl, m=mv_h, r=regv, s=last: e.matmul(r, a, m, start=False, stop=s),
                     inc=("PE" if (last and half == 1) else None))
        return t

    p.wait("gpsimd", pro["zeroed"])
    rel[("xld", 0)] = p.do("gpsimd", lambda e: e.dma_start(
        out=x_tm[0][:], in_=d_x[0, 0:128, :]), inc="LXA0", n=16)

    NGT = NB * NGRP

    def prefetch_x(tid):
        if tid + 1 >= NB * NSUB:
            return
        nb_, nk_ = divmod(tid + 1, NSUB)
        p.wait("gpsimd", rel.get(("xfree", tid - 1)))
        p.wait("gpsimd", rel.get(("stA_free_a", (tid - 1) // 2)))
        rel[("xld", tid + 1)] = p.do("gpsimd", lambda e, bb=nb_, kk=nk_: e.dma_start(
            out=x_tm[(kk % 2)][:], in_=d_x[bb, kk * 128:kk * 128 + 128, :]),
            inc=f"LXA{(tid + 1) % 2}", n=16)

    def emit_ln1_pair(G):
        b, g = divmod(G, NGRP)
        gi = g % 2
        grp = G
        tids = [b * NSUB + 2 * g, b * NSUB + 2 * g + 1]
        for tid in tids:
            p.wait("vector", rel[("xld", tid)])
        t_r = stats_pair([x_tm[0][:], x_tm[1][:]], stA[gi], sq,
                         rel.get(("stA_free", grp - 2)), rel.get(("stA_free_a", grp - 2)),
                         [rel[("xld", t)] for t in tids])
        for i, tid in enumerate(tids):
            ki = tid % 2
            k = tid % NSUB
            p.wait("vector", rel.get(("xq_tp", tid - 2)))
            t_xq = p.do("vector", lambda e, kk=ki, ii=i: e.tensor_scalar(
                xq_bf[kk][:], x_tm[kk][:], stA[gi]["negm"][:, ii:ii + 1],
                stA[gi]["rstd"][:, ii:ii + 1], ALU.add, ALU.mult), inc="VE")
            if i == 1:
                rel[("stA_free", grp)] = t_xq
            p.wait("sync", t_xq)
            p.wait("sync", rel.get(("q8_tile", tid - 2)))
            t_tp = p.do("sync", lambda e, kk=ki, gg=gi, ss=k % SPG: e.dma_start(
                out=chunk3(xq_fm[gg][:], ss, stride=GT),
                in_=chunk3(xq_bf[kk][:], 0), transpose=True), inc=f"TPQ{tid % 2}", n=16)
            rel[("xq_tp", tid)] = t_tp
            p.wait("scalar", t_tp)
            p.wait("scalar", rel.get(("ffn_mm", grp - 2)))
            t_q8 = p.do("scalar", lambda e, gg=gi, ss=k % SPG: e.activation(
                chunk3cm(xq8[gg][:], ss), chunk3(xq_fm[gg][:], ss, stride=GT),
                ACTF.Copy, scale=rsin[:]), inc="AE")
            if i == 1:
                rel[("stA_free_a", grp)] = t_q8
            rel[("q8_tile", tid)] = t_q8
            if i == 1:
                rel[("xq8_done", grp)] = t_q8

    def emit_ffn1(G):
        b, g = divmod(G, NGRP)
        gi = g % 2
        tid = b * NSUB + 2 * g + 1
        grp = G
        p.wait("tensor", rel[("xq8_done", grp)])
        if grp == 0:
            p.wait("tensor", pro["kq"])
        t_fmm = None
        t_zv = None
        for j in range(NCH):
            reg = (P_V if j % 2 == 0 else P_K)[:, 0:GT]
            if j == 0:
                p.wait("tensor", rel.get(("V_ev", tid - 3)))
            elif j == 1:
                p.wait("tensor", rel.get(("KP_ev", tid - 3)))
            else:
                p.wait("tensor", rel.get(("ffn_dq", grp, j - 2)))
            for i in range(NCH // 2):
                t_fmm = p.do("tensor", lambda e, ii=i, jj=j, gg=gi, r=reg: e.matmul(
                    r, kq8[:, (ii * NCH + jj) * 256:(ii * NCH + jj) * 256 + 256]
                        .rearrange("p (i m) -> p i m", i=2),
                    xq8[gg][:, ii * 512:(ii + 1) * 512].rearrange("p (i n) -> p i n", i=2),
                    start=(ii == 0), stop=(ii == NCH // 2 - 1), perf_mode=DR),
                    inc=("PE" if i == NCH // 2 - 1 else None))
            p.wait("scalar", t_fmm)
            p.wait("scalar", rel.get(("tz_free", grp, j - 2)))
            t_dq = p.do("scalar", lambda e, jj=j, r=reg: e.activation(
                tz[jj % 2][:], r, ACTF.Copy, scale=sprod[:]), inc="AE")
            rel[("ffn_dq", grp, j)] = t_dq
            p.wait("vector", t_dq)
            p.wait("vector", rel.get(("z_tp0", grp - 2)))
            p.wait("vector", rel.get(("z_tp1", grp - 2)))
            t_zv = p.do("vector", lambda e, jj=j, gg=gi: e.tensor_scalar(
                ch_view(z_fm[gg][:], jj), tz[jj % 2][:].rearrange("p (s f) -> p s f", s=SPG),
                fb32_c[:, jj:jj + 1], 0.5, ALU.add, ALU.mult), inc="VE")
            rel[("tz_free", grp, j)] = t_zv
        rel[("ffn_mm", grp)] = t_fmm
        rel[("ffn_zv", grp)] = t_zv

    def emit_tail_a(G):
        b, g = divmod(G, NGRP)
        gi = g % 2
        grp = G
        t_ffn = rel[("ffn_zv", grp)]
        t_x1s = []
        for k2 in (2 * g, 2 * g + 1):
            k2i = k2 % 2
            t2 = b * NSUB + k2
            ss = k2 % SPG
            p.wait("sync", t_ffn)
            p.wait("sync", rel.get(("x1_done", t2 - 2)))
            t_ztp = p.do("sync", lambda e, gg=gi, s2=ss, kk=k2i: e.dma_start(
                out=chunk3(z_tm[kk][:], 0), in_=chunk3(z_fm[gg][:], s2, stride=GT),
                transpose=True), inc=f"TPZ{k2 % 2}", n=16)
            rel[(f"z_tp{ss}", grp)] = t_ztp
            p.wait("vector", t_ztp)
            p.wait("vector", rel.get(("x1_st", t2 - 2)))
            p.wait("vector", rel.get(("stC_free_a", grp - 1)))
            t_x1 = p.do("vector", lambda e, kk=k2i: e.tensor_add(
                x1_tm[kk][:], x_tm[kk][:], z_tm[kk][:]), inc="VE")
            rel[("x1_done", t2)] = t_x1
            rel[("xfree", t2)] = t_x1
            t_x1s.append(t_x1)
            p.wait("gpsimd", t_x1)
            rel[("x1_st", t2)] = p.do("gpsimd", lambda e, bb=b, kk2=k2, kk=k2i: e.dma_start(
                out=d_sx1[bb, kk2], in_=x1_tm[kk][:]), inc=f"SV1{k2 % 2}", n=16)
        stats_pair([x1_tm[0][:], x1_tm[1][:]], stC[gi], sq,
                   rel.get(("stC_free", grp - 2)), rel.get(("stC_free_a", grp - 2)),
                   t_x1s)
        emit_tail_split(G, 0)

    def emit_tail_split(G, i):
        b, g = divmod(G, NGRP)
        gi = g % 2
        grp = G
        k2 = 2 * g + i
        k2i = k2 % 2
        t2 = b * NSUB + k2
        ss = k2 % SPG
        p.wait("vector", rel.get(("xl_hi_used_a", t2 - 2)))
        p.wait("vector", rel.get(("kp_split", t2 - 2)))
        t_u = p.do("vector", lambda e, kk=k2i, ii=i: e.tensor_scalar(
            u_ln[kk][:], x1_tm[kk][:], stC[gi]["negm"][:, ii:ii + 1],
            stC[gi]["rstd"][:, ii:ii + 1], ALU.add, ALU.mult), inc="VE")
        p.wait("vector", rel.get(("xl_tp", t2 - 2)))
        t_h = p.do("vector", lambda e, kk=k2i: e.tensor_copy(
            xlh_tm[kk][:], u_ln[kk][:]), inc="VE")
        rel[("xl_hi_used_a", t2)] = t_h
        t_l = p.do("vector", lambda e, kk=k2i: e.tensor_sub(
            xll_tm[kk][:], u_ln[kk][:], xlh_tm[kk][:]), inc="VE")
        if i == 1:
            rel[("stC_free", grp)] = t_l
            rel[("stC_free_a", grp)] = t_l
        p.wait("sync", t_l)
        p.wait("sync", rel.get(("xl_stored", grp - 2)))
        p.wait("sync", rel.get(("kmm", 2 * G - 3)))
        p.do("sync", lambda e, gg=gi, s2=ss, kk=k2i: e.dma_start(
            out=chunk3(xlh_fm[gg][:], s2, stride=GT),
            in_=chunk3(xlh_tm[kk][:], 0), transpose=True), inc=f"TPL{k2 % 2}", n=16)
        t_tpl = p.do("sync", lambda e, gg=gi, s2=ss, kk=k2i: e.dma_start(
            out=chunk3(xll_fm[gg][:], s2, stride=GT),
            in_=chunk3(xll_tm[kk][:], 0), transpose=True), inc=f"TPL{k2 % 2}", n=16)
        rel[("xl_tp", t2)] = t_tpl

    def emit_tail_b(G):
        b, g = divmod(G, NGRP)
        gi = g % 2
        grp = G
        emit_tail_split(G, 1)
        tid = b * NSUB + 2 * g + 1
        p.wait("gpsimd", rel[("xl_tp", tid - 1)])
        p.wait("gpsimd", rel[("xl_tp", tid)])
        p.do("gpsimd", lambda e, bb=b, gg2=g, gg=gi: e.dma_start(
            out=d_sxlh[bb, gg2], in_=xlh_fm[gg][:]), inc=f"SVX{gi}", n=16)
        rel[("xl_stored", grp)] = p.do("gpsimd", lambda e, bb=b, gg2=g, gg=gi: e.dma_start(
            out=d_sxll[bb, gg2], in_=xll_fm[gg][:]), inc=f"SVX{gi}", n=16)

    def emit_vk_projs(G, which):
        b, g = divmod(G, NGRP)
        gi = g % 2
        grp = G
        for k2 in (2 * g + which,):
            k2i = k2 % 2
            t2 = b * NSUB + k2
            ss = k2 % SPG
            p.wait("tensor", rel[("xl_tp", t2)])
            if grp == 0:
                p.wait("tensor", pro["wv"])
                p.wait("tensor", pro["bvb"])
                p.wait("tensor", pro["bkb"])
                p.wait("tensor", pro["wk"])
            t_vmm = None
            p.wait("tensor", rel.get(("V_ev", t2 - 1)))
            if which == 1:
                p.wait("tensor", rel.get(("ffn_dq", grp + 1, NCH - 2)))
            xh = chunk3(xlh_fm[gi][:], ss, stride=GT)
            xl = chunk3(xll_fm[gi][:], ss, stride=GT)
            for i in range(NCH):
                t_vmm = proj_tm(xh[:, i, :], xl[:, i, :], wb_hi, wb_lo, P_V, i,
                                first=(i == 0), last=(i == NCH - 1)) or t_vmm
            p.wait("vector", t_vmm)
            p.wait("vector", rel.get(("v_split", t2 - 2)))
            p.wait("vector", rel.get(("kv_mm", t2 - 2)))
            t_vt = p.do("vector", lambda e, kk=k2i: e.tensor_add(
                v_t[kk][:], P_V[:], bvb[:]), inc="VE")
            rel[("V_ev", t2)] = t_vt
            p.wait("vector", rel.get(("kv_mm", t2 - 2)))
            p.do("vector", lambda e, kk=k2i: e.tensor_copy(v_hi[kk][:], v_t[kk][:]))
            t_vl = p.do("vector", lambda e, kk=k2i: e.tensor_sub(
                v_lo[kk][:], v_t[kk][:], v_hi[kk][:]), inc="VE")
            rel[("v_split", t2)] = t_vl

            p.wait("tensor", rel.get(("KP_ev", t2 - 1)))
            if which == 1:
                p.wait("tensor", rel.get(("ffn_dq", grp + 1, NCH - 1)))
            t_kmm = None
            for i in range(NCH):
                t_kmm = proj_tm(xh[:, i, :], xl[:, i, :], wa_hi, wa_lo, P_K, i,
                                first=(i == 0), last=(i == NCH - 1)) or t_kmm
            p.wait("vector", t_kmm)
            p.wait("vector", rel.get(("kp_split", t2 - 2)))
            p.wait("vector", rel.get(("kv_mm", t2 - 2)))
            t_kt = p.do("vector", lambda e, kk=k2i: e.tensor_add(
                kp_t[kk][:], P_K[:], bkb[:]), inc="VE")
            rel[("KP_ev", t2)] = t_kt
            p.wait("scalar", t_kt)
            t_kr = p.do("scalar", lambda e, kk=k2i: e.activation(
                u_ln[kk][:], kp_t[kk][:], ACTF.Relu), inc="AE")
            p.wait("vector", t_kr)
            t_ke = p.do("vector", lambda e, kk=k2i: e.tensor_scalar_add(
                kp_t[kk][:], u_ln[kk][:], EPS_AT), inc="VE")
            p.do("vector", lambda e, kk=k2i: e.tensor_copy(kp_hi[kk][:], kp_t[kk][:]))
            t_kl = p.do("vector", lambda e, kk=k2i: e.tensor_sub(
                kp_lo[kk][:], kp_t[kk][:], kp_hi[kk][:]), inc="VE")
            rel[("kp_split", t2)] = t_kl
            rel[("kmm", t2)] = t_kmm

    def emit_kv(G, which):
        b, g = divmod(G, NGRP)
        grp = G
        k2 = 2 * g + which
        k2i = k2 % 2
        t2 = b * NSUB + k2
        p.wait("tensor", rel[("kp_split", t2)])
        p.wait("tensor", rel[("v_split", t2)])
        if b == 1 and k2 == 0:
            p.wait("tensor", rel[("kv_done", 0)])
        for c in range(NCH):
            kvreg = P_kv[:, c * 128:(c + 1) * 128]
            ksreg = P_ks[:, c:c + 1]
            sth = kp_hi[k2i][:, c * 128:(c + 1) * 128]
            stl = kp_lo[k2i][:, c * 128:(c + 1) * 128]
            mvh = v_hi[k2i][:, c * 128:(c + 1) * 128]
            mvl = v_lo[k2i][:, c * 128:(c + 1) * 128]
            kv_start = (k2 == 0 and c % 4 == 0)
            kv_stop = (k2 == NSUB - 1 and c % 4 == 3)
            ks_start = (k2 == 0 and c == 0)
            ks_stop = (k2 == NSUB - 1 and c == NCH - 1)
            p.do("tensor", lambda e, a=sth, m=mvh, r=kvreg, s=kv_start: e.matmul(r, a, m, start=s, stop=False))
            p.do("tensor", lambda e, a=sth, m=mvl, r=kvreg: e.matmul(r, a, m, start=False, stop=False))
            p.do("tensor", lambda e, a=sth, r=ksreg, s=ks_start: e.matmul(r, a, ones128[:], start=s, stop=False))
            p.do("tensor", lambda e, a=stl, m=mvh, r=kvreg, s=kv_stop: e.matmul(r, a, m, start=False, stop=s))
            t_kv = p.do("tensor", lambda e, a=stl, r=ksreg, s=ks_stop: e.matmul(
                r, a, ones128[:], start=False, stop=s),
                inc=("PE" if c == NCH - 1 else None))
        rel[("kv_mm", t2)] = t_kv
        if k2 == NSUB - 1:
            t_kv_last = t_kv
            p.wait("scalar", t_kv_last)
            p.wait("vector", t_kv_last)
            t_ev = None
            for c in range(NCH):
                for (r0, r1, c0, c1) in [(0, 64, 0, 64), (64, 128, 64, 128)]:
                    srcp = P_kv[r0:r1, c * 128 + c0:c * 128 + c1]
                    dh = kv_hi[r0:r1, b * D + c * 128 + c0:b * D + c * 128 + c1]
                    dl = kv_lo[r0:r1, b * D + c * 128 + c0:b * D + c * 128 + c1]
                    t_a = p.do("scalar", lambda e, s=srcp, d=dh: e.activation(d, s, ACTF.Copy), inc="AE")
                    p.wait("vector", t_a)
                    t_ev = p.do("vector", lambda e, s=srcp, d=dl, h=dh: e.tensor_sub(d, s, h), inc="VE")
                base = (b * NCH + c) * 16
                for (r0, r1, off) in [(0, 64, 0), (64, 128, 1)]:
                    srcp = P_ks[r0:r1, c:c + 1]
                    dh = ksb_hi[r0:r1, base + 2 * c + off:base + 2 * c + off + 1]
                    dl = ksb_lo[r0:r1, base + 2 * c + off:base + 2 * c + off + 1]
                    t_a = p.do("scalar", lambda e, s=srcp, d=dh: e.activation(d, s, ACTF.Copy), inc="AE")
                    p.wait("vector", t_a)
                    t_ev = p.do("vector", lambda e, s=srcp, d=dl, h=dh: e.tensor_sub(d, s, h), inc="VE")
            rel[("kv_done", b)] = t_ev

    rel[("xld", 1)] = p.do("gpsimd", lambda e: e.dma_start(
        out=x_tm[1][:], in_=d_x[0, 128:256, :]), inc="LXA1", n=16)
    emit_ln1_pair(0)
    for G in range(NGT):
        if G >= 1:
            emit_vk_projs(G - 1, 0)
        emit_ffn1(G)
        emit_tail_a(G)
        if G >= 1:
            emit_vk_projs(G - 1, 1)
            emit_kv(G - 1, 0)
            emit_kv(G - 1, 1)
        prefetch_x(2 * G + 1)
        if G + 1 < NGT:
            prefetch_x(2 * G + 2)
            emit_ln1_pair(G + 1)
        emit_tail_b(G)
    emit_vk_projs(NGT - 1, 0)
    emit_vk_projs(NGT - 1, 1)
    emit_kv(NGT - 1, 0)
    emit_kv(NGT - 1, 1)

    a_last = dict(kv=rel[("kv_done", 1)], pe=p.cur("PE"),
                  sv1a=p.cur("SV10"), sv1b=p.cur("SV11"),
                  svxa=p.cur("SVX0"), svxb=p.cur("SVX1"))
    emit_block(nc, p)
    aes.close()

    p = P(val, sems)
    bes = ExitStack()
    x1b = [sb(bes, f"x1b{i}", [128, D], F32) for i in range(2)]
    xlhb = [sb(bes, f"xlhb{i}", [128, NCH * GT], BF16) for i in range(2)]
    xllb = [sb(bes, f"xllb{i}", [128, NCH * GT], BF16) for i in range(2)]
    qp_hi = sb(bes, "qp_hi", [128, NCH * GT], BF16)
    qp_lo = sb(bes, "qp_lo", [128, NCH * GT], BF16)
    tq = [sb(bes, f"tq{i}", [128, GT], F32) for i in range(2)]
    uq = [sb(bes, f"uq{i}", [128, GT], F32) for i in range(2)]
    atmp = [sb(bes, f"atmp{i}", [128, GT], F32) for i in range(2)]
    dnm = sb(bes, "dnm", [16, GT], F32)
    rc0 = sb(bes, "rc0", [16, GT], F32)
    rc1 = sb(bes, "rc1", [16, GT], F32)
    r_hi = sb(bes, "r_hi", [16, GT], BF16)
    r_lo = sb(bes, "r_lo", [16, GT], BF16)
    rb_sb = sb(bes, "rb_sb", [128, 2 * GT], F32)
    at_hi = sb(bes, "at_hi", [128, NCH * GT], BF16)
    at_lo = sb(bes, "at_lo", [128, NCH * GT], BF16)
    o_hi = sb(bes, "o_hi", [128, NCH * GT], BF16)
    o_lo = sb(bes, "o_lo", [128, NCH * GT], BF16)
    oh_tm = [sb(bes, f"oh_tm{i}", [128, D], BF16, is_transpose_dest=True) for i in range(2)]
    ol_tm = [sb(bes, f"ol_tm{i}", [128, D], BF16, is_transpose_dest=True) for i in range(2)]
    x2_tm = [sb(bes, f"x2_tm{i}", [128, D], F32) for i in range(4)]
    sqb = sb(bes, "sqb", [128, D], BF16)
    stB = [dict(nsum=sb(bes, f"Bns{i}", [128, 2], F32), ssum=sb(bes, f"Bss{i}", [128, 2], F32),
                negm=sb(bes, f"Bnm{i}", [128, 2], F32), t1=sb(bes, f"Bt1{i}", [128, 2], F32),
                t2=sb(bes, f"Bt2{i}", [128, 2], F32), rstd=sb(bes, f"Brs{i}", [128, 2], F32))
           for i in range(2)]
    xl3_bf = [sb(bes, "xl3_bf0", [128, D], BF16)]
    xl3_fm = [sb(bes, "xl3_fm0", [128, NCH * GT], BF16, is_transpose_dest=True)]
    xq8b = [sb(bes, f"xq8b{i}", [128, NCH * GT], FP8) for i in range(2)]
    tzb = [sb(bes, f"tzb{i}", [128, GT], BF16) for i in range(2)]
    z2_fm = [sb(bes, "z2_fm0", [128, NCH * GT], BF16)]
    z2_tm = [sb(bes, f"z2_tm{i}", [128, D], BF16, is_transpose_dest=True) for i in range(2)]
    out_tm = [sb(bes, f"out_tm{i}", [128, D], F32) for i in range(2)]

    PBp = [bes.enter_context(nc.psum_tensor(f"PBp{i}", [128, GT], F32)) for i in range(6)]
    P_n = bes.enter_context(nc.psum_tensor("P_n", [16, GT], F32))

    pr = {}

    def stats_pair_b(x_aps, st, t_free_v, t_free_a, t_readys):
        p.wait("vector", t_free_v)
        for i, x_ap in enumerate(x_aps):
            p.do("vector", lambda e, xx=x_ap, ii=i: e.reduce_sum(
                st["nsum"][:, ii:ii + 1], xx, axis=AX, negate=True))
        t_sq = None
        for i, x_ap in enumerate(x_aps):
            if i == 0:
                p.wait("scalar", t_free_a)
            p.wait("scalar", t_readys[i])
            p.drain("scalar")
            t_sq = p.do("scalar", lambda e, xx=x_ap, ii=i: e.activation(
                sqb[:], xx, ACTF.Square, accum_out=st["ssum"][:, ii:ii + 1]), inc="AE")
        n = len(x_aps)
        p.do("vector", lambda e: e.tensor_scalar_mul(st["negm"][:, 0:n], st["nsum"][:, 0:n], 1.0 / D))
        p.wait("vector", t_sq)
        p.do("vector", lambda e: e.tensor_scalar_mul(st["t1"][:, 0:n], st["ssum"][:, 0:n], 1.0 / D))
        p.do("vector", lambda e: e.tensor_mul(st["t2"][:, 0:n], st["negm"][:, 0:n], st["negm"][:, 0:n]))
        t_var = p.do("vector", lambda e: e.tensor_sub(
            st["nsum"][:, 0:n], st["t2"][:, 0:n], st["t1"][:, 0:n]), inc="VE")
        p.wait("scalar", t_var)
        t_s = p.do("scalar", lambda e: e.activation(st["t2"][:, 0:n], st["nsum"][:, 0:n],
                                                    ACTF.Sqrt, bias=epsln[:], scale=-1.0), inc="AE")
        p.wait("vector", t_s)
        p.do("vector", lambda e: e.reciprocal(st["t1"][:, 0:n], st["t2"][:, 0:n]))
        p.do("vector", lambda e: e.tensor_mul(st["rstd"][:, 0:n], st["t2"][:, 0:n], st["t1"][:, 0:n]))
        p.do("vector", lambda e: e.tensor_scalar(st["rstd"][:, 0:n], st["rstd"][:, 0:n], 2.0, -1.0,
                                                 ALU.subtract, ALU.mult))
        p.do("vector", lambda e: e.tensor_mul(st["rstd"][:, 0:n], st["t1"][:, 0:n],
                                              st["rstd"][:, 0:n]), inc="VE")
        return t_sq

    p.wait("gpsimd", a_last["sv1a"])
    p.wait("gpsimd", a_last["sv1b"])
    p.wait("gpsimd", a_last["svxa"])
    p.wait("gpsimd", a_last["svxb"])
    p.wait("gpsimd", a_last["kv"])
    p.wait("gpsimd", a_last["pe"])
    t_wload = None
    for dst, src in [(wa_hi[:], d_swq_hi), (wa_lo[:], d_swq_lo),
                     (wb_hi[:], d_swo_hi), (wb_lo[:], d_swo_lo)]:
        t_wload = p.do("gpsimd", lambda e, d=dst, s=src: e.dma_start(out=d, in_=s), inc="WTW", n=16)

    NGT = NB * NGRP

    def emit_xl_load(G):
        if G >= NGT:
            return
        b, g = divmod(G, NGRP)
        gi = g % 2
        p.wait("gpsimd", pr.get(("xl_used", G - 2)))
        p.do("gpsimd", lambda e, bb=b, gg=g, ii=gi: e.dma_start(
            out=xlhb[ii][:], in_=d_sxlh[bb, gg]), inc=f"LXB{gi}", n=16)
        pr[("xlld", G)] = p.do("gpsimd", lambda e, bb=b, gg=g, ii=gi: e.dma_start(
            out=xllb[ii][:], in_=d_sxll[bb, gg]), inc=f"LXB{gi}", n=16)

    def emit_x1_load(G):
        if G >= NGT:
            return
        b, g = divmod(G, NGRP)
        p.wait("scalar", pr.get(("x1_used", G - 1)))
        for s2 in range(SPG):
            pr[("x1ld", G, s2)] = p.do("scalar", lambda e, bb=b, kk=g * SPG + s2, ii=s2: e.dma_start(
                out=x1b[ii][:], in_=d_sx1[bb, kk]), inc=f"LX1{s2}", n=16)

    def emit_qno(G, tail_G=None):
        b, g = divmod(G, NGRP)
        gi = g % 2
        grp = G
        p.wait("tensor", t_wload)
        p.wait("tensor", pr[("xlld", G)])
        if grp >= 2:
            p.wait("tensor", pr[("ffn2_dq", grp - 2, 4)])
        elif grp == 1:
            p.wait("tensor", pr[("o_hi", grp - 1, NCH - 1)])
        t_qmm = None
        for j in range(NCH):
            reg = PBp[j % 2][:]
            p.wait("tensor", pr.get(("qp_hi", grp, j - 2)))
            if j == 1 and grp >= 2:
                p.wait("tensor", pr[("ffn2_dq", grp - 2, 5)])
            for i in range(NCH):
                sh = wa_hi[:, (i * NCH + j) * 128:(i * NCH + j + 1) * 128]
                sl = wa_lo[:, (i * NCH + j) * 128:(i * NCH + j + 1) * 128]
                mh = ch_view(xlhb[gi][:], i)
                ml = ch_view(xllb[gi][:], i)
                p.do("tensor", lambda e, a=sh, m=mh, r=reg, ii=i: e.matmul(r, a, m, start=(ii == 0), stop=False))
                p.do("tensor", lambda e, a=sh, m=ml, r=reg: e.matmul(r, a, m, start=False, stop=False))
                t_qmm = p.do("tensor", lambda e, a=sl, m=mh, r=reg, ii=i: e.matmul(
                    r, a, m, start=False, stop=(ii == NCH - 1)),
                    inc=("PE" if i == NCH - 1 else None))
            p.wait("scalar", t_qmm)
            p.wait("scalar", pr.get(("tq_used", grp, j - 2)))
            if j < 2:
                p.wait("scalar", pr.get(("to_used", grp - 1, NCH - 2 + j)))
            p.wait("scalar", pr.get(("qkv_mm", grp - 1)))
            t_qr = p.do("scalar", lambda e, jj=j, r=reg: e.activation(
                tq[jj % 2][:], r, ACTF.Relu, bias=bqp_c[:, jj:jj + 1]), inc="AE")
            pr[("qp_hi", grp, j)] = t_qr
            p.wait("vector", t_qr)
            p.wait("vector", pr.get(("uq_used", grp, j - 2)))
            p.wait("vector", pr.get(("qkv_mm", grp - 1)))
            t_qe = p.do("vector", lambda e, jj=j: e.tensor_scalar_add(
                uq[jj % 2][:], tq[jj % 2][:], EPS_AT), inc="VE")
            pr[("tq_used", grp, j)] = t_qe
            p.do("vector", lambda e, jj=j: e.tensor_copy(
                qp_hi[:, jj * GT:(jj + 1) * GT], uq[jj % 2][:]))
            t_ql = p.do("vector", lambda e, jj=j: e.tensor_sub(
                qp_lo[:, jj * GT:(jj + 1) * GT], uq[jj % 2][:],
                qp_hi[:, jj * GT:(jj + 1) * GT]), inc="VE")
            pr[("uq_used", grp, j)] = t_ql
        t_qp = t_ql
        pr[("xl_used", grp)] = t_qmm

        p.wait("tensor", t_qp)
        p.wait("tensor", pr.get(("dnm_read", grp - 1)))
        for c in range(NCH):
            base = (b * NCH + c) * 16
            sh = ksb_hi[:, base:base + 16]
            sl = ksb_lo[:, base:base + 16]
            mh = qp_hi[:, c * GT:(c + 1) * GT]
            ml = qp_lo[:, c * GT:(c + 1) * GT]
            p.do("tensor", lambda e, a=sh, m=mh, cc=c: e.matmul(P_n[:], a, m, start=(cc == 0), stop=False))
            p.do("tensor", lambda e, a=sh, m=ml: e.matmul(P_n[:], a, m, start=False, stop=False))
            t_nmm = p.do("tensor", lambda e, a=sl, m=mh, cc=c: e.matmul(
                P_n[:], a, m, start=False, stop=(cc == NCH - 1)),
                inc=("PE" if c == NCH - 1 else None))

        p.wait("vector", t_nmm)
        t_dn = p.do("vector", lambda e: e.tensor_scalar_add(dnm[:], P_n[:], EPS_AT), inc="VE")
        pr[("dnm_read", grp)] = t_dn
        p.do("vector", lambda e: e.reciprocal(rc0[:], dnm[:]))
        p.drain("vector")
        p.do("vector", lambda e: e.tensor_mul(rc1[:], dnm[:], rc0[:]))
        p.drain("vector")
        p.do("vector", lambda e: e.tensor_scalar(rc1[:], rc1[:], 2.0, -1.0, ALU.subtract, ALU.mult))
        p.drain("vector")
        t_rc = p.do("vector", lambda e: e.tensor_mul(rc0[:], rc0[:], rc1[:]), inc="VE")
        p.wait("scalar", t_rc)
        t_rh = p.do("scalar", lambda e: e.activation(r_hi[:], rc0[:], ACTF.Copy), inc="AE")
        p.wait("vector", t_rh)
        t_rl = p.do("vector", lambda e: e.tensor_sub(r_lo[:], rc0[:], r_hi[:]), inc="VE")
        p.wait("tensor", t_rl)
        for c in range(NCH):
            reg = PBp[4 + c % 2][:]
            p.wait("tensor", pr.get(("rb_ev", grp, c - 2)))
            if c == 4:
                p.wait("tensor", pr[("at_hi", grp, 2)])
            elif c == 6:
                p.wait("tensor", pr[("at_hi", grp, 5)])
            if c == 0:
                p.wait("tensor", pr.get(("at_done", grp - 1, 5)))
                p.wait("tensor", pr.get(("ffn2_dq", grp - 2, 6)))
            elif c == 1:
                p.wait("tensor", pr.get(("rb_ev", grp - 1, NCH - 1)))
                p.wait("tensor", pr.get(("ffn2_dq", grp - 2, 7)))
            p.do("tensor", lambda e, cc=c, r=reg: e.matmul(
                r, masks[:, cc * 128:(cc + 1) * 128], r_hi[:], start=True, stop=False))
            t_bc = p.do("tensor", lambda e, cc=c, r=reg: e.matmul(
                r, masks[:, cc * 128:(cc + 1) * 128], r_lo[:], start=False, stop=True), inc="PE")
            p.wait("vector", t_bc)
            pr[("rb_ev", grp, c)] = p.do("vector", lambda e, cc=c % 2, r=reg: e.tensor_copy(
                rb_sb[:, cc * GT:(cc + 1) * GT], r), inc="VE")
            reg = PBp[2 + c % 3][:]
            p.wait("tensor", pr.get(("at_hi", grp, c - 3)))
            if c == 0:
                p.wait("tensor", pr.get(("at_done", grp - 1, 6)))
            elif c == 1:
                p.wait("tensor", pr.get(("at_done", grp - 1, 7)))
            elif c == 2:
                p.wait("tensor", pr[("rb_ev", grp, 2)])
            elif c == 5:
                p.wait("tensor", pr[("rb_ev", grp, 4)])
            sh = kv_hi[:, b * D + c * 128:b * D + (c + 1) * 128]
            sl = kv_lo[:, b * D + c * 128:b * D + (c + 1) * 128]
            mh = qp_hi[:, c * GT:(c + 1) * GT]
            ml = qp_lo[:, c * GT:(c + 1) * GT]
            p.do("tensor", lambda e, a=sh, m=mh, r=reg: e.matmul(r, a, m, start=True, stop=False))
            p.do("tensor", lambda e, a=sh, m=ml, r=reg: e.matmul(r, a, m, start=False, stop=False))
            t_omm = p.do("tensor", lambda e, a=sl, m=mh, r=reg: e.matmul(r, a, m, start=False, stop=True), inc="PE")
            if c == NCH - 1:
                pr[("qkv_mm", grp)] = t_omm
            p.wait("vector", t_omm)
            p.wait("vector", pr[("rb_ev", grp, c)])
            p.wait("vector", pr.get(("at_split", grp, c - 2)))
            p.wait("vector", pr.get(("o_mm", grp - 1)))
            t_at = p.do("vector", lambda e, cc=c, r=reg: e.tensor_mul(
                atmp[cc % 2][:], r, rb_sb[:, (cc % 2) * GT:(cc % 2 + 1) * GT]), inc="VE")
            pr[("at_hi", grp, c)] = t_at
            p.do("vector", lambda e, cc=c: e.tensor_copy(
                at_hi[:, cc * GT:(cc + 1) * GT], atmp[cc % 2][:]))
            t_al = p.do("vector", lambda e, cc=c: e.tensor_sub(
                at_lo[:, cc * GT:(cc + 1) * GT], atmp[cc % 2][:],
                at_hi[:, cc * GT:(cc + 1) * GT]), inc="VE")
            pr[("at_split", grp, c)] = t_al
            pr[("at_done", grp, c)] = t_al
        t_attn = t_al
        if tail_G is not None:
            tailB2_quant(tail_G, 0)
            tailB2_quant(tail_G, 1)

        p.wait("tensor", t_attn)
        t_omm2 = None
        for j in range(NCH):
            reg = PBp[j % 2][:]
            p.wait("tensor", pr.get(("o_hi", grp, j - 2)))
            if j < 2:
                p.wait("tensor", pr[("qp_hi", grp, NCH - 2 + j)])
            for i in range(NCH):
                sh = wb_hi[:, (i * NCH + j) * 128:(i * NCH + j + 1) * 128]
                sl = wb_lo[:, (i * NCH + j) * 128:(i * NCH + j + 1) * 128]
                mh = at_hi[:, i * GT:(i + 1) * GT]
                ml = at_lo[:, i * GT:(i + 1) * GT]
                p.do("tensor", lambda e, a=sh, m=mh, r=reg, ii=i: e.matmul(r, a, m, start=(ii == 0), stop=False))
                p.do("tensor", lambda e, a=sh, m=ml, r=reg: e.matmul(r, a, m, start=False, stop=False))
                t_omm2 = p.do("tensor", lambda e, a=sl, m=mh, r=reg, ii=i: e.matmul(
                    r, a, m, start=False, stop=(ii == NCH - 1)),
                    inc=("PE" if i == NCH - 1 else None))
            if j == NCH - 1:
                pr[("o_mm", grp)] = t_omm2
            p.wait("scalar", t_omm2)
            p.wait("scalar", pr.get(("to_used", grp, j - 2)))
            if j < 2:
                p.wait("scalar", pr[("tq_used", grp, NCH - 2 + j)])
            p.wait("scalar", pr.get(("o_tp0", grp - 1)))
            p.wait("scalar", pr.get(("o_tp1", grp - 1)))
            t_to = p.do("scalar", lambda e, jj=j, r=reg: e.activation(
                tq[jj % 2][:], r, ACTF.Identity, bias=bo_sb[:, jj:jj + 1]), inc="AE")
            pr[("o_hi", grp, j)] = t_to
            p.wait("vector", t_to)
            p.wait("vector", pr.get(("o_tp0", grp - 1)))
            p.wait("vector", pr.get(("o_tp1", grp - 1)))
            p.do("vector", lambda e, jj=j: e.tensor_copy(
                ch_view(o_hi[:], jj), tq[jj % 2][:].rearrange("p (s f) -> p s f", s=SPG)))
            t_ol = p.do("vector", lambda e, jj=j: e.tensor_sub(
                ch_view(o_lo[:], jj), tq[jj % 2][:].rearrange("p (s f) -> p s f", s=SPG),
                ch_view(o_hi[:], jj)), inc="VE")
            pr[("to_used", grp, j)] = t_ol
            if j == 3:
                emit_otp_half(grp, 0)
        pr[("o_ev", grp)] = t_ol

    def emit_otp_half(G, half):
        b, g = divmod(G, NGRP)
        grp = G
        lo, hi = (0, NCH // 2) if half == 0 else (NCH // 2, NCH)
        t_oev = pr[("to_used", grp, 3)] if half == 0 else pr[("o_ev", grp)]
        t_otp = None
        for s2 in range(SPG):
            k2 = g * SPG + s2
            t2 = b * NSUB + k2
            p.wait("sync", t_oev)
            if half == 0:
                p.wait("sync", pr.get(("x2A", t2 - 2)))
            else:
                p.wait("sync", pr.get(("x2_done", t2 - 2)))
            sem = f"TPO{s2}" if half else f"TPOA{s2}"
            p.do("sync", lambda e, ss=s2, kk=s2, l=lo, h=hi: e.dma_start(
                out=chunk3(oh_tm[kk][:], 0)[:, l:h], in_=chunk3(o_hi[:], ss, stride=GT)[:, l:h],
                transpose=True), inc=sem, n=16)
            t_otp = p.do("sync", lambda e, ss=s2, kk=s2, l=lo, h=hi: e.dma_start(
                out=chunk3(ol_tm[kk][:], 0)[:, l:h], in_=chunk3(o_lo[:], ss, stride=GT)[:, l:h],
                transpose=True), inc=sem, n=16)
            pr[(("o_tpA", t2) if half == 0 else ("o_tpt", t2))] = t_otp

    def emit_tailB1(G):
        emit_otp_half(G, 1)

    tb2 = {}

    def tailB2_x2h(G, half):
        b, g = divmod(G, NGRP)
        grp = G
        x2s = [x2_tm[s2 + 2 * (G % 2)] for s2 in range(SPG)]
        c0, c1 = (0, D // 2) if half == 0 else (D // 2, D)
        t_x2s = []
        for s2 in range(SPG):
            k2 = g * SPG + s2
            t2 = b * NSUB + k2
            if half == 0:
                p.wait("vector", pr[("o_tpA", t2)])
                p.wait("vector", pr[("x1ld", G, s2)])
                p.wait("vector", pr.get(("x3_done", t2 - 4)))
                p.wait("vector", pr.get(("stB_free_a", grp - 1)))
            else:
                p.wait("vector", pr[("o_tpt", t2)])
            p.do("vector", lambda e, kk=s2, xv=x2s[s2]: e.tensor_add(
                xv[:, c0:c1], x1b[kk][:, c0:c1], oh_tm[kk][:, c0:c1]))
            p.drain("vector")
            t_x2 = p.do("vector", lambda e, kk=s2, xv=x2s[s2]: e.tensor_add(
                xv[:, c0:c1], xv[:, c0:c1], ol_tm[kk][:, c0:c1]), inc="VE")
            if half == 0:
                pr[("x2A", t2)] = t_x2
            else:
                pr[("x2_done", t2)] = t_x2
                t_x2s.append(t_x2)
                if s2 == SPG - 1:
                    pr[("x1_used", grp)] = t_x2
        if half == 1:
            tb2[G] = (x2s, t_x2s)

    def tailB2_stats(G):
        g = G % NGRP
        x2s, t_x2s = tb2[G]
        pr[("stB_free_a", G)] = stats_pair_b(
            [x2s[0][:], x2s[1][:]], stB[g % 2],
            pr.get(("stB_free", G - 2)), pr.get(("stB_free_a", G - 2)), t_x2s)

    def tailB2_apply(G, s2):
        b, g = divmod(G, NGRP)
        gi = g % 2
        x2s, _ = tb2[G]
        k2 = g * SPG + s2
        t2 = b * NSUB + k2
        p.wait("vector", pr.get(("xl3_tp", t2 - 1)))
        t_x3 = p.do("vector", lambda e, ii=s2, xv=x2s[s2]: e.tensor_scalar(
            xl3_bf[0][:], xv[:], stB[gi]["negm"][:, ii:ii + 1],
            stB[gi]["rstd"][:, ii:ii + 1], ALU.add, ALU.mult), inc="VE")
        if s2 == SPG - 1:
            pr[("stB_free", G)] = t_x3
        p.wait("sync", t_x3)
        p.wait("sync", pr.get(("xq8b_done", G - 1)))
        t_t3 = p.do("sync", lambda e, ss=s2: e.dma_start(
            out=chunk3(xl3_fm[0][:], ss, stride=GT),
            in_=chunk3(xl3_bf[0][:], 0), transpose=True), inc=f"TP3{s2}", n=16)
        pr[("xl3_tp", t2)] = t_t3

    def tailB2_quant(G, s2):
        b, g = divmod(G, NGRP)
        t2 = b * NSUB + g * SPG + s2
        p.wait("scalar", pr[("xl3_tp", t2)])
        p.wait("scalar", pr.get(("ffn2_mm", G - 2)))
        t_q3 = p.do("scalar", lambda e, ss=s2, gg=G % 2: e.activation(
            chunk3cm(xq8b[gg][:], ss), chunk3(xl3_fm[0][:], ss, stride=GT),
            ACTF.Copy, scale=rsin[:]), inc="AE")
        if s2 == SPG - 1:
            pr[("xq8b_done", G)] = t_q3

    def emit_tailB2(G):
        tailB2_x2h(G, 0)
        tailB2_x2h(G, 1)
        tailB2_stats(G)
        tailB2_apply(G, 0)
        tailB2_apply(G, 1)

    def emit_ffn2(G, tail_G=None):
        grp = G
        p.wait("tensor", pr[("xq8b_done", grp)])
        p.wait("tensor", pr[("o_hi", min(grp + 1, NGT - 1), NCH - 1)])
        p.wait("tensor", pr.get(("ffn2_dq_last", grp - 1)))
        t_zmm = None
        t_z2 = None
        if tail_G is not None:
            tailB2_x2h(tail_G, 0)
        for j in range(NCH):
            reg = PBp[j % 2][:] if j < 6 else PBp[j - 2][:]
            if j < 6:
                p.wait("tensor", pr.get(("ffn2_dq", grp, j - 2)))
            else:
                p.wait("tensor", pr.get(("rb_ev", grp + 1, j)))
            for i in range(NCH // 2):
                t_zmm = p.do("tensor", lambda e, ii=i, jj=j, gg=G % 2, r=reg: e.matmul(
                    r, kq8[:, (ii * NCH + jj) * 256:(ii * NCH + jj) * 256 + 256]
                        .rearrange("p (i m) -> p i m", i=2),
                    xq8b[gg][:, ii * 512:(ii + 1) * 512].rearrange("p (i n) -> p i n", i=2),
                    start=(ii == 0), stop=(ii == NCH // 2 - 1), perf_mode=DR),
                    inc=("PE" if i == NCH // 2 - 1 else None))
            p.wait("scalar", t_zmm)
            p.wait("scalar", pr.get(("tzb_used", grp, j - 2)))
            if j < 2:
                p.wait("scalar", pr.get(("tzb_used", grp - 1, NCH - 2 + j)))
            t_dq = p.do("scalar", lambda e, jj=j, r=reg: e.activation(
                tzb[jj % 2][:], r, ACTF.Copy, scale=sprod[:]), inc="AE")
            pr[("ffn2_dq", grp, j)] = t_dq
            if j == NCH - 1:
                pr[("ffn2_dq_last", grp)] = t_dq
            p.wait("vector", t_dq)
            p.wait("vector", pr.get(("z2_tp0", grp - 1)))
            p.wait("vector", pr.get(("z2_tp1", grp - 1)))
            t_z2 = p.do("vector", lambda e, jj=j, gg=0: e.tensor_scalar(
                ch_view(z2_fm[gg][:], jj), tzb[jj % 2][:].rearrange("p (s f) -> p s f", s=SPG),
                fb32_c[:, jj:jj + 1], 0.5, ALU.add, ALU.mult), inc="VE")
            pr[("tzb_used", grp, j)] = t_z2
            if tail_G is not None and j == 1:
                tailB2_x2h(tail_G, 1)
        pr[("ffn2_mm", grp)] = t_zmm
        pr[("ffn2_zv", grp)] = t_z2
        if tail_G is not None:
            tailB2_stats(tail_G)
            tailB2_apply(tail_G, 0)
            tailB2_apply(tail_G, 1)

    def emit_f2tail(G):
        b, g = divmod(G, NGRP)
        grp = G
        t_ffn2 = pr[("ffn2_zv", grp)]
        t_z2tp = None
        for s2 in range(SPG):
            k2 = g * SPG + s2
            t2 = b * NSUB + k2
            p.wait("sync", t_ffn2)
            p.wait("sync", pr.get(("out_done", t2 - 2)))
            t_z2tp = p.do("sync", lambda e, ss=s2, kk=s2, gg=0: e.dma_start(
                out=chunk3(z2_tm[kk][:], 0), in_=chunk3(z2_fm[gg][:], ss, stride=GT),
                transpose=True), inc=f"TPZ2{s2}", n=16)
            pr[(f"z2_tp{s2}", grp)] = t_z2tp
            p.wait("vector", t_z2tp)
            p.wait("vector", pr.get(("out_st", t2 - 2)))
            t_out = p.do("vector", lambda e, kk=s2, xv=x2_tm[s2 + 2 * (G % 2)]: e.tensor_add(
                out_tm[kk][:], xv[:], z2_tm[kk][:]), inc="VE")
            pr[("out_done", t2)] = t_out
            pr[("x3_done", t2)] = t_out
            p.wait("gpsimd", t_out)
            pr[("out_st", t2)] = p.do("gpsimd", lambda e, bb=b, kk=k2, kki=s2: e.dma_start(
                out=d_out[bb, kk * 128:kk * 128 + 128, :], in_=out_tm[kki][:]), inc=f"SVO{s2}", n=16)

    emit_xl_load(0)
    emit_x1_load(0)
    for G in range(NGT):
        if G >= 2:
            emit_ffn2(G - 2, tail_G=G - 1)
        elif G >= 1:
            emit_tailB2(G - 1)
        emit_qno(G, tail_G=(G - 1 if G >= 1 else None))
        emit_xl_load(G + 1)
        if G >= 2:
            emit_f2tail(G - 2)
        emit_tailB1(G)
        if G >= 1:
            emit_x1_load(G)
    emit_ffn2(NGT - 2, tail_G=NGT - 1)
    tailB2_quant(NGT - 1, 0)
    tailB2_quant(NGT - 1, 1)
    emit_f2tail(NGT - 2)
    emit_ffn2(NGT - 1)
    emit_f2tail(NGT - 1)

    p.wait("gpsimd", p.cur("SVO0"))
    p.wait("gpsimd", p.cur("SVO1"))
    emit_block(nc, p)
    bes.close()
    es.close()
    return nc


def host_inputs(core_idx, inputs):
    x = np.ascontiguousarray(np.asarray(inputs["L0_attn_x"])[2 * core_idx:2 * core_idx + 2])
    wq = np.asarray(inputs["L0_attn_Wq"])
    wk = np.asarray(inputs["L0_attn_Wk"])
    rw = np.asarray(inputs["L0_attn_rw"])
    rwbd = np.zeros((NCH, 128, 128), dtype=np.float32)
    for h in range(16):
        c, r = divmod(h, 2)
        rwbd[c, r * 64:r * 64 + 64, r * 64:r * 64 + 64] = rw[h]

    def col8(v):
        return np.ascontiguousarray(np.asarray(v).reshape(NCH, 128).T)

    masks = np.zeros((16, NCH * 128), dtype=ml_dtypes.bfloat16)
    for c in range(NCH):
        masks[2 * c, c * 128:c * 128 + 64] = 1
        masks[2 * c + 1, c * 128 + 64:c * 128 + 128] = 1
    return {
        "x": x,
        "wqT": np.ascontiguousarray(wq.T),
        "wkT": np.ascontiguousarray(wk.T),
        "wv": np.ascontiguousarray(np.asarray(inputs["L0_attn_Wv"])),
        "wo": np.ascontiguousarray(np.asarray(inputs["L0_attn_Wo"])),
        "rwbd": rwbd,
        "bq_c": col8(inputs["L0_attn_bq"]),
        "bk_c": col8(inputs["L0_attn_bk"]),
        "bv_row": np.ascontiguousarray(np.asarray(inputs["L0_attn_bv"]).reshape(1, D)),
        "bo_c": col8(inputs["L0_attn_bo"]),
        "fk": np.ascontiguousarray(
            np.asarray(inputs["L0_ffn_kernel"]).reshape(4, 2, 128, 8, 128)
            .transpose(2, 0, 3, 1, 4).reshape(128, 64 * 128)),
        "fb_c": np.ascontiguousarray(np.asarray(inputs["L0_ffn_bias"]).reshape(NCH, 128).T),
        "s_in": np.asarray(inputs["L0_ffn_input_scale"], dtype=np.float32).reshape(1, 1),
        "s_k": np.asarray(inputs["L0_ffn_kernel_scale"], dtype=np.float32).reshape(1, 1),
        "c_ones128": np.ones((128, 1), dtype=ml_dtypes.bfloat16),
        "c_onesrow": np.ones((1, 128), dtype=ml_dtypes.bfloat16),
        "c_masks": masks,
    }


def kernel(**inputs):
    nc = build_core_program()
    in_maps = [host_inputs(c, inputs) for c in range(NCORES)]
    res = run_bass_kernel_spmd(nc, in_maps, core_ids=list(range(NCORES)))
    out = np.concatenate([r["out"] for r in res.results], axis=0)
    return np.ascontiguousarray(out)
